# version 17
# baseline (speedup 1.0000x reference)
import contextlib
from collections import defaultdict

import numpy as np
import concourse.bass as bass
import concourse.mybir as mybir
from concourse.bass_utils import run_bass_kernel_spmd

F32 = mybir.dt.float32
BF16 = mybir.dt.bfloat16
AX = mybir.AxisListType
ALU = mybir.AluOpType
ACT = mybir.ActivationFunctionType

MAGIC = 12582912.0
NCOR = 8
D = 1024
FF = 4096
T_FULL = 4096
T_OWN = 2048
EPS = 1e-5
ENGS = ("sync", "gpsimd", "vector", "scalar", "tensor")
KB = 1024


class ColView:

    def __init__(self, t, off, n):
        self.t = t
        self.off = off
        self.n = n

    def __getitem__(self, idx):
        if isinstance(idx, tuple):
            p, c = idx
        else:
            p, c = idx, slice(None)
        cs = c.start if c.start is not None else 0
        ce = c.stop if c.stop is not None else self.n
        return self.t[p, self.off + cs:self.off + ce]


class Prog:
    def __init__(self):
        self.ops = {e: [] for e in ENGS}
        self.cnt = defaultdict(int)

    def op(self, eng, fn, waits=(), inc=None, inc_n=1):
        w = [(s, v) for (s, v) in waits if v and v > 0]
        self.ops[eng].append((w, fn, inc, inc_n))
        if inc is not None:
            self.cnt[inc] += inc_n
            return self.cnt[inc]
        return None


def emit(block, prog, sems):
    def make_body(ops):
        def body(eng):
            for waits, fn, inc, inc_n in ops:
                for s, v in waits:
                    eng.wait_ge(sems[s], v)
                inst = fn(eng)
                if inc is not None:
                    inst.then_inc(sems[inc], inc_n)
        return body
    for e in ENGS:
        getattr(block, e)(make_body(prog.ops[e]))


def build(debug=False):
    nc = bass.Bass("TRN2", target_bir_lowering=False)

    X = nc.dram_tensor("x", [T_FULL, D], F32, kind="ExternalInput").ap()
    WQKV = nc.dram_tensor("wqkv", [3 * D, D], F32, kind="ExternalInput").ap()
    WOUT = nc.dram_tensor("wout", [D, D], F32, kind="ExternalInput").ap()
    WIN = nc.dram_tensor("win", [D, FF], F32, kind="ExternalInput").ap()
    BIN = nc.dram_tensor("bin", [1, FF], F32, kind="ExternalInput").ap()
    WO2 = nc.dram_tensor("wo2", [FF, D], F32, kind="ExternalInput").ap()
    BOUT = nc.dram_tensor("bout", [1, D], F32, kind="ExternalInput").ap()
    CVEC = nc.dram_tensor("cvec", [1, D], F32, kind="ExternalInput").ap()
    WMOD = nc.dram_tensor("wmod", [D, 6 * D], F32, kind="ExternalInput").ap()
    OUT = nc.dram_tensor("out", [T_OWN, D], F32, kind="ExternalOutput").ap()

    kd = "ExternalOutput" if debug else "Internal"
    kv_k = nc.dram_tensor("kv_k", [8, 128, T_FULL], BF16, kind=kd).ap()
    q_d = nc.dram_tensor("q_d", [8, 128, T_OWN], BF16, kind=kd).ap()
    kv_v = nc.dram_tensor("kv_v", [32, 128, 8 * 130], BF16, kind=kd).ap()
    at_d = nc.dram_tensor("at_d", [8, 128, T_OWN], BF16, kind=kd).ap()
    x2_d = nc.dram_tensor("x2_d", [16, 128, D], F32, kind=kd).ap()
    h2_d = nc.dram_tensor("h2_d", [16, 128, D], BF16, kind=kd).ap()
    hid_d = nc.dram_tensor("hid_d", [32, 128, T_OWN], BF16, kind=kd).ap()
    w2_d = nc.dram_tensor("w2_d", [32, 128, D], BF16, kind=kd).ap()
    xq2_d = nc.dram_tensor("xq2_d", [16, 128, FF], BF16, kind=kd).ap()
    cc1i = nc.dram_tensor("cc1i", [1, 1], F32)
    cc1o = nc.dram_tensor("cc1o", [1, 1], F32)
    cc2i = nc.dram_tensor("cc2i", [1, 1], F32)
    cc2o = nc.dram_tensor("cc2o", [1, 1], F32)

    with contextlib.ExitStack() as ctx:
        sb = lambda n, s, d: ctx.enter_context(nc.sbuf_tensor(n, s, d))

        ident = sb("ident", [128, 128], F32)
        ones_mat = sb("ones_mat", [128, 128], BF16)
        ones1f = sb("ones1f", [1, 128], F32)
        e65 = sb("e65", [65, 128], F32)
        negones = sb("negones", [1, 128], BF16)
        bc_sa = sb("bc_sa", [128, D], F32)
        bc_ca = sb("bc_ca", [128, D], F32)
        bc_ga = sb("bc_ga", [128, D], F32)
        bc_sbv = sb("bc_sbv", [128, D], F32)
        bc_cb = sb("bc_cb", [128, D], F32)
        bc_gb = sb("bc_gb", [128, D], F32)
        bc_gbo = sb("bc_gbo", [128, D], F32)
        xst = [sb(f"xst{i}", [128, D], F32) for i in range(2)]
        scrap = [sb(f"scrap{i}", [128, D], F32) for i in range(2)]
        h1t32 = [sb(f"h1t32{i}", [128, D], F32) for i in range(2)]
        ln_a = sb("ln_a", [128, 1], F32)
        ln_b = sb("ln_b", [128, 1], F32)
        ln_c = sb("ln_c", [128, 1], F32)
        ln_d = sb("ln_d", [128, 1], F32)
        nmq = sb("nmq", [128, 1], F32)
        nmk = sb("nmk", [128, 1], F32)
        nmt = sb("nmt", [128, 1], F32)
        negmA = [sb(f"negmA{i}", [128, 1], F32) for i in range(2)]
        negmB = [sb(f"negmB{i}", [128, 1], F32) for i in range(2)]
        amx = sb("amx", [128, 1], F32)
        amrow = sb("amrow", [1, 128], F32)
        sc_s = sb("sc_s", [1, 8], F32)
        s1t = sb("s1t", [128, 1], F32)
        rs1t = sb("rs1t", [128, 1], F32)
        s2t = sb("s2t", [128, 1], F32)
        rs2t = sb("rs2t", [128, 1], F32)
        am1 = sb("am1", [128, 16], F32)
        am2a = sb("am2a", [128, 64], F32)
        red1 = sb("red1", [128, 1], F32)
        eps_t = sb("eps_t", [128, 1], F32)
        nb26 = sb("nb26", [128, 1], F32)
        magict = sb("magict", [128, 1], F32)
        nmagict = sb("nmagict", [128, 1], F32)
        rsf = sb("rsf", [128, 32], F32)
        amaxr = sb("amaxr", [128, 32], F32)
        srr = sb("srr", [128, 32], F32)
        rsr = sb("rsr", [128, 32], F32)
        rs2rsr = sb("rs2rsr", [128, 32], F32)
        b_inT = sb("b_inT", [128, 32], F32)

        top = nc.sbuf_top
        Z1 = top - 58 * KB
        Z2 = Z1 - 56 * KB
        Z3 = Z2 - 36 * KB

        def at(name, shape, dtype, off):
            return nc.alloc_sbuf_tensor_at(name, shape, dtype, offset=off)

        wmod_st = [at(f"wmod_st{i}", [128, 1536], F32, Z1 + 6 * KB * i) for i in range(4)]
        wm16 = [at(f"wm16_{i}", [128, 1536], BF16, Z1 + 24 * KB + 3 * KB * i)
                for i in range(2)]
        kbufA = [at(f"kbufA{i}", [65, T_FULL], BF16, Z1 + 8 * KB * i) for i in range(2)]
        kbufB = [at(f"kbufB{i}", [65, T_FULL], BF16, Z1 + 16 * KB + 8 * KB * i) for i in range(2)]
        vbuf = [at(f"vbuf{i}", [128, 32 * 130], BF16, Z1 + 32 * KB + 8320 * i) for i in range(2)]
        probs = [at(f"probs{i}", [128, KB], BF16, Z1 + 49408 + 2 * KB * i) for i in range(4)]
        probsP = [at(f"probsP{i}", [128, 2 * KB], BF16, Z1 + 49408 + 4 * KB * i)
                  for i in range(2)]
        sksq = [at(f"sksq{i}", [65, 512], BF16, Z1 + 55552 + KB * i) for i in range(2)]
        hidld = [at(f"hidld{i}", [128, FF], BF16, Z1 + 8 * KB * i) for i in range(2)]
        hidy = at("hidy", [128, FF], F32, Z1 + 16 * KB)
        atres = at("atres", [128, 16 * KB], BF16, Z1)
        xq2t = [at(f"xq2t{i}", [128, FF], BF16, Z1 + 32 * KB + 8 * KB * i) for i in range(2)]
        outst = [at(f"outst{i}", [128, 512], F32, Z1 + 48 * KB + 2 * KB * i) for i in range(2)]

        wqkvT_all = at("wqkvT_all", [128, 24 * KB], BF16, Z2)
        wqkvT = [ColView(wqkvT_all, 3 * D * e, 3 * D) for e in range(8)]
        wq_st = [at(f"wq_st{i}", [128, D], F32, Z2 + 48 * KB + 4 * KB * i) for i in range(2)]
        m1q = at("m1q", [128, KB], F32, Z2)
        w1st = [at(f"w1st{i}", [128, KB], F32, Z2 + 4 * KB + 4 * KB * i) for i in range(2)]
        wo2_st = [at(f"wo2_st{i}", [128, D], F32, Z2 + 12 * KB + 4 * KB * i) for i in range(2)]
        amaxf_row = at("amaxf_row", [1, FF], F32, Z2 + 20 * KB)
        woutT = at("woutT", [128, 8 * D], BF16, Z2 + 36 * KB)
        xq1f_all = at("xq1f_all", [128, 16 * KB], BF16, Z2)
        xq1f = [ColView(xq1f_all, T_OWN * e, T_OWN) for e in range(8)]
        xq1st = [at(f"xq1st{i}", [128, D], F32, Z2 + 32 * KB + 4 * KB * i) for i in range(2)]
        win_st = [at(f"win_st{i}", [128, KB], F32, Z2 + 40 * KB + 4 * KB * i) for i in range(3)]
        wqf32 = [at(f"wqf32{i}", [128, KB], F32, Z2 + 32 * KB + 4 * KB * i) for i in range(2)]
        sq16 = at("sq16", [128, KB], BF16, Z2 + 52 * KB)

        silu_c32 = at("silu_c32", [1, D], F32, Z3)
        gate_tmp = at("gate_tmp", [1, D], F32, Z3 + 4 * KB)
        silu_cT = at("silu_cT", [128, 8], F32, Z3 + 8 * KB)
        silu_cT16 = at("silu_cT16", [128, 8], BF16, Z3 + 8 * KB + 64)
        mod_raw = at("mod_raw", [1, 6 * D], F32, Z3 + 12 * KB)
        h1seg_all = at("h1seg_all", [128, 8 * KB], BF16, Z3)
        h1seg = [ColView(h1seg_all, KB * e, KB) for e in range(8)]
        kstage = [at(f"kstage{i}", [128, 512], BF16, Z3 + 16 * KB + KB * i) for i in range(2)]
        qstage = [at(f"qstage{i}", [128, 512], BF16, Z3 + 18 * KB + KB * i) for i in range(2)]
        vstage = [at(f"vstage{i}", [128, 8 * 130], BF16, Z3 + 20 * KB + 2080 * i) for i in range(2)]
        qaugA = [at(f"qaugA{i}", [65, T_OWN], BF16, Z3 + 4 * KB * i) for i in range(2)]
        qaugB = [at(f"qaugB{i}", [65, T_OWN], BF16, Z3 + 8 * KB + 4 * KB * i) for i in range(2)]
        den_sb = at("den_sb", [65, 2 * KB], F32, Z3 + 16 * KB)
        attAst = at("attAst", [64, KB], BF16, Z2 + 54 * KB)
        rbc = at("rbc", [128, KB], F32, Z3 + 24 * KB)
        atstage = [at(f"atstage{i}", [128, T_OWN], BF16, Z3 + 28 * KB + 4 * KB * i) for i in range(2)]
        h2st = [at(f"h2st{i}", [128, D], BF16, Z3 + 28 * KB + 2 * KB * i) for i in range(2)]
        wint1 = [at(f"wint1{i}", [128, KB], BF16, Z3 + 2 * KB * i) for i in range(2)]
        hidst = [at(f"hidst{i}", [128, T_OWN], BF16, Z3 + 4 * KB + 4 * KB * i) for i in range(2)]
        sf_bc = at("sf_bc", [128, FF], F32, Z3 + 12 * KB)
        w2eh = at("w2eh", [128, 32 * 512], BF16, Z3 + 0)

        assert nc.sbuf_bytes_remaining >= (top - Z3), f"sbuf overflow: {nc.sbuf_bytes_remaining} < {top - Z3}"

        psAB = ctx.enter_context(nc.psum_tensor("psAB", [128, 2 * KB], F32))
        psA = ColView(psAB, 0, KB)
        psB = ColView(psAB, KB, KB)
        psC = ctx.enter_context(nc.psum_tensor("psC", [128, KB], F32))
        psD = ctx.enter_context(nc.psum_tensor("psD", [128, KB], F32))

        semnames = (["T", "V", "S", "G", "DM", "DZ", "DC"] +
                    [f"D{c}{i}" for c in "WNXKQVOASYHIUGE" for i in range(2)])
        sems = {n: ctx.enter_context(nc.semaphore(n)) for n in semnames}
        block = ctx.enter_context(nc.Block())

        p = Prog()
        C = p.cnt

        def dma(qeng, qsem, out, in_, waits=(), **kw):
            n = p.op(qeng, lambda e, o=out, i=in_, k=kw: e.dma_start(out=o, in_=i, **k),
                     waits=waits, inc=qsem, inc_n=16)
            return (qsem, n)

        def tot(*names):
            return [(s, C[s]) for s in names if C[s] > 0]

        def ve(fn, waits=()):
            return p.op("vector", fn, waits=waits, inc="V")

        def sc(fn, waits=()):
            return p.op("scalar", fn, waits=waits, inc="S")

        def te(fn, waits=(), inc="T"):
            return p.op("tensor", fn, waits=waits, inc=inc)

        g_ms = p.op("gpsimd", lambda e: e.memset(ident[:], 0.0), inc="G")
        g_id = p.op("gpsimd", lambda e: e.affine_select(
            out=ident[:], in_=ident[:], compare_op=ALU.not_equal, fill=1.0,
            base=0, pattern=[[-1, 128]], channel_multiplier=1),
            waits=[("G", g_ms)], inc="G")
        p.op("vector", lambda e: e.memset(ones_mat[:], 1.0))
        p.op("vector", lambda e: e.memset(negones[:], -1.0))
        p.op("vector", lambda e: e.memset(ones1f[:], 1.0))
        v_e65 = ve(lambda e: e.memset(e65[:], 0.0))
        p.op("vector", lambda e: e.memset(e65[0:1, :], 1.0),
             waits=[("V", v_e65)])
        p.op("vector", lambda e: e.memset(eps_t[:], EPS))
        p.op("vector", lambda e: e.memset(magict[:], MAGIC))
        p.op("vector", lambda e: e.memset(nmagict[:], -MAGIC))
        p.op("vector", lambda e: e.memset(nb26[:], -26.0))
        v_consts = ve(lambda e: e.memset(amx[:], 0.0))

        wq_fin = [0] * 24
        for fc in range(24):
            wv = dma("sync", f"DW{fc % 2}", wq_st[fc % 2][:], WQKV[128 * fc:128 * fc + 128, :],
                     waits=[("V", wq_fin[fc - 2])] if fc >= 2 else [])
            ps_ = psC if (fc % 2 == 0) else psD
            tt = 0
            for ec in range(8):
                tt = te(lambda e, f=fc, k=ec, P=ps_: e.transpose(
                    P[:, 128 * k:128 * k + 128],
                    wq_st[f % 2][:, 128 * k:128 * k + 128], ident[:]),
                    waits=([wv, ("G", g_id)] +
                           ([("V", wq_fin[fc - 2])] if fc >= 2 else []))
                    if ec == 0 else [],
                    inc="T" if ec == 7 else None)
            wq_fin[fc] = ve(lambda e, f=fc, P=ps_: e.tensor_copy(
                wqkvT_all[:].rearrange("p (g c) -> p g c", c=3 * D)
                [:, :, 128 * f:128 * f + 128],
                P[:].rearrange("p (g c) -> p g c", c=128)),
                waits=[("T", tt)])
        v_wqkvT = C["V"]
        t_s1a = C["T"]

        w_c = dma("sync", "DM", silu_c32[:], CVEC)
        s_sil = sc(lambda e: e.activation(silu_c32[:], silu_c32[:], ACT.Silu),
                   waits=[w_c])
        last_sct = 0
        for k in range(8):
            tt = te(lambda e, kk=k: e.transpose(
                psA[:, 0:1], silu_c32[0:1, 128 * kk:128 * kk + 128], ident[0:1, 0:1]),
                waits=[("S", s_sil), ("G", g_id)] +
                      ([("V", last_sct)] if k > 0 else []))
            last_sct = ve(lambda e, kk=k: e.tensor_copy(silu_cT[:, kk:kk + 1], psA[:, 0:1]),
                          waits=[("T", tt)])
        v_sct = ve(lambda e: e.tensor_copy(silu_cT16[:], silu_cT[:]),
                   waits=[("V", last_sct)])

        s_mod = 0
        t_modmm = 0
        wm_w = {}
        vc_d = {}
        tmm = {}

        def wm_load(ch):
            q2, rc = divmod(ch, 8)
            qeng, qsem = [("scalar", "DN0"), ("gpsimd", "DG0"),
                          ("scalar", "DN1"), ("gpsimd", "DG1")][ch % 4]
            wm_w[ch] = dma(qeng, qsem, wmod_st[ch % 4][:],
                           WMOD[128 * rc:128 * rc + 128,
                                1536 * q2:1536 * q2 + 1536],
                           waits=[("V", vc_d[ch - 4])] if ch >= 4 else [])

        for ch0 in range(4):
            wm_load(ch0)
        for q2 in range(4):
            for rc in range(8):
                ch = 8 * q2 + rc
                vc = ve(lambda e, c2=ch: e.tensor_copy(wm16[c2 % 2][:],
                                                       wmod_st[c2 % 4][:]),
                        waits=[wm_w[ch], ("V", v_sct)] +
                              ([("T", tmm[ch - 2])] if ch >= 2 else []))
                vc_d[ch] = vc
                if ch + 4 < 32:
                    wm_load(ch + 4)
                for sg in range(3):
                    pst = [psB, psB, psC][sg]
                    off = [0, 512, 0][sg]
                    t_modmm = te(lambda e, c2=ch, P=pst, o=off, s=sg: e.matmul(
                        P[0:1, o:o + 512], silu_cT16[:, c2 % 8:c2 % 8 + 1],
                        wm16[c2 % 2][:, 512 * s:512 * s + 512],
                        start=(c2 % 8 == 0), stop=(c2 % 8 == 7)),
                        waits=([("V", vc)] if sg == 0 else []) +
                              ([("T", t_s1a), ("S", C["S"])]
                               if (ch == 0 and sg == 0) else []) +
                              ([("S", s_mod)]
                               if (rc == 0 and q2 > 0 and sg == 0) else []),
                        inc="T" if sg == 2 else None)
                tmm[ch] = t_modmm
            for sg in range(3):
                pst = [psB, psB, psC][sg]
                off = [0, 512, 0][sg]
                s_mod = sc(lambda e, s=sg, P=pst, o=off, q3=q2: e.copy(
                    mod_raw[0:1, 1536 * q3 + 512 * s:1536 * q3 + 512 * s + 512],
                    P[0:1, o:o + 512]),
                    waits=[("T", t_modmm)])
        t_mod_ps = C["T"]

        def gate_fix(off):
            a = sc(lambda e, o=off: e.activation(
                gate_tmp[0:1, :], mod_raw[0:1, o:o + D], ACT.Sign),
                waits=[("S", s_mod), ("V", C["V"])])
            c = sc(lambda e, o=off: e.activation(
                mod_raw[0:1, o:o + D], mod_raw[0:1, o:o + D], ACT.Abs),
                waits=[("S", a)])
            d = ve(lambda e, o=off: e.tensor_scalar_max(
                mod_raw[0:1, o:o + D], mod_raw[0:1, o:o + D], 0.25),
                waits=[("S", c)])
            f = ve(lambda e, o=off: e.tensor_tensor(
                mod_raw[0:1, o:o + D], mod_raw[0:1, o:o + D],
                gate_tmp[0:1, :], ALU.mult),
                waits=[("V", d)])
            return f
        gate_fix(2 * D)
        gate_fix(5 * D)
        ve(lambda e: e.tensor_scalar_add(mod_raw[0:1, D:2 * D],
                                         mod_raw[0:1, D:2 * D], 1.0),
           waits=[("S", s_mod)])
        v_modfix = ve(lambda e: e.tensor_scalar_add(mod_raw[0:1, 4 * D:5 * D],
                                                    mod_raw[0:1, 4 * D:5 * D], 1.0),
                      waits=[("S", s_mod)])

        def bcast_row(dst, row_ap, waits0):
            te(lambda e: e.matmul(psA[:, 0:512], ones1f[:], row_ap[0:1, 0:512],
                                  start=True, stop=True),
               waits=list(waits0), inc=None)
            t2 = te(lambda e: e.matmul(psA[:, 512:1024], ones1f[:], row_ap[0:1, 512:1024],
                                       start=True, stop=True))
            return ve(lambda e, d=dst: e.tensor_copy(d[:], psA[:]), waits=[("T", t2)])

        v_bc = bcast_row(bc_sa, mod_raw[0:1, 0:D],
                         [("V", v_modfix), ("S", C["S"]), ("V", C["V"]), ("T", t_mod_ps)])
        v_bc = bcast_row(bc_ca, mod_raw[0:1, D:2 * D], [("V", v_bc)])
        v_bc_ln1 = v_bc
        v_bc = bcast_row(bc_ga, mod_raw[0:1, 2 * D:3 * D], [("V", v_bc)])
        v_bc = bcast_row(bc_sbv, mod_raw[0:1, 3 * D:4 * D], [("V", v_bc)])
        v_bc = bcast_row(bc_cb, mod_raw[0:1, 4 * D:5 * D], [("V", v_bc)])
        v_bc = bcast_row(bc_gb, mod_raw[0:1, 5 * D:6 * D], [("V", v_bc)])
        w_bo = dma("sync", "DM", gate_tmp[0:1, :], BOUT, waits=[("V", C["V"])])
        v_bo = bcast_row(bc_gbo, gate_tmp[0:1, :], [w_bo, ("V", v_bc)])
        v_gbo = ve(lambda e: e.tensor_tensor(bc_gbo[:], bc_gbo[:], bc_gb[:], ALU.mult),
                   waits=[("V", v_bo)])
        v_bc_all = v_gbo

        v_vst = 0
        for sl in range(2):
            v_vst = ve(lambda e, s=sl: e.memset(
                vstage[s][:].rearrange("r (p c) -> r p c", c=130)[:, :, 64:65], 1.0),
                waits=[("V", v_bc_all)] if sl == 0 else [])
            v_vst = ve(lambda e, s=sl: e.memset(
                vstage[s][:].rearrange("r (p c) -> r p c", c=130)[:, :, 129:130], 1.0))
        x_w = [0] * 32
        v_xfree = [0] * 32
        v_h1seg = [0] * 8
        ev_h1 = [0] * 32
        s_kq = 0
        qs_k_w = [0] * 100
        qc_v_w = []
        qg_q_w = []
        nqs_k = 0
        for i in range(32):
            s = i // 4
            x_w[i] = dma("sync" if i % 2 == 0 else "gpsimd",
                         "DX0" if i % 2 == 0 else "DE0",
                         xst[i % 2][:], X[128 * i:128 * i + 128, :],
                         waits=[("V", v_xfree[i - 2])] if i >= 2 else [])
            r1 = ve(lambda e, ii=i: e.tensor_reduce(
                ln_a[:], xst[ii % 2][:], AX.X, ALU.add, negate=True),
                waits=[x_w[i], ("V", C["V"])])
            s1 = sc(lambda e, ii=i: e.activation(
                scrap[ii % 2][:], xst[ii % 2][:], ACT.Square, accum_out=ln_b[:]),
                waits=[x_w[i], ("S", C["S"]), ("V", C["V"] - 1)])
            r2 = ve(lambda e: e.tensor_scalar(ln_a[:], ln_a[:], 1.0 / D, None, ALU.mult),
                    waits=[("V", r1)])
            r3 = ve(lambda e: e.tensor_tensor(ln_d[:], ln_a[:], ln_a[:], ALU.mult),
                    waits=[("V", r2)])
            r4 = ve(lambda e: e.tensor_scalar(ln_c[:], ln_b[:], 1.0 / D, None, ALU.mult),
                    waits=[("S", s1), ("V", r3)])
            r5 = ve(lambda e: e.tensor_tensor(ln_c[:], ln_c[:], ln_d[:], ALU.subtract),
                    waits=[("V", r4)])
            s2_ = sc(lambda e: e.activation(ln_c[:], ln_c[:], ACT.Sqrt, bias=eps_t[:]),
                     waits=[("V", r5), ("S", C["S"])])
            r6 = ve(lambda e: e.reciprocal(ln_c[:], ln_c[:]), waits=[("S", s2_)])
            z1 = ve(lambda e, ii=i: e.tensor_scalar(
                scrap[ii % 2][:], xst[ii % 2][:], ln_a[:], ln_c[:], ALU.add, ALU.mult),
                waits=[("V", r6)])
            v_xfree[i] = z1
            z2 = ve(lambda e, ii=i: e.tensor_tensor(
                h1t32[ii % 2][:], scrap[ii % 2][:], bc_ca[:], ALU.mult),
                waits=[("V", z1), ("V", v_bc_ln1)])
            z3 = ve(lambda e, ii=i: e.tensor_tensor(
                h1t32[ii % 2][:], h1t32[ii % 2][:], bc_sa[:], ALU.add),
                waits=[("V", z2)])
            ps_ = psC if (i % 2 == 0) else psD
            tt = 0
            for ec in range(8):
                tt = te(lambda e, ii=i, k=ec, P=ps_: e.transpose(
                    P[:, 128 * k:128 * k + 128],
                    h1t32[ii % 2][:, 128 * k:128 * k + 128], ident[:]),
                    waits=([("V", z3)] +
                           ([("T", t_mod_ps), ("V", v_bc_all)] if i == 0 else []) +
                           ([("V", ev_h1[i - 2])] if i >= 2 else []) +
                           ([("V", C["V"])] if (i % 4 == 0 and i > 0) else []))
                    if ec == 0 else [],
                    inc="T" if ec == 7 else None)
            ev_h1[i] = ve(lambda e, ii=i, P=ps_: e.tensor_copy(
                h1seg_all[:].rearrange("p (g c) -> p g c", c=KB)
                [:, :, 512 * ((ii // 4) % 2) + 128 * (ii % 4):
                       512 * ((ii // 4) % 2) + 128 * (ii % 4) + 128],
                P[:].rearrange("p (g c) -> p g c", c=128)),
                waits=[("T", tt)])
            if i % 4 != 3:
                continue
            v_h1seg[s] = C["V"]
            soff = 512 * (s % 2)
            for pr in range(8):
                ps_ = psA if (pr % 2 == 0) else psB
                tl = 0
                for ec in range(8):
                    tl = te(lambda e, P2=pr, k=ec, P=ps_, so=soff: e.matmul(
                        P[:, 0:512], wqkvT[k][:, D + 128 * P2:D + 128 * P2 + 128],
                        h1seg[k][:, so:so + 512],
                        start=(k == 0), stop=(k == 7)),
                        waits=([("V", v_h1seg[s]), ("V", v_wqkvT), ("S", s_kq)]
                               if ec == 0 else []),
                        inc="T" if ec == 7 else None)
                s_kq = sc(lambda e, P2=pr, P=ps_: e.copy(kstage[P2 % 2][:], P[:, 0:512]),
                          waits=[("T", tl)] +
                                ([qs_k_w[nqs_k - 2]] if nqs_k >= 2 else []))
                qs_k_w[nqs_k] = dma("sync", f"DK{pr % 2}", kv_k[pr, :, 512 * s:512 * s + 512],
                                    kstage[pr % 2][:], waits=[("S", s_kq)])
                nqs_k += 1
            if s < 4:
                for pr in range(8):
                    ps_ = psA if (pr % 2 == 0) else psB
                    tl = 0
                    for ec in range(8):
                        tl = te(lambda e, P2=pr, k=ec, P=ps_, so=soff: e.matmul(
                            P[:, 512:1024], wqkvT[k][:, 128 * P2:128 * P2 + 128],
                            h1seg[k][:, so:so + 512],
                            start=(k == 0), stop=(k == 7)),
                            waits=[("S", s_kq)] if ec == 0 else [],
                            inc="T" if ec == 7 else None)
                    s_kq = sc(lambda e, P2=pr, P=ps_: e.copy(qstage[P2 % 2][:],
                                                             P[:, 512:1024]),
                              waits=[("T", tl)] +
                                    ([qg_q_w[-2]] if len(qg_q_w) >= 2 else []))
                    qg_q_w.append(dma("sync", f"DQ{pr % 2}",
                                      q_d[pr, :, 512 * s:512 * s + 512],
                                      qstage[pr % 2][:], waits=[("S", s_kq)]))
            for it in range(4):
                ii = 4 * s + it
                tl = 0
                for ec in range(8):
                    for sg in range(2):
                        tl = te(lambda e, k=ec, g=sg, ad=soff + 128 * it: e.matmul(
                            psC[:, 512 * g:512 * g + 512],
                            h1seg[k][:, ad:ad + 128],
                            wqkvT[k][:, 2 * D + 512 * g:2 * D + 512 * g + 512],
                            start=(k == 0), stop=(k == 7)),
                            waits=([("V", C["V"])] if (ec == 0 and sg == 0) else []),
                            inc="T" if (ec == 7 and sg == 1) else None)
                vv1 = ve(lambda e, ii2=ii: e.tensor_copy(
                    vstage[ii2 % 2][:].rearrange("r (p c) -> r p c", c=130)[:, :, 0:64],
                    psC[:].rearrange("r (p c) -> r p c", c=128)[:, :, 0:64]),
                    waits=[("T", tl), ("V", v_vst)] +
                          ([qc_v_w[-2]] if len(qc_v_w) >= 2 else []))
                vv2 = ve(lambda e, ii2=ii: e.tensor_copy(
                    vstage[ii2 % 2][:].rearrange("r (p c) -> r p c", c=130)[:, :, 65:129],
                    psC[:].rearrange("r (p c) -> r p c", c=128)[:, :, 64:128]),
                    waits=[("V", vv1)])
                qc_v_w.append(dma("scalar", f"DV{ii % 2}", kv_v[ii, :, :],
                                  vstage[ii % 2][:], waits=[("V", vv2)]))
        v_s2 = C["V"]
        s_s2 = C["S"]
        t_s2 = C["T"]
        pass
        pass
        pass

        wo_fin = [0] * 8
        for ec in range(8):
            wv = dma("scalar", f"DO{ec % 2}", wo2_st[ec % 2][:], WOUT[128 * ec:128 * ec + 128, :],
                     waits=([("V", wo_fin[ec - 2])] if ec >= 2 else
                            [("V", v_s2)]))
            last = 0
            for fc in range(8):
                ps_ = psC if (fc % 2 == 0) else psD
                tt = te(lambda e, k=ec, f=fc, P=ps_: e.transpose(
                    P[:, 0:128], wo2_st[k % 2][:, 128 * f:128 * f + 128], ident[:]),
                    waits=[wv, ("T", t_s2)] +
                          ([("V", C["V"] - 1)] if (ec > 0 or fc >= 2) else []))
                last = ve(lambda e, k=ec, f=fc, P=ps_: e.tensor_copy(
                    woutT[:, D * f + 128 * k:D * f + 128 * k + 128], P[:, 0:128]),
                    waits=[("T", tt)])
            wo_fin[ec] = last
        v_woutT = C["V"]

        w_bi = dma("scalar", "DM", amaxf_row[:], BIN, waits=[("V", v_s2)])
        last_bi = 0
        for k in range(32):
            tt = te(lambda e, kk=k: e.transpose(
                psC[:, 0:1], amaxf_row[0:1, 128 * kk:128 * kk + 128], ident[0:1, 0:1]),
                waits=[w_bi, ("T", C["T"])] +
                      ([("V", last_bi)] if k > 0 else [("V", v_woutT)]))
            last_bi = ve(lambda e, kk=k: e.tensor_copy(b_inT[:, kk:kk + 1], psC[:, 0:1]),
                         waits=[("T", tt)])
        v_bint = last_bi

        kB_w = [0] * 8
        vv_w = [0] * 8
        qB_w = [0] * 8
        at_sp = [0] * 8
        v_kone = ve(lambda e: e.memset(kbufA[0][64:65, :], -1.0),
                    waits=[("T", C["T"]), ("V", C["V"]), ("S", C["S"])] +
                          tot("DK0", "DK1", "DQ0", "DQ1", "DV0", "DV1"))
        v_kone = ve(lambda e: e.memset(kbufA[1][64:65, :], -1.0))
        v_kone = ve(lambda e: e.memset(kbufB[0][64:65, :], -1.0))
        v_kone = ve(lambda e: e.memset(kbufB[1][64:65, :], -1.0))
        v_kone = ve(lambda e: e.memset(den_sb[:], 0.0))
        dma("sync", "DK0", kbufA[0][0:64, :], kv_k[0, 0:64, :],
            waits=tot("DK0", "DK1"))
        kB_w[0] = dma("sync", "DK0", kbufB[0][0:64, :], kv_k[0, 64:128, :])
        vv_w[0] = dma("sync", "DV0", vbuf[0][:].rearrange("r (i c) -> r i c", c=130),
                      kv_v[:, :, 0:130].rearrange("i r c -> r i c"),
                      waits=tot("DV0", "DV1"))
        dma("sync", "DQ0", qaugA[0][0:64, :], q_d[0, 0:64, :],
            waits=tot("DQ0", "DQ1"))
        qB_w[0] = dma("sync", "DQ0", qaugB[0][0:64, :], q_d[0, 64:128, :])
        v_attev = v_bint
        pvA_d = 0
        pvB_d = 0
        exA_last = 0
        exB_last = 0
        pv_ring = [0, 0, 0]
        for pr in range(8):
            slot = pr % 2
            if pr + 1 < 8:
                sl2 = (pr + 1) % 2
                dma("sync", f"DK{sl2}", kbufA[sl2][0:64, :], kv_k[pr + 1, 0:64, :],
                    waits=[("V", C["V"]), ("S", C["S"])])
                kB_w[pr + 1] = dma("sync", f"DK{sl2}", kbufB[sl2][0:64, :],
                                   kv_k[pr + 1, 64:128, :])
                vv_w[pr + 1] = dma("sync", f"DV{sl2}",
                                   vbuf[sl2][:].rearrange("r (i c) -> r i c", c=130),
                                   kv_v[:, :, 130 * (pr + 1):130 * (pr + 1) + 130]
                                   .rearrange("i r c -> r i c"),
                                   waits=[("T", C["T"])])
                dma("sync", f"DQ{sl2}", qaugA[sl2][0:64, :], q_d[pr + 1, 0:64, :],
                    waits=[("V", C["V"]), ("S", C["S"])])
                qB_w[pr + 1] = dma("sync", f"DQ{sl2}", qaugB[sl2][0:64, :],
                                   q_d[pr + 1, 64:128, :])
            w_free = [] if pr > 0 else [("V", v_kone)]
            s_nm = 0
            for head in range(2):
                kb = kbufA if head == 0 else kbufB
                qa = qaugA if head == 0 else qaugB
                mmn = [0] * 8
                lastv = 0
                for sg in range(8):
                    ps_ = psA if sg % 2 == 0 else psB
                    tt_in = ve(lambda e, g=sg, s2l=slot, K2=kb: e.tensor_tensor(
                        sksq[g % 2][:], K2[s2l][0:65, 512 * g:512 * g + 512],
                        K2[s2l][0:65, 512 * g:512 * g + 512], ALU.mult),
                        waits=([kB_w[pr], ("V", C["V"])] + w_free
                               if (head == 0 and sg == 0) else []) +
                              ([("T", mmn[sg - 2])] if sg >= 2 else []) +
                              ([("T", qmm1), ("S", s_nm)]
                               if (head == 1 and sg <= 1) else []))
                    mmn[sg] = te(lambda e, g=sg, P=ps_: e.matmul(
                        P[:, 0:512], ones_mat[0:65, :], sksq[g % 2][0:65, :],
                        start=True, stop=True),
                        waits=[("V", tt_in)] +
                              ([("V", v_attev)] if (head == 0 and sg <= 1) else []))
                    if sg == 0:
                        lastv = ve(lambda e, P=ps_: e.tensor_reduce(
                            nmk[:], P[:, 0:512], AX.X, ALU.max),
                            waits=[("T", mmn[sg]), ("V", C["V"])])
                    else:
                        rv = ve(lambda e, P=ps_: e.tensor_reduce(
                            nmt[:], P[:, 0:512], AX.X, ALU.max),
                            waits=[("T", mmn[sg]), ("V", lastv)])
                        lastv = ve(lambda e: e.tensor_tensor(
                            nmk[:], nmk[:], nmt[:], ALU.max), waits=[("V", rv)])
                vsc = ve(lambda e: e.tensor_scalar(nmt[:], nmk[:], 1.0404, None,
                                                   ALU.mult), waits=[("V", lastv)])
                qmm = [0] * 4
                for sg in range(4):
                    ps_ = psA if sg < 2 else psB
                    tq_in = ve(lambda e, g=sg, s2l=slot, Q2=qa: e.tensor_tensor(
                        sksq[g % 2][0:64, :], Q2[s2l][0:64, 512 * g:512 * g + 512],
                        Q2[s2l][0:64, 512 * g:512 * g + 512], ALU.mult),
                        waits=([qB_w[pr], ("V", vsc), ("T", mmn[6]), ("T", mmn[7])]
                               if sg == 0 else []) +
                              ([("V", vsc)] if sg == 1 else []) +
                              ([("T", qmm[sg - 2])] if sg >= 2 else []))
                    qmm[sg] = te(lambda e, g=sg, P=ps_: e.matmul(
                        P[:, 512 * (g % 2):512 * (g % 2) + 512],
                        ones_mat[0:65, :], sksq[g % 2][0:65, :],
                        start=True, stop=True),
                        waits=[("V", tq_in)] + ([("S", s_nm)] if sg == 0 else []))
                s_nm = sc(lambda e, Q2=qa, s2l=slot: e.activation(
                    Q2[s2l][64:65, 0:2048], psAB[64:65, 0:2048],
                    ACT.Sqrt, scale=nmt[64:65, :]),
                    waits=[("T", qmm[3]), ("S", C["S"])])
                qmm1 = qmm[3]
            for hf in range(2):
                qoff = 1024 * hf
                d1 = 0
                for head in range(2):
                    kb = kbufA if head == 0 else kbufB
                    qa = qaugA if head == 0 else qaugB
                    pp = psC if head == 0 else psD
                    slots = [psA, psB, psD] if head == 0 else [psA, psB, psC]
                    ex = [0] * 32
                    pvj = [0] * 32

                    def emit_pv(jj, head=head, pp=pp, ex=ex, pvj=pvj):
                        t_ = 0
                        for sg in range(2):
                            t_ = te(lambda e, j2=jj, g=sg, s2l=slot, h=head,
                                    P=pp: e.matmul(
                                P[0:65, 512 * g:512 * g + 512],
                                vbuf[s2l][:, 130 * j2 + 65 * h:130 * j2 + 65 * h + 65],
                                probs[j2 % 3][:, 512 * g:512 * g + 512],
                                start=(j2 == 0), stop=(j2 == 31)),
                                waits=([("S", ex[jj])] if sg == 0 else []) +
                                      ([("V", v_attev)] +
                                       ([vv_w[pr]] if hf == 0 else [])
                                       if (jj == 0 and sg == 0) else []))
                        pvj[jj] = t_
                        pv_ring[jj % 3] = t_

                    for j in range(32):
                        ps_ = slots[j % 3]
                        if j >= 3:
                            w_sc = [("S", ex[j - 3])]
                        elif j >= 1:
                            w_sc = []
                        elif head == 1:
                            w_sc = [("S", exA_last), ("V", av1)]
                        elif hf == 0:
                            w_sc = [("S", s_nm), ("V", C["V"])]
                        else:
                            w_sc = [("S", exB_last), ("V", v_attev)]
                        tsa = 0
                        for sg in range(2):
                            tsa = te(lambda e, jj=j, g=sg, s2l=slot, qo=qoff,
                                     P=ps_, K2=kb, Q2=qa: e.matmul(
                                P[:, 512 * g:512 * g + 512],
                                K2[s2l][0:65, 128 * jj:128 * jj + 128],
                                Q2[s2l][0:65, qo + 512 * g:qo + 512 * g + 512],
                                start=True, stop=True),
                                waits=w_sc if sg == 0 else [],
                                inc="T" if sg == 1 else None)
                        if j >= 2:
                            emit_pv(j - 2)
                        ex[j] = sc(lambda e, jj=j, P=ps_: e.activation(
                            probs[jj % 3][:], P[:], ACT.Exp,
                            bias=0.0, scale=0.125),
                            waits=[("T", tsa)] +
                                  ([("T", pv_ring[j % 3])] if pv_ring[j % 3] else []))
                    emit_pv(30)
                    emit_pv(31)
                    if head == 0:
                        exA_last = ex[31]
                        pvA_d = pvj[31]
                        d1 = ve(lambda e: e.tensor_copy(den_sb[0:1, 0:1024],
                                                        psC[64:65, :]),
                                waits=[("T", pvj[31]), ("V", C["V"])])
                        av1 = ve(lambda e: e.tensor_copy(attAst[:], psC[0:64, :]),
                                 waits=[("V", d1)])
                    else:
                        exB_last = ex[31]
                        pvB_d = pvj[31]
                d2 = ve(lambda e: e.tensor_copy(den_sb[0:1, 1024:2048], psD[64:65, :]),
                        waits=[("T", pvB_d), ("V", d1)])
                tb1 = te(lambda e: e.matmul(psA[:, 0:512], e65[:],
                                            den_sb[0:65, 0:512], start=True, stop=True),
                         waits=[("V", d2), ("S", exB_last)], inc=None)
                tb1 = te(lambda e: e.matmul(psA[:, 512:1024], e65[:],
                                            den_sb[0:65, 512:1024], start=True, stop=True))
                tb2 = te(lambda e: e.matmul(psB[:, 0:512], e65[:],
                                            den_sb[0:65, 1024:1536], start=True, stop=True),
                         inc=None)
                tb2 = te(lambda e: e.matmul(psB[:, 512:1024], e65[:],
                                            den_sb[0:65, 1536:2048], start=True, stop=True))
                rb1 = ve(lambda e: e.tensor_copy(rbc[0:64, :], psA[0:64, :]),
                         waits=[("T", tb1)])
                rb2 = ve(lambda e: e.tensor_copy(rbc[64:128, :], psB[64:128, :]),
                         waits=[("T", tb2), ("V", rb1)])
                rr = ve(lambda e: e.reciprocal(rbc[:], rbc[:]), waits=[("V", rb2)])
                a1 = ve(lambda e, s2l=slot, o=qoff: e.tensor_tensor(
                    atstage[s2l][0:64, o:o + 1024], attAst[:], rbc[0:64, :],
                    ALU.mult),
                    waits=[("V", rr)] +
                          ([at_sp[pr - 2]] if (hf == 0 and pr >= 2) else []))
                v_attev = ve(lambda e, s2l=slot, o=qoff: e.scalar_tensor_tensor(
                    atstage[s2l][64:128, o:o + 1024], psD[0:64, :], 1.0, rbc[64:128, :],
                    ALU.bypass, ALU.mult),
                    waits=[("V", a1)])
            at_sp[pr] = dma("sync", f"DA{pr % 2}", at_d[pr], atstage[slot][:],
                            waits=[("V", v_attev)])
        v_attn_end = C["V"]
        t_attn_end = C["T"]
        s_attn_end = C["S"]
        pass

        v_m1 = ve(lambda e: e.memset(m1q[:], 0.0), waits=[("T", t_s2), ("V", C["V"])])
        for fq in range(4):
            for ec in range(8):
                wv = dma("scalar", f"DN{ec % 2}", w1st[ec % 2][:],
                         WIN[128 * ec:128 * ec + 128, 1024 * fq:1024 * fq + 1024],
                         waits=[("V", v_m1)])
                vsq = ve(lambda e, k=ec: e.tensor_tensor(
                    sq16[:], w1st[k % 2][:], w1st[k % 2][:], ALU.mult),
                    waits=[wv, ("V", C["V"])])
                v_m1 = ve(lambda e: e.tensor_tensor(
                    m1q[:], m1q[:], sq16[:], ALU.max),
                    waits=[("V", vsq)])
            for b in range(8):
                fc = 8 * fq + b
                tt = te(lambda e, bb=b: e.transpose(
                    psC[:, 0:128], m1q[:, 128 * bb:128 * bb + 128], ident[:]),
                    waits=[("V", v_m1), ("T", t_attn_end), ("V", C["V"])])
                rv = ve(lambda e: e.tensor_reduce(red1[:], psC[:, 0:128], AX.X, ALU.max),
                        waits=[("T", tt)])
                rvs = sc(lambda e: e.activation(red1[:], red1[:], ACT.Sqrt),
                         waits=[("V", rv), ("S", C["S"])])
                rv2 = ve(lambda e: e.tensor_scalar_max(red1[:], red1[:], 1e-8),
                         waits=[("S", rvs)])
                rv3 = ve(lambda e, f=fc: e.tensor_scalar(
                    rsf[:, f:f + 1], red1[:], 1.0 / 127.0, None, ALU.mult),
                    waits=[("V", rv2)])
                tt2 = te(lambda e: e.transpose(psC[0:1, 512:640], red1[:], ident[:]),
                         waits=[("V", rv3)])
                v_m1 = ve(lambda e, f=fc: e.tensor_copy(
                    amaxf_row[0:1, 128 * f:128 * f + 128], psC[0:1, 512:640]),
                    waits=[("T", tt2)])
            if fq < 3:
                v_m1 = ve(lambda e: e.memset(m1q[:], 0.0), waits=[("V", v_m1)])
        f1 = ve(lambda e: e.tensor_scalar_max(amaxf_row[:], amaxf_row[:], 1e-8),
                waits=[("V", v_m1)])
        f2 = ve(lambda e: e.reciprocal(amaxf_row[:], amaxf_row[:]), waits=[("V", f1)])
        v_sfrow = ve(lambda e: e.tensor_scalar(amaxf_row[:], amaxf_row[:], 127.0, None,
                                               ALU.mult), waits=[("V", f2)])

        v_wo = 0
        for fc in range(32):
            wv = dma("scalar", f"DO{fc % 2}", wo2_st[fc % 2][:], WO2[128 * fc:128 * fc + 128, :],
                     waits=[("V", v_wo)] if fc >= 2 else [("V", v_woutT)])
            v_wo = ve(lambda e, f=fc: e.tensor_reduce(
                amaxr[:, f:f + 1], wo2_st[f % 2][:], AX.X, ALU.max,
                apply_absolute_value=True),
                waits=[wv])
        c1 = ve(lambda e: e.tensor_scalar_max(amaxr[:], amaxr[:], 1e-8),
                waits=[("V", v_wo)])
        c2 = ve(lambda e: e.reciprocal(srr[:], amaxr[:]), waits=[("V", c1)])
        c3 = ve(lambda e: e.tensor_scalar(rsr[:], amaxr[:], 1.0 / 127.0, None, ALU.mult),
                waits=[("V", c2)])
        c4 = ve(lambda e: e.tensor_scalar(srr[:], srr[:], 127.0, None, ALU.mult),
                waits=[("V", c3)])
        v_w2q = c4
        qg_w2 = []
        for fc in range(32):
            wv = dma("scalar", f"DO{fc % 2}", wo2_st[fc % 2][:], WO2[128 * fc:128 * fc + 128, :],
                     waits=[("V", v_w2q)] if fc >= 2 else [("V", c4)])
            y1 = ve(lambda e, f=fc: e.tensor_scalar(
                w1st[f % 2][:], wo2_st[f % 2][:], srr[:, f:f + 1], MAGIC,
                ALU.mult, ALU.add),
                waits=[wv, ("V", C["V"])] +
                      ([qg_w2[-2]] if len(qg_w2) >= 2 else []))
            v_w2q = ve(lambda e, f=fc: e.tensor_scalar(
                hidst[f % 2][:, 0:1024], w1st[f % 2][:], MAGIC, rsr[:, f:f + 1],
                ALU.subtract, ALU.mult),
                waits=[("V", y1)] + (tot("DA0", "DA1") if fc < 2 else []))
            qg_w2.append(dma("scalar", f"DS{fc % 2}", w2_d[fc], hidst[fc % 2][:, 0:1024],
                             waits=[("V", v_w2q)]))
        pass

        at_w = [0] * 16
        x4_w = [0] * 16
        v_x2w = [0] * 16
        qs_x2 = []
        qc_h2 = []
        v_tc = 0
        for tc in range(16):
            pp = psA if tc % 2 == 0 else psB
            if tc == 0:
                w0_at = ([("T", t_attn_end), ("S", s_attn_end), ("V", v_attn_end)] +
                         tot("DA0", "DA1", "DE0"))
                for c4 in range(8):
                    dma("sync" if c4 % 2 == 0 else "gpsimd",
                        "DA0" if c4 % 2 == 0 else "DE0",
                        atres[:, T_OWN * c4:T_OWN * c4 + T_OWN], at_d[c4],
                        waits=w0_at if c4 <= 1 else [])
                w_atres = tot("DA0", "DE0")
            x4_w[tc] = dma("sync", f"DX{tc % 2}", xst[tc % 2][:], X[128 * tc:128 * tc + 128, :],
                           waits=[("V", v_x2w[tc - 2])] if tc >= 2 else [("V", v_s2)])
            tl = 0
            for fc in range(8):
                for sg in range(2):
                    tl = te(lambda e, f=fc, g=sg, t2=tc, P=pp: e.matmul(
                        P[:, 512 * g:512 * g + 512],
                        atres[:, T_OWN * f + 128 * t2:T_OWN * f + 128 * t2 + 128],
                        woutT[:, D * f + 512 * g:D * f + 512 * g + 512],
                        start=(f == 0), stop=(f == 7)),
                        waits=(([("V", v_woutT)] +
                                (w_atres + [("T", t_attn_end), ("S", s_attn_end),
                                            ("V", v_attn_end)] if tc == 0 else []) +
                               ([("V", v_x2w[tc - 2])] if tc >= 2 else [])))
                        if (fc == 0 and sg == 0) else [],
                        inc="T" if (fc == 7 and sg == 1) else None)
            g1 = ve(lambda e, t2=tc, P=pp: e.scalar_tensor_tensor(
                scrap[t2 % 2][:], P[:], 0.0, bc_ga[:], ALU.add, ALU.mult),
                waits=[("T", tl), ("V", C["V"]), ("V", v_bc_all)])
            g2 = ve(lambda e, t2=tc: e.tensor_tensor(
                h1t32[t2 % 2][:], scrap[t2 % 2][:], xst[t2 % 2][:], ALU.add),
                waits=[("V", g1), x4_w[tc]] +
                      ([qs_x2[tc - 2]] if tc >= 2 else []))
            qs_x2.append(dma("scalar", f"DY{tc % 2}", x2_d[tc], h1t32[tc % 2][:],
                             waits=[("V", g2)]))
            r1 = ve(lambda e, t2=tc: e.tensor_reduce(
                ln_a[:], h1t32[t2 % 2][:], AX.X, ALU.add, negate=True),
                waits=[("V", g2)])
            sq = sc(lambda e, t2=tc: e.activation(
                scrap[t2 % 2][:], h1t32[t2 % 2][:], ACT.Square, accum_out=ln_b[:]),
                waits=[("V", g2), ("S", C["S"])])
            r2 = ve(lambda e: e.tensor_scalar(ln_a[:], ln_a[:], 1.0 / D, None, ALU.mult),
                    waits=[("V", r1)])
            r3 = ve(lambda e: e.tensor_tensor(ln_d[:], ln_a[:], ln_a[:], ALU.mult),
                    waits=[("V", r2)])
            r4 = ve(lambda e: e.tensor_scalar(ln_c[:], ln_b[:], 1.0 / D, None, ALU.mult),
                    waits=[("S", sq), ("V", r3)])
            r5 = ve(lambda e: e.tensor_tensor(ln_c[:], ln_c[:], ln_d[:], ALU.subtract),
                    waits=[("V", r4)])
            s3_ = sc(lambda e: e.activation(ln_c[:], ln_c[:], ACT.Sqrt, bias=eps_t[:]),
                     waits=[("V", r5), ("S", C["S"])])
            r6 = ve(lambda e: e.reciprocal(ln_c[:], ln_c[:]), waits=[("S", s3_)])
            z1 = ve(lambda e, t2=tc: e.tensor_scalar(
                scrap[t2 % 2][:], h1t32[t2 % 2][:], ln_a[:], ln_c[:], ALU.add, ALU.mult),
                waits=[("V", r6)])
            z2 = ve(lambda e, t2=tc: e.tensor_tensor(
                scrap[t2 % 2][:], scrap[t2 % 2][:], bc_cb[:], ALU.mult),
                waits=[("V", z1)])
            z3 = ve(lambda e, t2=tc: e.tensor_tensor(
                h2st[t2 % 2][:], scrap[t2 % 2][:], bc_sbv[:], ALU.add),
                waits=[("V", z2)] + ([qc_h2[-2]] if len(qc_h2) >= 2 else []))
            v_tc = ve(lambda e, t2=tc: e.tensor_reduce(
                am1[:, t2:t2 + 1], h2st[t2 % 2][:], AX.X, ALU.max,
                apply_absolute_value=True),
                waits=[("V", z3)])
            v_x2w[tc] = v_tc
            qc_h2.append(dma("scalar", f"DH{tc % 2}", h2_d[tc], h2st[tc % 2][:],
                             waits=[("V", v_tc)]))
        v_s4 = C["V"]
        pass

        def global_amax(src, cci, cco, dst_col, waits0):
            a1_ = ve(lambda e: e.tensor_reduce(amx[:], src, AX.X, ALU.max),
                     waits=list(waits0) + [("V", C["V"])])
            tt = te(lambda e: e.transpose(psC[0:1, 0:128], amx[:, 0:1], ident[:]),
                    waits=[("V", a1_), ("T", C["T"]), ("V", C["V"])])
            a2_ = ve(lambda e: e.tensor_copy(amrow[:], psC[0:1, 0:128]),
                     waits=[("T", tt)])
            a3_ = ve(lambda e: e.tensor_reduce(sc_s[0:1, 0:1], amrow[:], AX.X, ALU.max),
                     waits=[("V", a2_)])
            a4_ = ve(lambda e: e.tensor_scalar_max(sc_s[0:1, 0:1], sc_s[0:1, 0:1], 1e-8),
                     waits=[("V", a3_)])
            w1 = dma("gpsimd", "DC", cci.ap(), sc_s[0:1, 0:1], waits=[("V", a4_)])
            gc = p.op("gpsimd", lambda e: e.collective_compute(
                "AllReduce", ALU.max, replica_groups=[list(range(NCOR))],
                ins=[cci.ap()], outs=[cco.ap()]),
                waits=[w1], inc="G")
            return dma("gpsimd", "DC", sc_s[0:1, dst_col:dst_col + 1], cco.ap(),
                       waits=[("G", gc)])

        def bcast_scalar(dst, col, wv):
            tt = te(lambda e, c2=col: e.matmul(psC[:, 0:1], ones1f[:],
                                               sc_s[0:1, c2:c2 + 1],
                                               start=True, stop=True),
                    waits=[wv, ("T", C["T"])])
            return ve(lambda e, d=dst: e.tensor_copy(d[:], psC[:, 0:1]),
                      waits=[("T", tt)])

        w_am1 = global_amax(am1[:, 0:16], cc1i, cc1o, 1, [("V", v_s4)])
        b1 = ve(lambda e: e.reciprocal(sc_s[0:1, 2:3], sc_s[0:1, 1:2]),
                waits=[w_am1])
        b2 = ve(lambda e: e.tensor_scalar(sc_s[0:1, 2:3], sc_s[0:1, 2:3], 32767.0,
                                          None, ALU.mult), waits=[("V", b1)])
        b3 = ve(lambda e: e.tensor_scalar(sc_s[0:1, 3:4], sc_s[0:1, 1:2], 1.0 / 32767.0,
                                          None, ALU.mult), waits=[("V", b2)])
        bs1 = bcast_scalar(s1t, 2, ("V", b3))
        bs2 = bcast_scalar(rs1t, 3, ("V", bs1))

        v_sfbc = 0
        for qd in range(4):
            te(lambda e, q=qd: e.matmul(psD[:, 0:512], ones1f[:],
                                        amaxf_row[0:1, 1024 * q:1024 * q + 512],
                                        start=True, stop=True),
               waits=[("V", v_sfrow), ("T", C["T"])] if qd == 0 else
                     [("V", v_sfbc)], inc=None)
            t2 = te(lambda e, q=qd: e.matmul(psD[:, 512:1024], ones1f[:],
                                             amaxf_row[0:1, 1024 * q + 512:1024 * q + 1024],
                                             start=True, stop=True))
            v_sfbc = ve(lambda e, q=qd: e.tensor_copy(
                sf_bc[:, 1024 * q:1024 * q + 1024], psD[:]), waits=[("T", t2)])

        h2r_w = [0] * 16
        v_xq1 = [0] * 16
        for tc in range(16):
            h2r_w[tc] = dma("sync", f"DH{tc % 2}", h2st[tc % 2][:], h2_d[tc],
                            waits=([("V", v_xq1[tc - 2])] if tc >= 2 else
                                   tot("DH0", "DH1")))
            y1 = ve(lambda e, t2=tc: e.tensor_scalar(
                xq1st[t2 % 2][:], h2st[t2 % 2][:], s1t[:], MAGIC, ALU.mult, ALU.add),
                waits=[h2r_w[tc], ("V", bs2), ("V", C["V"])])
            y2 = ve(lambda e, t2=tc: e.tensor_scalar(
                xq1st[t2 % 2][:], xq1st[t2 % 2][:], MAGIC, rs1t[:],
                ALU.subtract, ALU.mult),
                waits=[("V", y1)])
            ps_ = psC if (tc % 2 == 0) else psD
            tt = 0
            for ec in range(8):
                tt = te(lambda e, t2=tc, k=ec, P=ps_: e.transpose(
                    P[:, 128 * k:128 * k + 128],
                    xq1st[t2 % 2][:, 128 * k:128 * k + 128], ident[:]),
                    waits=([("V", y2)] +
                           ([("T", C["T"])] if tc <= 1 else []) +
                           ([("V", v_xq1[tc - 2])] if tc >= 2 else
                            [("V", v_sfbc)]))
                    if ec == 0 else [],
                    inc="T" if ec == 7 else None)
            v_xq1[tc] = ve(lambda e, t2=tc, P=ps_: e.tensor_copy(
                xq1f_all[:].rearrange("p (g c) -> p g c", c=T_OWN)
                [:, :, 128 * t2:128 * t2 + 128],
                P[:].rearrange("p (g c) -> p g c", c=128)),
                waits=[("T", tt)])
        v_xq1e = C["V"]

        v_mm1 = [0] * 32
        s_mm1 = 0
        qc_hw = []
        for fc in range(32):
            qe, qs = [("scalar", "DW0"), ("sync", "DN0"),
                      ("scalar", "DW1"), ("sync", "DN1")][fc % 4]
            wv = dma(qe, qs, win_st[fc % 3][:],
                     WIN[:, 128 * fc:128 * fc + 128].rearrange("(k p) f -> p k f", p=128),
                     waits=([("V", v_mm1[fc - 3])] if fc >= 3 else
                            tot("DS0", "DS1") + [("V", v_xq1e)]))
            q1 = ve(lambda e, f=fc: e.tensor_tensor(
                wqf32[f % 2][:].rearrange("p (k f) -> p k f", k=8),
                win_st[f % 3][:].rearrange("p (k f) -> p k f", k=8),
                sf_bc[:, None, 128 * f:128 * f + 128].broadcast_to([128, 8, 128]),
                ALU.mult),
                waits=[wv, ("V", v_sfbc), ("V", C["V"])])
            q2_ = ve(lambda e, f=fc: e.tensor_scalar(
                wint1[f % 2][:], wqf32[f % 2][:], MAGIC, MAGIC, ALU.add, ALU.subtract),
                waits=[("V", q1)])
            pp = (psA, psB) if fc % 2 == 0 else (psC, psD)
            tl = 0
            for th in range(2):
                for ec in range(8):
                    for sg in range(2):
                        tl = te(lambda e, f=fc, t2=th, k=ec, g=sg, P=pp[th]: e.matmul(
                            P[:, 512 * g:512 * g + 512],
                            wint1[f % 2][:, 128 * k:128 * k + 128],
                            xq1f[k][:, 1024 * t2 + 512 * g:1024 * t2 + 512 * g + 512],
                            start=(k == 0), stop=(k == 7)),
                            waits=([("V", q2_)] +
                                   ([("S", s_mm1)] if fc >= 2 else [("T", C["T"])]))
                            if (th == 0 and ec == 0 and sg == 0) else [],
                            inc="T" if (th == 1 and ec == 7 and sg == 1) else None)
            for th in range(2):
                s_mm1 = sc(lambda e, f=fc, t2=th, P=pp[th]: e.activation(
                    hidst[f % 2][:, 1024 * t2:1024 * t2 + 1024], P[:], ACT.Gelu,
                    bias=b_inT[:, f:f + 1], scale=rsf[:, f:f + 1]),
                    waits=([("T", tl), ("V", v_bint), ("S", C["S"])] if th == 0 else
                           [("S", C["S"])]) +
                          ([qc_hw[-2]] if (th == 0 and len(qc_hw) >= 2) else []))
            vr = ve(lambda e, f=fc: e.tensor_reduce(
                am2a[:, 2 * f:2 * f + 1], hidst[f % 2][:, 0:1024], AX.X, ALU.max,
                apply_absolute_value=True),
                waits=[("S", s_mm1), ("V", C["V"])])
            v_mm1[fc] = ve(lambda e, f=fc: e.tensor_reduce(
                am2a[:, 2 * f + 1:2 * f + 2], hidst[f % 2][:, 1024:2048], AX.X, ALU.max,
                apply_absolute_value=True),
                waits=[("V", vr)])
            qc_hw.append(dma("scalar", f"DI{fc % 2}", hid_d[fc], hidst[fc % 2][:],
                             waits=[("V", v_mm1[fc])]))
        t_mm1 = C["T"]
        pass

        w_am2 = global_amax(am2a[:, 0:64], cc2i, cc2o, 5, [("V", v_mm1[31])])
        e1 = ve(lambda e: e.reciprocal(sc_s[0:1, 6:7], sc_s[0:1, 5:6]),
                waits=[w_am2])
        e2 = ve(lambda e: e.tensor_scalar(sc_s[0:1, 6:7], sc_s[0:1, 6:7], 32767.0,
                                          None, ALU.mult), waits=[("V", e1)])
        e3 = ve(lambda e: e.tensor_scalar(sc_s[0:1, 7:8], sc_s[0:1, 5:6], 1.0 / 32767.0,
                                          None, ALU.mult), waits=[("V", e2)])
        bs3 = bcast_scalar(s2t, 6, ("V", e3))
        bs4 = bcast_scalar(rs2t, 7, ("V", bs3))
        e4 = bs4

        out_w = []
        v_mm2 = defaultdict(int)
        qg_weh = 0
        t_eh0 = 0
        qx_d = {}
        tl_d = {}
        y1_d = {}
        for eh in range(2):
            qg_weh = dma("sync", "DZ",
                         w2eh[:].rearrange("p (c f) -> p c f", c=32),
                         w2_d[:, :, 512 * eh:512 * eh + 512].rearrange("c p f -> p c f"),
                         waits=tot("DI0", "DI1") + [("V", v_mm1[31])] if eh == 0 else
                               [("T", t_eh0), ("V", C["V"])])
            vsw = ve(lambda e: e.tensor_scalar(
                w2eh[:], w2eh[:], rs2t[:], None, ALU.mult),
                waits=[qg_weh, ("V", e4), ("V", C["V"])])
            for tc in range(16):
                pp = psA if tc % 2 == 0 else psB
                if eh == 0:
                    hw = dma("sync", f"DI{tc % 2}", hidld[tc % 2][:],
                             hid_d[:, :, 128 * tc:128 * tc + 128].rearrange("c f t -> f c t"),
                             waits=([("S", y1_d[tc - 2])] if tc >= 2 else
                                    tot("DI0", "DI1")))
                    y1_d[tc] = sc(lambda e, t2=tc: e.activation(
                        hidy[:], hidld[t2 % 2][:], ACT.Copy,
                        bias=MAGIC, scale=s2t[:]),
                        waits=[hw, ("V", e4), ("S", C["S"])])
                    y2s = ve(lambda e, t2=tc: e.tensor_scalar(
                        xq2t[t2 % 2][:], hidy[:], MAGIC, None, ALU.subtract),
                        waits=[("S", y1_d[tc]), ("V", C["V"])] +
                              ([qx_d[(0, tc - 2)], ("T", tl_d[(0, tc - 2)])]
                               if tc >= 2 else []))
                    qx = dma("scalar", f"DV{tc % 2}", xq2_d[tc], xq2t[tc % 2][:],
                             waits=[("V", y2s)])
                    wmm = [("V", y2s)]
                else:
                    qx = dma("sync", f"DV{tc % 2}", xq2t[tc % 2][:], xq2_d[tc],
                             waits=([("T", tl_d[(1, tc - 2)])] if tc >= 2 else
                                    tot("DV0", "DV1") + [("T", t_eh0)]))
                    wmm = [qx]
                xw = dma("sync", f"DY{tc % 2}", xst[tc % 2][:, 0:512],
                         x2_d[tc, :, 512 * eh:512 * eh + 512],
                         waits=(tot("DY0", "DY1") if (eh == 0 and tc <= 1) else []) +
                               ([("V", v_mm2[(eh, tc - 2)])] if tc >= 2 else []))
                tl = 0
                for f5 in range(32):
                    tl = te(lambda e, f6=f5, t2=tc, P=pp: e.matmul(
                        P[:, 0:512], xq2t[t2 % 2][:, 128 * f6:128 * f6 + 128],
                        w2eh[:, 512 * f6:512 * f6 + 512],
                        start=(f6 == 0), stop=(f6 == 31)),
                        waits=(wmm + [("V", vsw)] +
                               ([("T", t_mm1)] if (eh == 0 and tc <= 1) else []) +
                               ([("V", v_mm2[(eh, tc - 2)])] if tc >= 2 else []))
                        if f5 == 0 else [],
                        inc="T" if f5 == 31 else None)
                qx_d[(eh, tc)] = qx
                tl_d[(eh, tc)] = tl
                o1 = ve(lambda e, t2=tc, P=pp, g=eh: e.scalar_tensor_tensor(
                    scrap[t2 % 2][:, 0:512], P[:, 0:512], 0.0,
                    bc_gb[:, 512 * g:512 * g + 512], ALU.add, ALU.mult),
                    waits=[("T", tl), ("V", C["V"])])
                o2 = ve(lambda e, t2=tc, g=eh: e.tensor_tensor(
                    scrap[t2 % 2][:, 0:512], scrap[t2 % 2][:, 0:512],
                    bc_gbo[:, 512 * g:512 * g + 512], ALU.add),
                    waits=[("V", o1)])
                o3 = ve(lambda e, t2=tc, g=eh: e.tensor_tensor(
                    outst[t2 % 2][:], scrap[t2 % 2][:, 0:512], xst[t2 % 2][:, 0:512],
                    ALU.add),
                    waits=[("V", o2), xw] +
                          ([out_w[-2]] if len(out_w) >= 2 else []))
                v_mm2[(eh, tc)] = o3
                out_w.append(dma("sync", f"DU{tc % 2}",
                                 OUT[128 * tc:128 * tc + 128, 512 * eh:512 * eh + 512],
                                 outst[tc % 2][:], waits=[("V", o3)]))
                if tc == 15:
                    t_eh0 = C["T"]
        du0, du1 = C["DU0"], C["DU1"]
        p.op("sync", lambda e: e.wait_ge(sems["DU0"], du0))
        p.op("sync", lambda e: e.wait_ge(sems["DU1"], du1))

        emit(block, p, sems)
    return nc


_BUILT = None


def kernel(L0_attn_x, L0_attn_w_qkv, L0_attn_w_out, L0_ffn_W_in, L0_ffn_b_in,
           L0_ffn_W_out, L0_ffn_b_out, L0_c, L0_wmod):
    global _BUILT
    if _BUILT is None:
        _BUILT = build()
    nc = _BUILT
    x = np.ascontiguousarray(np.asarray(L0_attn_x, dtype=np.float32))
    wqkv = np.ascontiguousarray(np.asarray(L0_attn_w_qkv, dtype=np.float32))
    wout = np.ascontiguousarray(np.asarray(L0_attn_w_out, dtype=np.float32))
    win = np.ascontiguousarray(np.asarray(L0_ffn_W_in, dtype=np.float32))
    b_in = np.ascontiguousarray(np.asarray(L0_ffn_b_in, dtype=np.float32).reshape(1, FF))
    wo2 = np.ascontiguousarray(np.asarray(L0_ffn_W_out, dtype=np.float32))
    b_out = np.ascontiguousarray(np.asarray(L0_ffn_b_out, dtype=np.float32).reshape(1, D))
    cv = np.ascontiguousarray(np.asarray(L0_c, dtype=np.float32))
    wmod = np.ascontiguousarray(np.asarray(L0_wmod, dtype=np.float32))
    in_maps = []
    for c in range(8):
        b, h = c // 2, c % 2
        xb = x[b]
        xperm = np.ascontiguousarray(
            np.concatenate([xb[T_OWN * h:T_OWN * h + T_OWN],
                            xb[T_OWN * (1 - h):T_OWN * (1 - h) + T_OWN]], axis=0))
        in_maps.append({
            "x": xperm, "wqkv": wqkv, "wout": wout, "win": win, "bin": b_in,
            "wo2": wo2, "bout": b_out, "cvec": np.ascontiguousarray(cv[b].reshape(1, D)),
            "wmod": wmod,
        })
    res = run_bass_kernel_spmd(nc, in_maps, core_ids=list(range(8)))
    out = np.empty((4, T_FULL, D), dtype=np.float32)
    for c in range(8):
        b, h = c // 2, c % 2
        out[b, T_OWN * h:T_OWN * h + T_OWN] = res.results[c]["out"]
    return out


# version 18
# speedup vs baseline: 1.0306x; 1.0306x over previous
import contextlib
from collections import defaultdict

import numpy as np
import concourse.bass as bass
import concourse.mybir as mybir
from concourse.bass_utils import run_bass_kernel_spmd

F32 = mybir.dt.float32
BF16 = mybir.dt.bfloat16
AX = mybir.AxisListType
ALU = mybir.AluOpType
ACT = mybir.ActivationFunctionType

MAGIC = 12582912.0
NCOR = 8
D = 1024
FF = 4096
T_FULL = 4096
T_OWN = 2048
EPS = 1e-5
ENGS = ("sync", "gpsimd", "vector", "scalar", "tensor")
KB = 1024


class ColView:

    def __init__(self, t, off, n):
        self.t = t
        self.off = off
        self.n = n

    def __getitem__(self, idx):
        if isinstance(idx, tuple):
            p, c = idx
        else:
            p, c = idx, slice(None)
        cs = c.start if c.start is not None else 0
        ce = c.stop if c.stop is not None else self.n
        return self.t[p, self.off + cs:self.off + ce]


class Prog:
    def __init__(self):
        self.ops = {e: [] for e in ENGS}
        self.cnt = defaultdict(int)

    def op(self, eng, fn, waits=(), inc=None, inc_n=1):
        w = [(s, v) for (s, v) in waits if v and v > 0]
        self.ops[eng].append((w, fn, inc, inc_n))
        if inc is not None:
            self.cnt[inc] += inc_n
            return self.cnt[inc]
        return None


def emit(block, prog, sems):
    def make_body(ops):
        def body(eng):
            for waits, fn, inc, inc_n in ops:
                for s, v in waits:
                    eng.wait_ge(sems[s], v)
                inst = fn(eng)
                if inc is not None:
                    inst.then_inc(sems[inc], inc_n)
        return body
    for e in ENGS:
        getattr(block, e)(make_body(prog.ops[e]))


def build(debug=False):
    nc = bass.Bass("TRN2", target_bir_lowering=False)

    X = nc.dram_tensor("x", [T_FULL, D], F32, kind="ExternalInput").ap()
    WQKV = nc.dram_tensor("wqkv", [3 * D, D], F32, kind="ExternalInput").ap()
    WOUT = nc.dram_tensor("wout", [D, D], F32, kind="ExternalInput").ap()
    WIN = nc.dram_tensor("win", [D, FF], F32, kind="ExternalInput").ap()
    BIN = nc.dram_tensor("bin", [1, FF], F32, kind="ExternalInput").ap()
    WO2 = nc.dram_tensor("wo2", [FF, D], F32, kind="ExternalInput").ap()
    BOUT = nc.dram_tensor("bout", [1, D], F32, kind="ExternalInput").ap()
    CVEC = nc.dram_tensor("cvec", [1, D], F32, kind="ExternalInput").ap()
    WMOD = nc.dram_tensor("wmod", [D, 6 * D], F32, kind="ExternalInput").ap()
    OUT = nc.dram_tensor("out", [T_OWN, D], F32, kind="ExternalOutput").ap()

    kd = "ExternalOutput" if debug else "Internal"
    kv_k = nc.dram_tensor("kv_k", [8, 128, T_FULL], BF16, kind=kd).ap()
    q_d = nc.dram_tensor("q_d", [8, 128, T_OWN], BF16, kind=kd).ap()
    kv_v = nc.dram_tensor("kv_v", [32, 128, 8 * 130], BF16, kind=kd).ap()
    at_d = nc.dram_tensor("at_d", [8, 128, T_OWN], BF16, kind=kd).ap()
    x2_d = nc.dram_tensor("x2_d", [16, 128, D], F32, kind=kd).ap()
    h2_d = nc.dram_tensor("h2_d", [16, 128, D], BF16, kind=kd).ap()
    hid_d = nc.dram_tensor("hid_d", [32, 128, T_OWN], BF16, kind=kd).ap()
    w2_d = nc.dram_tensor("w2_d", [32, 128, D], BF16, kind=kd).ap()
    xq2_d = nc.dram_tensor("xq2_d", [16, 128, FF], BF16, kind=kd).ap()
    cc1i = nc.dram_tensor("cc1i", [1, 1], F32)
    cc1o = nc.dram_tensor("cc1o", [1, 1], F32)
    cc2i = nc.dram_tensor("cc2i", [1, 1], F32)
    cc2o = nc.dram_tensor("cc2o", [1, 1], F32)

    with contextlib.ExitStack() as ctx:
        sb = lambda n, s, d: ctx.enter_context(nc.sbuf_tensor(n, s, d))

        ident = sb("ident", [128, 128], F32)
        ones_mat = sb("ones_mat", [128, 128], BF16)
        ones1f = sb("ones1f", [1, 128], F32)
        e65 = sb("e65", [65, 128], F32)
        negones = sb("negones", [1, 128], BF16)
        bc_sa = sb("bc_sa", [128, D], F32)
        bc_ca = sb("bc_ca", [128, D], F32)
        bc_ga = sb("bc_ga", [128, D], F32)
        bc_sbv = sb("bc_sbv", [128, D], F32)
        bc_cb = sb("bc_cb", [128, D], F32)
        bc_gb = sb("bc_gb", [128, D], F32)
        bc_gbo = sb("bc_gbo", [128, D], F32)
        xst = [sb(f"xst{i}", [128, D], F32) for i in range(2)]
        scrap = [sb(f"scrap{i}", [128, D], F32) for i in range(2)]
        h1t32 = [sb(f"h1t32{i}", [128, D], F32) for i in range(2)]
        ln_a = sb("ln_a", [128, 1], F32)
        ln_b = sb("ln_b", [128, 1], F32)
        ln_c = sb("ln_c", [128, 1], F32)
        ln_d = sb("ln_d", [128, 1], F32)
        nmq = sb("nmq", [128, 1], F32)
        nmk = sb("nmk", [128, 1], F32)
        nmt = sb("nmt", [128, 1], F32)
        negmA = [sb(f"negmA{i}", [128, 1], F32) for i in range(2)]
        negmB = [sb(f"negmB{i}", [128, 1], F32) for i in range(2)]
        amx = sb("amx", [128, 1], F32)
        amrow = sb("amrow", [1, 128], F32)
        sc_s = sb("sc_s", [1, 8], F32)
        s1t = sb("s1t", [128, 1], F32)
        rs1t = sb("rs1t", [128, 1], F32)
        s2t = sb("s2t", [128, 1], F32)
        rs2t = sb("rs2t", [128, 1], F32)
        am1 = sb("am1", [128, 16], F32)
        am2a = sb("am2a", [128, 64], F32)
        red1 = sb("red1", [128, 1], F32)
        eps_t = sb("eps_t", [128, 1], F32)
        nb26 = sb("nb26", [128, 1], F32)
        magict = sb("magict", [128, 1], F32)
        nmagict = sb("nmagict", [128, 1], F32)
        rsf = sb("rsf", [128, 32], F32)
        amaxr = sb("amaxr", [128, 32], F32)
        srr = sb("srr", [128, 32], F32)
        rsr = sb("rsr", [128, 32], F32)
        rs2rsr = sb("rs2rsr", [128, 32], F32)
        b_inT = sb("b_inT", [128, 32], F32)

        top = nc.sbuf_top
        Z1 = top - 58 * KB
        Z2 = Z1 - 56 * KB
        Z3 = Z2 - 36 * KB

        def at(name, shape, dtype, off):
            return nc.alloc_sbuf_tensor_at(name, shape, dtype, offset=off)

        wmod_st = [at(f"wmod_st{i}", [128, 1536], F32, Z1 + 6 * KB * i) for i in range(4)]
        wm16 = [at(f"wm16_{i}", [128, 1536], BF16, Z1 + 24 * KB + 3 * KB * i)
                for i in range(2)]
        kbufA = [at(f"kbufA{i}", [65, T_FULL], BF16, Z1 + 8 * KB * i) for i in range(2)]
        kbufB = [at(f"kbufB{i}", [65, T_FULL], BF16, Z1 + 16 * KB + 8 * KB * i) for i in range(2)]
        vbuf = [at(f"vbuf{i}", [128, 32 * 130], BF16, Z1 + 32 * KB + 8320 * i) for i in range(2)]
        probs = [at(f"probs{i}", [128, KB], BF16, Z1 + 49408 + 2 * KB * i) for i in range(4)]
        probsP = [at(f"probsP{i}", [128, 2 * KB], BF16, Z1 + 49408 + 4 * KB * i)
                  for i in range(2)]
        sksq = [at(f"sksq{i}", [65, 512], BF16, Z1 + 55552 + KB * i) for i in range(2)]
        hidld = [at(f"hidld{i}", [128, FF], BF16, Z1 + 8 * KB * i) for i in range(2)]
        hidy = at("hidy", [128, FF], F32, Z1 + 16 * KB)
        atres = at("atres", [128, 16 * KB], BF16, Z1)
        xq2t = [at(f"xq2t{i}", [128, FF], BF16, Z1 + 32 * KB + 8 * KB * i) for i in range(2)]
        outst = [at(f"outst{i}", [128, 512], F32, Z1 + 48 * KB + 2 * KB * i) for i in range(2)]

        wqkvT_all = at("wqkvT_all", [128, 24 * KB], BF16, Z2)
        wqkvT = [ColView(wqkvT_all, 3 * D * e, 3 * D) for e in range(8)]
        wq_st = [at(f"wq_st{i}", [128, D], F32, Z2 + 48 * KB + 4 * KB * i) for i in range(2)]
        m1q = at("m1q", [128, KB], F32, Z2)
        w1st = [at(f"w1st{i}", [128, KB], F32, Z2 + 4 * KB + 4 * KB * i) for i in range(2)]
        wo2_st = [at(f"wo2_st{i}", [128, D], F32, Z2 + 12 * KB + 4 * KB * i) for i in range(2)]
        amaxf_row = at("amaxf_row", [1, FF], F32, Z2 + 20 * KB)
        woutT = at("woutT", [128, 8 * D], BF16, Z2 + 36 * KB)
        xq1f_all = at("xq1f_all", [128, 16 * KB], BF16, Z2)
        xq1f = [ColView(xq1f_all, T_OWN * e, T_OWN) for e in range(8)]
        xq1st = [at(f"xq1st{i}", [128, D], F32, Z2 + 32 * KB + 4 * KB * i) for i in range(2)]
        win_st = [at(f"win_st{i}", [128, KB], F32, Z2 + 40 * KB + 4 * KB * i) for i in range(3)]
        wqf32 = [at(f"wqf32{i}", [128, KB], F32, Z2 + 32 * KB + 4 * KB * i) for i in range(2)]
        sq16 = at("sq16", [128, KB], BF16, Z2 + 52 * KB)

        silu_c32 = at("silu_c32", [1, D], F32, Z3)
        gate_tmp = at("gate_tmp", [1, D], F32, Z3 + 4 * KB)
        silu_cT = at("silu_cT", [128, 8], F32, Z3 + 8 * KB)
        silu_cT16 = at("silu_cT16", [128, 8], BF16, Z3 + 8 * KB + 64)
        mod_raw = at("mod_raw", [1, 6 * D], F32, Z3 + 12 * KB)
        h1seg_all = at("h1seg_all", [128, 8 * KB], BF16, Z3)
        h1seg = [ColView(h1seg_all, KB * e, KB) for e in range(8)]
        kstage = [at(f"kstage{i}", [128, 512], BF16, Z3 + 16 * KB + KB * i) for i in range(2)]
        qstage = [at(f"qstage{i}", [128, 512], BF16, Z3 + 18 * KB + KB * i) for i in range(2)]
        vstage = [at(f"vstage{i}", [128, 8 * 130], BF16, Z3 + 20 * KB + 2080 * i) for i in range(2)]
        qaugA = [at(f"qaugA{i}", [65, T_OWN], BF16, Z3 + 4 * KB * i) for i in range(2)]
        qaugB = [at(f"qaugB{i}", [65, T_OWN], BF16, Z3 + 8 * KB + 4 * KB * i) for i in range(2)]
        den_sb = at("den_sb", [65, 2 * KB], F32, Z3 + 16 * KB)
        attAst = at("attAst", [64, KB], BF16, Z2 + 54 * KB)
        rbc = at("rbc", [128, KB], F32, Z3 + 24 * KB)
        atstage = [at(f"atstage{i}", [128, T_OWN], BF16, Z3 + 28 * KB + 4 * KB * i) for i in range(2)]
        h2st = [at(f"h2st{i}", [128, D], BF16, Z3 + 28 * KB + 2 * KB * i) for i in range(2)]
        wint1 = [at(f"wint1{i}", [128, KB], BF16, Z3 + 2 * KB * i) for i in range(2)]
        hidst = [at(f"hidst{i}", [128, T_OWN], BF16, Z3 + 4 * KB + 4 * KB * i) for i in range(2)]
        sf_bc = at("sf_bc", [128, FF], F32, Z3 + 12 * KB)
        w2eh = at("w2eh", [128, 32 * 512], BF16, Z3 + 0)

        assert nc.sbuf_bytes_remaining >= (top - Z3), f"sbuf overflow: {nc.sbuf_bytes_remaining} < {top - Z3}"

        psAB = ctx.enter_context(nc.psum_tensor("psAB", [128, 2 * KB], F32))
        psA = ColView(psAB, 0, KB)
        psB = ColView(psAB, KB, KB)
        psC = ctx.enter_context(nc.psum_tensor("psC", [128, KB], F32))
        psD = ctx.enter_context(nc.psum_tensor("psD", [128, KB], F32))

        semnames = (["T", "V", "S", "G", "DM", "DZ", "DC"] +
                    [f"D{c}{i}" for c in "WNXKQVOASYHIUGE" for i in range(2)])
        sems = {n: ctx.enter_context(nc.semaphore(n)) for n in semnames}
        block = ctx.enter_context(nc.Block())

        p = Prog()
        C = p.cnt

        def dma(qeng, qsem, out, in_, waits=(), **kw):
            n = p.op(qeng, lambda e, o=out, i=in_, k=kw: e.dma_start(out=o, in_=i, **k),
                     waits=waits, inc=qsem, inc_n=16)
            return (qsem, n)

        def tot(*names):
            return [(s, C[s]) for s in names if C[s] > 0]

        def ve(fn, waits=()):
            return p.op("vector", fn, waits=waits, inc="V")

        def sc(fn, waits=()):
            return p.op("scalar", fn, waits=waits, inc="S")

        def te(fn, waits=(), inc="T"):
            return p.op("tensor", fn, waits=waits, inc=inc)

        g_ms = p.op("gpsimd", lambda e: e.memset(ident[:], 0.0), inc="G")
        g_id = p.op("gpsimd", lambda e: e.affine_select(
            out=ident[:], in_=ident[:], compare_op=ALU.not_equal, fill=1.0,
            base=0, pattern=[[-1, 128]], channel_multiplier=1),
            waits=[("G", g_ms)], inc="G")
        p.op("vector", lambda e: e.memset(ones_mat[:], 1.0))
        p.op("vector", lambda e: e.memset(negones[:], -1.0))
        p.op("vector", lambda e: e.memset(ones1f[:], 1.0))
        v_e65 = ve(lambda e: e.memset(e65[:], 0.0))
        p.op("vector", lambda e: e.memset(e65[0:1, :], 1.0),
             waits=[("V", v_e65)])
        p.op("vector", lambda e: e.memset(eps_t[:], EPS))
        p.op("vector", lambda e: e.memset(magict[:], MAGIC))
        p.op("vector", lambda e: e.memset(nmagict[:], -MAGIC))
        p.op("vector", lambda e: e.memset(nb26[:], -26.0))
        v_consts = ve(lambda e: e.memset(amx[:], 0.0))

        wq_fin = [0] * 24
        for fc in range(24):
            wv = dma("sync", f"DW{fc % 2}", wq_st[fc % 2][:], WQKV[128 * fc:128 * fc + 128, :],
                     waits=[("V", wq_fin[fc - 2])] if fc >= 2 else [])
            ps_ = psC if (fc % 2 == 0) else psD
            tt = 0
            for ec in range(8):
                tt = te(lambda e, f=fc, k=ec, P=ps_: e.transpose(
                    P[:, 128 * k:128 * k + 128],
                    wq_st[f % 2][:, 128 * k:128 * k + 128], ident[:]),
                    waits=([wv, ("G", g_id)] +
                           ([("V", wq_fin[fc - 2])] if fc >= 2 else []))
                    if ec == 0 else [],
                    inc="T" if ec == 7 else None)
            wq_fin[fc] = ve(lambda e, f=fc, P=ps_: e.tensor_copy(
                wqkvT_all[:].rearrange("p (g c) -> p g c", c=3 * D)
                [:, :, 128 * f:128 * f + 128],
                P[:].rearrange("p (g c) -> p g c", c=128)),
                waits=[("T", tt)])
        v_wqkvT = C["V"]
        t_s1a = C["T"]

        w_c = dma("sync", "DM", silu_c32[:], CVEC)
        s_sil = sc(lambda e: e.activation(silu_c32[:], silu_c32[:], ACT.Silu),
                   waits=[w_c])
        last_sct = 0
        for k in range(8):
            tt = te(lambda e, kk=k: e.transpose(
                psA[:, 0:1], silu_c32[0:1, 128 * kk:128 * kk + 128], ident[0:1, 0:1]),
                waits=[("S", s_sil), ("G", g_id)] +
                      ([("V", last_sct)] if k > 0 else []))
            last_sct = ve(lambda e, kk=k: e.tensor_copy(silu_cT[:, kk:kk + 1], psA[:, 0:1]),
                          waits=[("T", tt)])
        v_sct = ve(lambda e: e.tensor_copy(silu_cT16[:], silu_cT[:]),
                   waits=[("V", last_sct)])

        s_mod = 0
        t_modmm = 0
        wm_w = {}
        vc_d = {}
        tmm = {}

        def wm_load(ch):
            q2, rc = divmod(ch, 8)
            qeng, qsem = [("scalar", "DN0"), ("gpsimd", "DG0"),
                          ("scalar", "DN1"), ("gpsimd", "DG1")][ch % 4]
            wm_w[ch] = dma(qeng, qsem, wmod_st[ch % 4][:],
                           WMOD[128 * rc:128 * rc + 128,
                                1536 * q2:1536 * q2 + 1536],
                           waits=[("V", vc_d[ch - 4])] if ch >= 4 else [])

        for ch0 in range(4):
            wm_load(ch0)
        for q2 in range(4):
            for rc in range(8):
                ch = 8 * q2 + rc
                vc = ve(lambda e, c2=ch: e.tensor_copy(wm16[c2 % 2][:],
                                                       wmod_st[c2 % 4][:]),
                        waits=[wm_w[ch], ("V", v_sct)] +
                              ([("T", tmm[ch - 2])] if ch >= 2 else []))
                vc_d[ch] = vc
                if ch + 4 < 32:
                    wm_load(ch + 4)
                for sg in range(3):
                    pst = [psB, psB, psC][sg]
                    off = [0, 512, 0][sg]
                    t_modmm = te(lambda e, c2=ch, P=pst, o=off, s=sg: e.matmul(
                        P[0:1, o:o + 512], silu_cT16[:, c2 % 8:c2 % 8 + 1],
                        wm16[c2 % 2][:, 512 * s:512 * s + 512],
                        start=(c2 % 8 == 0), stop=(c2 % 8 == 7)),
                        waits=([("V", vc)] if sg == 0 else []) +
                              ([("T", t_s1a), ("S", C["S"])]
                               if (ch == 0 and sg == 0) else []) +
                              ([("S", s_mod)]
                               if (rc == 0 and q2 > 0 and sg == 0) else []),
                        inc="T" if sg == 2 else None)
                tmm[ch] = t_modmm
            for sg in range(3):
                pst = [psB, psB, psC][sg]
                off = [0, 512, 0][sg]
                s_mod = sc(lambda e, s=sg, P=pst, o=off, q3=q2: e.copy(
                    mod_raw[0:1, 1536 * q3 + 512 * s:1536 * q3 + 512 * s + 512],
                    P[0:1, o:o + 512]),
                    waits=[("T", t_modmm)])
        t_mod_ps = C["T"]

        def gate_fix(off):
            a = sc(lambda e, o=off: e.activation(
                gate_tmp[0:1, :], mod_raw[0:1, o:o + D], ACT.Sign),
                waits=[("S", s_mod), ("V", C["V"])])
            c = sc(lambda e, o=off: e.activation(
                mod_raw[0:1, o:o + D], mod_raw[0:1, o:o + D], ACT.Abs),
                waits=[("S", a)])
            d = ve(lambda e, o=off: e.tensor_scalar_max(
                mod_raw[0:1, o:o + D], mod_raw[0:1, o:o + D], 0.25),
                waits=[("S", c)])
            f = ve(lambda e, o=off: e.tensor_tensor(
                mod_raw[0:1, o:o + D], mod_raw[0:1, o:o + D],
                gate_tmp[0:1, :], ALU.mult),
                waits=[("V", d)])
            return f
        gate_fix(2 * D)
        gate_fix(5 * D)
        ve(lambda e: e.tensor_scalar_add(mod_raw[0:1, D:2 * D],
                                         mod_raw[0:1, D:2 * D], 1.0),
           waits=[("S", s_mod)])
        v_modfix = ve(lambda e: e.tensor_scalar_add(mod_raw[0:1, 4 * D:5 * D],
                                                    mod_raw[0:1, 4 * D:5 * D], 1.0),
                      waits=[("S", s_mod)])

        def bcast_row(dst, row_ap, waits0):
            te(lambda e: e.matmul(psA[:, 0:512], ones1f[:], row_ap[0:1, 0:512],
                                  start=True, stop=True),
               waits=list(waits0), inc=None)
            t2 = te(lambda e: e.matmul(psA[:, 512:1024], ones1f[:], row_ap[0:1, 512:1024],
                                       start=True, stop=True))
            return ve(lambda e, d=dst: e.tensor_copy(d[:], psA[:]), waits=[("T", t2)])

        v_bc = bcast_row(bc_sa, mod_raw[0:1, 0:D],
                         [("V", v_modfix), ("S", C["S"]), ("V", C["V"]), ("T", t_mod_ps)])
        v_bc = bcast_row(bc_ca, mod_raw[0:1, D:2 * D], [("V", v_bc)])
        v_bc_ln1 = v_bc
        v_bc = bcast_row(bc_ga, mod_raw[0:1, 2 * D:3 * D], [("V", v_bc)])
        v_bc = bcast_row(bc_sbv, mod_raw[0:1, 3 * D:4 * D], [("V", v_bc)])
        v_bc = bcast_row(bc_cb, mod_raw[0:1, 4 * D:5 * D], [("V", v_bc)])
        v_bc = bcast_row(bc_gb, mod_raw[0:1, 5 * D:6 * D], [("V", v_bc)])
        w_bo = dma("sync", "DM", gate_tmp[0:1, :], BOUT, waits=[("V", C["V"])])
        v_bo = bcast_row(bc_gbo, gate_tmp[0:1, :], [w_bo, ("V", v_bc)])
        v_gbo = ve(lambda e: e.tensor_tensor(bc_gbo[:], bc_gbo[:], bc_gb[:], ALU.mult),
                   waits=[("V", v_bo)])
        v_bc_all = v_gbo

        v_vst = 0
        for sl in range(2):
            v_vst = ve(lambda e, s=sl: e.memset(
                vstage[s][:].rearrange("r (p c) -> r p c", c=130)[:, :, 64:65], 1.0),
                waits=[("V", v_bc_all)] if sl == 0 else [])
            v_vst = ve(lambda e, s=sl: e.memset(
                vstage[s][:].rearrange("r (p c) -> r p c", c=130)[:, :, 129:130], 1.0))
        x_w = [0] * 32
        v_xfree = [0] * 32
        v_h1seg = [0] * 8
        ev_h1 = [0] * 32
        s_kq = 0
        qs_k_w = [0] * 100
        qc_v_w = []
        qg_q_w = []
        nqs_k = 0
        for i in range(32):
            s = i // 4
            x_w[i] = dma("sync" if i % 2 == 0 else "gpsimd",
                         "DX0" if i % 2 == 0 else "DE0",
                         xst[i % 2][:], X[128 * i:128 * i + 128, :],
                         waits=[("V", v_xfree[i - 2])] if i >= 2 else [])
            r1 = ve(lambda e, ii=i: e.tensor_reduce(
                ln_a[:], xst[ii % 2][:], AX.X, ALU.add, negate=True),
                waits=[x_w[i], ("V", C["V"])])
            s1 = sc(lambda e, ii=i: e.activation(
                scrap[ii % 2][:], xst[ii % 2][:], ACT.Square, accum_out=ln_b[:]),
                waits=[x_w[i], ("S", C["S"]), ("V", C["V"] - 1)])
            r2 = ve(lambda e: e.tensor_scalar(ln_a[:], ln_a[:], 1.0 / D, None, ALU.mult),
                    waits=[("V", r1)])
            r3 = ve(lambda e: e.tensor_tensor(ln_d[:], ln_a[:], ln_a[:], ALU.mult),
                    waits=[("V", r2)])
            r4 = ve(lambda e: e.tensor_scalar(ln_c[:], ln_b[:], 1.0 / D, None, ALU.mult),
                    waits=[("S", s1), ("V", r3)])
            r5 = ve(lambda e: e.tensor_tensor(ln_c[:], ln_c[:], ln_d[:], ALU.subtract),
                    waits=[("V", r4)])
            s2_ = sc(lambda e: e.activation(ln_c[:], ln_c[:], ACT.Sqrt, bias=eps_t[:]),
                     waits=[("V", r5), ("S", C["S"])])
            r6 = ve(lambda e: e.reciprocal(ln_c[:], ln_c[:]), waits=[("S", s2_)])
            z1 = ve(lambda e, ii=i: e.tensor_scalar(
                scrap[ii % 2][:], xst[ii % 2][:], ln_a[:], ln_c[:], ALU.add, ALU.mult),
                waits=[("V", r6)])
            v_xfree[i] = z1
            z2 = ve(lambda e, ii=i: e.tensor_tensor(
                h1t32[ii % 2][:], scrap[ii % 2][:], bc_ca[:], ALU.mult),
                waits=[("V", z1), ("V", v_bc_ln1)])
            z3 = ve(lambda e, ii=i: e.tensor_tensor(
                h1t32[ii % 2][:], h1t32[ii % 2][:], bc_sa[:], ALU.add),
                waits=[("V", z2)])
            ps_ = psC if (i % 2 == 0) else psD
            tt = 0
            for ec in range(8):
                tt = te(lambda e, ii=i, k=ec, P=ps_: e.transpose(
                    P[:, 128 * k:128 * k + 128],
                    h1t32[ii % 2][:, 128 * k:128 * k + 128], ident[:]),
                    waits=([("V", z3)] +
                           ([("T", t_mod_ps), ("V", v_bc_all)] if i == 0 else []) +
                           ([("V", ev_h1[i - 2])] if i >= 2 else []) +
                           ([("V", C["V"])] if (i % 4 == 0 and i > 0) else []))
                    if ec == 0 else [],
                    inc="T" if ec == 7 else None)
            ev_h1[i] = ve(lambda e, ii=i, P=ps_: e.tensor_copy(
                h1seg_all[:].rearrange("p (g c) -> p g c", c=KB)
                [:, :, 512 * ((ii // 4) % 2) + 128 * (ii % 4):
                       512 * ((ii // 4) % 2) + 128 * (ii % 4) + 128],
                P[:].rearrange("p (g c) -> p g c", c=128)),
                waits=[("T", tt)])
            if i % 4 != 3:
                continue
            v_h1seg[s] = C["V"]
            soff = 512 * (s % 2)
            for pr in range(8):
                ps_ = psA if (pr % 2 == 0) else psB
                tl = 0
                for ec in range(8):
                    tl = te(lambda e, P2=pr, k=ec, P=ps_, so=soff: e.matmul(
                        P[:, 0:512], wqkvT[k][:, D + 128 * P2:D + 128 * P2 + 128],
                        h1seg[k][:, so:so + 512],
                        start=(k == 0), stop=(k == 7)),
                        waits=([("V", v_h1seg[s]), ("V", v_wqkvT), ("S", s_kq)]
                               if ec == 0 else []),
                        inc="T" if ec == 7 else None)
                s_kq = sc(lambda e, P2=pr, P=ps_: e.copy(kstage[P2 % 2][:], P[:, 0:512]),
                          waits=[("T", tl)] +
                                ([qs_k_w[nqs_k - 2]] if nqs_k >= 2 else []))
                qs_k_w[nqs_k] = dma("sync", f"DK{pr % 2}", kv_k[pr, :, 512 * s:512 * s + 512],
                                    kstage[pr % 2][:], waits=[("S", s_kq)])
                nqs_k += 1
            if s < 4:
                for pr in range(8):
                    ps_ = psA if (pr % 2 == 0) else psB
                    tl = 0
                    for ec in range(8):
                        tl = te(lambda e, P2=pr, k=ec, P=ps_, so=soff: e.matmul(
                            P[:, 512:1024], wqkvT[k][:, 128 * P2:128 * P2 + 128],
                            h1seg[k][:, so:so + 512],
                            start=(k == 0), stop=(k == 7)),
                            waits=[("S", s_kq)] if ec == 0 else [],
                            inc="T" if ec == 7 else None)
                    s_kq = sc(lambda e, P2=pr, P=ps_: e.copy(qstage[P2 % 2][:],
                                                             P[:, 512:1024]),
                              waits=[("T", tl)] +
                                    ([qg_q_w[-2]] if len(qg_q_w) >= 2 else []))
                    qg_q_w.append(dma("sync", f"DQ{pr % 2}",
                                      q_d[pr, :, 512 * s:512 * s + 512],
                                      qstage[pr % 2][:], waits=[("S", s_kq)]))
            for it in range(4):
                ii = 4 * s + it
                tl = 0
                for ec in range(8):
                    for sg in range(2):
                        tl = te(lambda e, k=ec, g=sg, ad=soff + 128 * it: e.matmul(
                            psC[:, 512 * g:512 * g + 512],
                            h1seg[k][:, ad:ad + 128],
                            wqkvT[k][:, 2 * D + 512 * g:2 * D + 512 * g + 512],
                            start=(k == 0), stop=(k == 7)),
                            waits=([("V", C["V"])] if (ec == 0 and sg == 0) else []),
                            inc="T" if (ec == 7 and sg == 1) else None)
                vv1 = ve(lambda e, ii2=ii: e.tensor_copy(
                    vstage[ii2 % 2][:].rearrange("r (p c) -> r p c", c=130)[:, :, 0:64],
                    psC[:].rearrange("r (p c) -> r p c", c=128)[:, :, 0:64]),
                    waits=[("T", tl), ("V", v_vst)] +
                          ([qc_v_w[-2]] if len(qc_v_w) >= 2 else []))
                vv2 = ve(lambda e, ii2=ii: e.tensor_copy(
                    vstage[ii2 % 2][:].rearrange("r (p c) -> r p c", c=130)[:, :, 65:129],
                    psC[:].rearrange("r (p c) -> r p c", c=128)[:, :, 64:128]),
                    waits=[("V", vv1)])
                qc_v_w.append(dma("scalar", f"DV{ii % 2}", kv_v[ii, :, :],
                                  vstage[ii % 2][:], waits=[("V", vv2)]))
        v_s2 = C["V"]
        s_s2 = C["S"]
        t_s2 = C["T"]
        pass
        pass
        pass

        wo_fin = [0] * 8
        for ec in range(8):
            wv = dma("scalar", f"DO{ec % 2}", wo2_st[ec % 2][:], WOUT[128 * ec:128 * ec + 128, :],
                     waits=([("V", wo_fin[ec - 2])] if ec >= 2 else
                            [("V", v_s2)]))
            last = 0
            for fc in range(8):
                ps_ = psC if (fc % 2 == 0) else psD
                tt = te(lambda e, k=ec, f=fc, P=ps_: e.transpose(
                    P[:, 0:128], wo2_st[k % 2][:, 128 * f:128 * f + 128], ident[:]),
                    waits=[wv, ("T", t_s2)] +
                          ([("V", C["V"] - 1)] if (ec > 0 or fc >= 2) else []))
                last = ve(lambda e, k=ec, f=fc, P=ps_: e.tensor_copy(
                    woutT[:, D * f + 128 * k:D * f + 128 * k + 128], P[:, 0:128]),
                    waits=[("T", tt)])
            wo_fin[ec] = last
        v_woutT = C["V"]

        w_bi = dma("scalar", "DM", amaxf_row[:], BIN, waits=[("V", v_s2)])
        last_bi = 0
        for k in range(32):
            tt = te(lambda e, kk=k: e.transpose(
                psC[:, 0:1], amaxf_row[0:1, 128 * kk:128 * kk + 128], ident[0:1, 0:1]),
                waits=[w_bi, ("T", C["T"])] +
                      ([("V", last_bi)] if k > 0 else [("V", v_woutT)]))
            last_bi = ve(lambda e, kk=k: e.tensor_copy(b_inT[:, kk:kk + 1], psC[:, 0:1]),
                         waits=[("T", tt)])
        v_bint = last_bi

        kB_w = [0] * 8
        vv_w = [0] * 8
        qB_w = [0] * 8
        at_sp = [0] * 8
        v_kone = ve(lambda e: e.memset(kbufA[0][64:65, :], -1.0),
                    waits=[("T", C["T"]), ("V", C["V"]), ("S", C["S"])] +
                          tot("DK0", "DK1", "DQ0", "DQ1", "DV0", "DV1"))
        v_kone = ve(lambda e: e.memset(kbufA[1][64:65, :], -1.0))
        v_kone = ve(lambda e: e.memset(kbufB[0][64:65, :], -1.0))
        v_kone = ve(lambda e: e.memset(kbufB[1][64:65, :], -1.0))
        v_kone = ve(lambda e: e.memset(den_sb[:], 0.0))
        dma("sync", "DK0", kbufA[0][0:64, :], kv_k[0, 0:64, :],
            waits=tot("DK0", "DK1"))
        kB_w[0] = dma("sync", "DK0", kbufB[0][0:64, :], kv_k[0, 64:128, :])
        vv_w[0] = dma("sync", "DV0", vbuf[0][:].rearrange("r (i c) -> r i c", c=130),
                      kv_v[:, :, 0:130].rearrange("i r c -> r i c"),
                      waits=tot("DV0", "DV1"))
        dma("sync", "DQ0", qaugA[0][0:64, :], q_d[0, 0:64, :],
            waits=tot("DQ0", "DQ1"))
        qB_w[0] = dma("sync", "DQ0", qaugB[0][0:64, :], q_d[0, 64:128, :])
        v_attev = v_bint
        pvA_d = 0
        pvB_d = 0
        exA_last = 0
        exB_last = 0
        pv_ring = [0, 0, 0]
        for pr in range(8):
            slot = pr % 2
            if pr + 1 < 8:
                sl2 = (pr + 1) % 2
                dma("sync", f"DK{sl2}", kbufA[sl2][0:64, :], kv_k[pr + 1, 0:64, :],
                    waits=[("V", C["V"]), ("S", C["S"])])
                kB_w[pr + 1] = dma("sync", f"DK{sl2}", kbufB[sl2][0:64, :],
                                   kv_k[pr + 1, 64:128, :])
                vv_w[pr + 1] = dma("sync", f"DV{sl2}",
                                   vbuf[sl2][:].rearrange("r (i c) -> r i c", c=130),
                                   kv_v[:, :, 130 * (pr + 1):130 * (pr + 1) + 130]
                                   .rearrange("i r c -> r i c"),
                                   waits=[("T", C["T"])])
                dma("sync", f"DQ{sl2}", qaugA[sl2][0:64, :], q_d[pr + 1, 0:64, :],
                    waits=[("V", C["V"]), ("S", C["S"])])
                qB_w[pr + 1] = dma("sync", f"DQ{sl2}", qaugB[sl2][0:64, :],
                                   q_d[pr + 1, 64:128, :])
            w_free = [] if pr > 0 else [("V", v_kone)]
            s_nm = 0
            for head in range(2):
                kb = kbufA if head == 0 else kbufB
                qa = qaugA if head == 0 else qaugB
                mmn = [0] * 8
                lastv = 0
                for sg in range(8):
                    ps_ = psA if sg % 2 == 0 else psB
                    tt_in = ve(lambda e, g=sg, s2l=slot, K2=kb: e.tensor_tensor(
                        sksq[g % 2][:], K2[s2l][0:65, 512 * g:512 * g + 512],
                        K2[s2l][0:65, 512 * g:512 * g + 512], ALU.mult),
                        waits=([kB_w[pr], ("V", C["V"])] + w_free
                               if (head == 0 and sg == 0) else []) +
                              ([("T", mmn[sg - 2])] if sg >= 2 else []) +
                              ([("T", qmm1), ("S", s_nm)]
                               if (head == 1 and sg <= 1) else []))
                    mmn[sg] = te(lambda e, g=sg, P=ps_: e.matmul(
                        P[:, 0:512], ones_mat[0:65, :], sksq[g % 2][0:65, :],
                        start=True, stop=True),
                        waits=[("V", tt_in)] +
                              ([("V", v_attev)] if (head == 0 and sg <= 1) else []))
                    if sg == 0:
                        lastv = ve(lambda e, P=ps_: e.tensor_reduce(
                            nmk[:], P[:, 0:512], AX.X, ALU.max),
                            waits=[("T", mmn[sg]), ("V", C["V"])])
                    else:
                        rv = ve(lambda e, P=ps_: e.tensor_reduce(
                            nmt[:], P[:, 0:512], AX.X, ALU.max),
                            waits=[("T", mmn[sg]), ("V", lastv)])
                        lastv = ve(lambda e: e.tensor_tensor(
                            nmk[:], nmk[:], nmt[:], ALU.max), waits=[("V", rv)])
                vsc = ve(lambda e: e.tensor_scalar(nmt[:], nmk[:], 1.0404, None,
                                                   ALU.mult), waits=[("V", lastv)])
                qmm = [0] * 4
                for sg in range(4):
                    ps_ = psA if sg < 2 else psB
                    tq_in = ve(lambda e, g=sg, s2l=slot, Q2=qa: e.tensor_tensor(
                        sksq[g % 2][0:64, :], Q2[s2l][0:64, 512 * g:512 * g + 512],
                        Q2[s2l][0:64, 512 * g:512 * g + 512], ALU.mult),
                        waits=([qB_w[pr], ("V", vsc), ("T", mmn[6]), ("T", mmn[7])]
                               if sg == 0 else []) +
                              ([("V", vsc)] if sg == 1 else []) +
                              ([("T", qmm[sg - 2])] if sg >= 2 else []))
                    qmm[sg] = te(lambda e, g=sg, P=ps_: e.matmul(
                        P[:, 512 * (g % 2):512 * (g % 2) + 512],
                        ones_mat[0:65, :], sksq[g % 2][0:65, :],
                        start=True, stop=True),
                        waits=[("V", tq_in)] + ([("S", s_nm)] if sg == 0 else []))
                s_nm = sc(lambda e, Q2=qa, s2l=slot: e.activation(
                    Q2[s2l][64:65, 0:2048], psAB[64:65, 0:2048],
                    ACT.Sqrt, scale=nmt[64:65, :]),
                    waits=[("T", qmm[3]), ("S", C["S"])])
                qmm1 = qmm[3]
            for hf in range(2):
                qoff = 1024 * hf
                d1 = 0
                for head in range(2):
                    kb = kbufA if head == 0 else kbufB
                    qa = qaugA if head == 0 else qaugB
                    pp = psC if head == 0 else psD
                    slots = [psA, psB, psD] if head == 0 else [psA, psB, psC]
                    ex = [0] * 32
                    pvj = [0] * 32

                    def emit_pv(jj, head=head, pp=pp, ex=ex, pvj=pvj):
                        t_ = 0
                        for sg in range(2):
                            t_ = te(lambda e, j2=jj, g=sg, s2l=slot, h=head,
                                    P=pp: e.matmul(
                                P[0:65, 512 * g:512 * g + 512],
                                vbuf[s2l][:, 130 * j2 + 65 * h:130 * j2 + 65 * h + 65],
                                probs[j2 % 3][:, 512 * g:512 * g + 512],
                                start=(j2 == 0), stop=(j2 == 31)),
                                waits=([("S", ex[jj])] if sg == 0 else []) +
                                      ([("V", v_attev)] +
                                       ([vv_w[pr]] if hf == 0 else [])
                                       if (jj == 0 and sg == 0) else []))
                        pvj[jj] = t_
                        pv_ring[jj % 3] = t_

                    for j in range(32):
                        ps_ = slots[j % 3]
                        if j >= 3:
                            w_sc = [("S", ex[j - 3])]
                        elif j >= 1:
                            w_sc = []
                        elif head == 1:
                            w_sc = [("S", exA_last), ("V", av1)]
                        elif hf == 0:
                            w_sc = [("S", s_nm), ("V", C["V"])]
                        else:
                            w_sc = [("S", exB_last), ("V", v_attev)]
                        tsa = 0
                        for sg in range(2):
                            tsa = te(lambda e, jj=j, g=sg, s2l=slot, qo=qoff,
                                     P=ps_, K2=kb, Q2=qa: e.matmul(
                                P[:, 512 * g:512 * g + 512],
                                K2[s2l][0:65, 128 * jj:128 * jj + 128],
                                Q2[s2l][0:65, qo + 512 * g:qo + 512 * g + 512],
                                start=True, stop=True),
                                waits=w_sc if sg == 0 else [],
                                inc="T" if sg == 1 else None)
                        if j >= 2:
                            emit_pv(j - 2)
                        ex[j] = sc(lambda e, jj=j, P=ps_: e.activation(
                            probs[jj % 3][:], P[:], ACT.Exp,
                            bias=0.0, scale=0.125),
                            waits=[("T", tsa)] +
                                  ([("T", pv_ring[j % 3])] if pv_ring[j % 3] else []))
                    emit_pv(30)
                    emit_pv(31)
                    if head == 0:
                        exA_last = ex[31]
                        pvA_d = pvj[31]
                        d1 = ve(lambda e: e.tensor_copy(den_sb[0:1, 0:1024],
                                                        psC[64:65, :]),
                                waits=[("T", pvj[31]), ("V", C["V"])])
                        av1 = ve(lambda e: e.tensor_copy(attAst[:], psC[0:64, :]),
                                 waits=[("V", d1)])
                    else:
                        exB_last = ex[31]
                        pvB_d = pvj[31]
                d2 = ve(lambda e: e.tensor_copy(den_sb[0:1, 1024:2048], psD[64:65, :]),
                        waits=[("T", pvB_d), ("V", d1)])
                tb1 = te(lambda e: e.matmul(psA[:, 0:512], e65[:],
                                            den_sb[0:65, 0:512], start=True, stop=True),
                         waits=[("V", d2), ("S", exB_last)], inc=None)
                tb1 = te(lambda e: e.matmul(psA[:, 512:1024], e65[:],
                                            den_sb[0:65, 512:1024], start=True, stop=True))
                tb2 = te(lambda e: e.matmul(psB[:, 0:512], e65[:],
                                            den_sb[0:65, 1024:1536], start=True, stop=True),
                         inc=None)
                tb2 = te(lambda e: e.matmul(psB[:, 512:1024], e65[:],
                                            den_sb[0:65, 1536:2048], start=True, stop=True))
                rb1 = ve(lambda e: e.tensor_copy(rbc[0:64, :], psA[0:64, :]),
                         waits=[("T", tb1)])
                rb2 = ve(lambda e: e.tensor_copy(rbc[64:128, :], psB[64:128, :]),
                         waits=[("T", tb2), ("V", rb1)])
                rr = ve(lambda e: e.reciprocal(rbc[:], rbc[:]), waits=[("V", rb2)])
                a1 = ve(lambda e, s2l=slot, o=qoff: e.tensor_tensor(
                    atstage[s2l][0:64, o:o + 1024], attAst[:], rbc[0:64, :],
                    ALU.mult),
                    waits=[("V", rr)] +
                          ([at_sp[pr - 2]] if (hf == 0 and pr >= 2) else []))
                v_attev = ve(lambda e, s2l=slot, o=qoff: e.scalar_tensor_tensor(
                    atstage[s2l][64:128, o:o + 1024], psD[0:64, :], 1.0, rbc[64:128, :],
                    ALU.bypass, ALU.mult),
                    waits=[("V", a1)])
            at_sp[pr] = dma("sync", f"DA{pr % 2}", at_d[pr], atstage[slot][:],
                            waits=[("V", v_attev)])
        v_attn_end = C["V"]
        t_attn_end = C["T"]
        s_attn_end = C["S"]
        pass

        v_m1 = ve(lambda e: e.memset(m1q[:], 0.0), waits=[("T", t_s2), ("V", C["V"])])
        for fq in range(4):
            for ec in range(8):
                wv = dma("scalar", f"DN{ec % 2}", w1st[ec % 2][:],
                         WIN[128 * ec:128 * ec + 128, 1024 * fq:1024 * fq + 1024],
                         waits=[("V", v_m1)])
                vsq = ve(lambda e, k=ec: e.tensor_tensor(
                    sq16[:], w1st[k % 2][:], w1st[k % 2][:], ALU.mult),
                    waits=[wv, ("V", C["V"])])
                v_m1 = ve(lambda e: e.tensor_tensor(
                    m1q[:], m1q[:], sq16[:], ALU.max),
                    waits=[("V", vsq)])
            for b in range(8):
                fc = 8 * fq + b
                tt = te(lambda e, bb=b: e.transpose(
                    psC[:, 0:128], m1q[:, 128 * bb:128 * bb + 128], ident[:]),
                    waits=[("V", v_m1), ("T", t_attn_end), ("V", C["V"])])
                rv = ve(lambda e: e.tensor_reduce(red1[:], psC[:, 0:128], AX.X, ALU.max),
                        waits=[("T", tt)])
                rvs = sc(lambda e: e.activation(red1[:], red1[:], ACT.Sqrt),
                         waits=[("V", rv), ("S", C["S"])])
                rv2 = ve(lambda e: e.tensor_scalar_max(red1[:], red1[:], 1e-8),
                         waits=[("S", rvs)])
                rv3 = ve(lambda e, f=fc: e.tensor_scalar(
                    rsf[:, f:f + 1], red1[:], 1.0 / 127.0, None, ALU.mult),
                    waits=[("V", rv2)])
                tt2 = te(lambda e: e.transpose(psC[0:1, 512:640], red1[:], ident[:]),
                         waits=[("V", rv3)])
                v_m1 = ve(lambda e, f=fc: e.tensor_copy(
                    amaxf_row[0:1, 128 * f:128 * f + 128], psC[0:1, 512:640]),
                    waits=[("T", tt2)])
            if fq < 3:
                v_m1 = ve(lambda e: e.memset(m1q[:], 0.0), waits=[("V", v_m1)])
        f1 = ve(lambda e: e.tensor_scalar_max(amaxf_row[:], amaxf_row[:], 1e-8),
                waits=[("V", v_m1)])
        f2 = ve(lambda e: e.reciprocal(amaxf_row[:], amaxf_row[:]), waits=[("V", f1)])
        v_sfrow = ve(lambda e: e.tensor_scalar(amaxf_row[:], amaxf_row[:], 127.0, None,
                                               ALU.mult), waits=[("V", f2)])

        v_wo = 0
        for fc in range(32):
            wv = dma("scalar", f"DO{fc % 2}", wo2_st[fc % 2][:], WO2[128 * fc:128 * fc + 128, :],
                     waits=[("V", v_wo)] if fc >= 2 else [("V", v_woutT)])
            v_wo = ve(lambda e, f=fc: e.tensor_reduce(
                amaxr[:, f:f + 1], wo2_st[f % 2][:], AX.X, ALU.max,
                apply_absolute_value=True),
                waits=[wv])
        c1 = ve(lambda e: e.tensor_scalar_max(amaxr[:], amaxr[:], 1e-8),
                waits=[("V", v_wo)])
        c2 = ve(lambda e: e.reciprocal(srr[:], amaxr[:]), waits=[("V", c1)])
        c3 = ve(lambda e: e.tensor_scalar(rsr[:], amaxr[:], 1.0 / 127.0, None, ALU.mult),
                waits=[("V", c2)])
        c4 = ve(lambda e: e.tensor_scalar(srr[:], srr[:], 127.0, None, ALU.mult),
                waits=[("V", c3)])
        v_w2q = c4
        qg_w2 = []
        for fc in range(32):
            wv = dma("scalar", f"DO{fc % 2}", wo2_st[fc % 2][:], WO2[128 * fc:128 * fc + 128, :],
                     waits=[("V", v_w2q)] if fc >= 2 else [("V", c4)])
            y1 = ve(lambda e, f=fc: e.tensor_scalar(
                w1st[f % 2][:], wo2_st[f % 2][:], srr[:, f:f + 1], MAGIC,
                ALU.mult, ALU.add),
                waits=[wv, ("V", C["V"])] +
                      ([qg_w2[-2]] if len(qg_w2) >= 2 else []))
            v_w2q = ve(lambda e, f=fc: e.tensor_scalar(
                hidst[f % 2][:, 0:1024], w1st[f % 2][:], MAGIC, rsr[:, f:f + 1],
                ALU.subtract, ALU.mult),
                waits=[("V", y1)] + (tot("DA0", "DA1") if fc < 2 else []))
            qg_w2.append(dma("scalar", f"DS{fc % 2}", w2_d[fc], hidst[fc % 2][:, 0:1024],
                             waits=[("V", v_w2q)]))
        pass

        at_w = [0] * 16
        x4_w = [0] * 16
        v_x2w = [0] * 16
        qs_x2 = []
        qc_h2 = []
        v_tc = 0
        for tc in range(16):
            pp = psA if tc % 2 == 0 else psB
            if tc == 0:
                w0_at = ([("T", t_attn_end), ("S", s_attn_end), ("V", v_attn_end)] +
                         tot("DA0", "DA1", "DE0"))
                for c4 in range(8):
                    dma("sync" if c4 % 2 == 0 else "gpsimd",
                        "DA0" if c4 % 2 == 0 else "DE0",
                        atres[:, T_OWN * c4:T_OWN * c4 + T_OWN], at_d[c4],
                        waits=w0_at if c4 <= 1 else [])
                w_atres = tot("DA0", "DE0")
            x4_w[tc] = dma("sync", f"DX{tc % 2}", xst[tc % 2][:], X[128 * tc:128 * tc + 128, :],
                           waits=[("V", v_x2w[tc - 2])] if tc >= 2 else [("V", v_s2)])
            tl = 0
            for fc in range(8):
                for sg in range(2):
                    tl = te(lambda e, f=fc, g=sg, t2=tc, P=pp: e.matmul(
                        P[:, 512 * g:512 * g + 512],
                        atres[:, T_OWN * f + 128 * t2:T_OWN * f + 128 * t2 + 128],
                        woutT[:, D * f + 512 * g:D * f + 512 * g + 512],
                        start=(f == 0), stop=(f == 7)),
                        waits=(([("V", v_woutT)] +
                                (w_atres + [("T", t_attn_end), ("S", s_attn_end),
                                            ("V", v_attn_end)] if tc == 0 else []) +
                               ([("V", v_x2w[tc - 2])] if tc >= 2 else [])))
                        if (fc == 0 and sg == 0) else [],
                        inc="T" if (fc == 7 and sg == 1) else None)
            g1 = ve(lambda e, t2=tc, P=pp: e.scalar_tensor_tensor(
                scrap[t2 % 2][:], P[:], 0.0, bc_ga[:], ALU.add, ALU.mult),
                waits=[("T", tl), ("V", C["V"]), ("V", v_bc_all)])
            g2 = ve(lambda e, t2=tc: e.tensor_tensor(
                h1t32[t2 % 2][:], scrap[t2 % 2][:], xst[t2 % 2][:], ALU.add),
                waits=[("V", g1), x4_w[tc]] +
                      ([qs_x2[tc - 2]] if tc >= 2 else []))
            qs_x2.append(dma("scalar", f"DY{tc % 2}", x2_d[tc], h1t32[tc % 2][:],
                             waits=[("V", g2)]))
            r1 = ve(lambda e, t2=tc: e.tensor_reduce(
                ln_a[:], h1t32[t2 % 2][:], AX.X, ALU.add, negate=True),
                waits=[("V", g2)])
            sq = sc(lambda e, t2=tc: e.activation(
                scrap[t2 % 2][:], h1t32[t2 % 2][:], ACT.Square, accum_out=ln_b[:]),
                waits=[("V", g2), ("S", C["S"])])
            r2 = ve(lambda e: e.tensor_scalar(ln_a[:], ln_a[:], 1.0 / D, None, ALU.mult),
                    waits=[("V", r1)])
            r3 = ve(lambda e: e.tensor_tensor(ln_d[:], ln_a[:], ln_a[:], ALU.mult),
                    waits=[("V", r2)])
            r4 = ve(lambda e: e.tensor_scalar(ln_c[:], ln_b[:], 1.0 / D, None, ALU.mult),
                    waits=[("S", sq), ("V", r3)])
            r5 = ve(lambda e: e.tensor_tensor(ln_c[:], ln_c[:], ln_d[:], ALU.subtract),
                    waits=[("V", r4)])
            s3_ = sc(lambda e: e.activation(ln_c[:], ln_c[:], ACT.Sqrt, bias=eps_t[:]),
                     waits=[("V", r5), ("S", C["S"])])
            r6 = ve(lambda e: e.reciprocal(ln_c[:], ln_c[:]), waits=[("S", s3_)])
            z1 = ve(lambda e, t2=tc: e.tensor_scalar(
                scrap[t2 % 2][:], h1t32[t2 % 2][:], ln_a[:], ln_c[:], ALU.add, ALU.mult),
                waits=[("V", r6)])
            z2 = ve(lambda e, t2=tc: e.tensor_tensor(
                scrap[t2 % 2][:], scrap[t2 % 2][:], bc_cb[:], ALU.mult),
                waits=[("V", z1)])
            z3 = ve(lambda e, t2=tc: e.tensor_tensor(
                h2st[t2 % 2][:], scrap[t2 % 2][:], bc_sbv[:], ALU.add),
                waits=[("V", z2)] + ([qc_h2[-2]] if len(qc_h2) >= 2 else []))
            v_tc = ve(lambda e, t2=tc: e.tensor_reduce(
                am1[:, t2:t2 + 1], h2st[t2 % 2][:], AX.X, ALU.max,
                apply_absolute_value=True),
                waits=[("V", z3)])
            v_x2w[tc] = v_tc
            qc_h2.append(dma("scalar", f"DH{tc % 2}", h2_d[tc], h2st[tc % 2][:],
                             waits=[("V", v_tc)]))
        v_s4 = C["V"]
        pass

        def global_amax(src, cci, cco, dst_col, waits0):
            a1_ = ve(lambda e: e.tensor_reduce(amx[:], src, AX.X, ALU.max),
                     waits=list(waits0) + [("V", C["V"])])
            tt = te(lambda e: e.transpose(psC[0:1, 0:128], amx[:, 0:1], ident[:]),
                    waits=[("V", a1_), ("T", C["T"]), ("V", C["V"])])
            a2_ = ve(lambda e: e.tensor_copy(amrow[:], psC[0:1, 0:128]),
                     waits=[("T", tt)])
            a3_ = ve(lambda e: e.tensor_reduce(sc_s[0:1, 0:1], amrow[:], AX.X, ALU.max),
                     waits=[("V", a2_)])
            a4_ = ve(lambda e: e.tensor_scalar_max(sc_s[0:1, 0:1], sc_s[0:1, 0:1], 1e-8),
                     waits=[("V", a3_)])
            w1 = dma("gpsimd", "DC", cci.ap(), sc_s[0:1, 0:1], waits=[("V", a4_)])
            gc = p.op("gpsimd", lambda e: e.collective_compute(
                "AllReduce", ALU.max, replica_groups=[list(range(NCOR))],
                ins=[cci.ap()], outs=[cco.ap()]),
                waits=[w1], inc="G")
            return dma("gpsimd", "DC", sc_s[0:1, dst_col:dst_col + 1], cco.ap(),
                       waits=[("G", gc)])

        def bcast_scalar(dst, col, wv):
            tt = te(lambda e, c2=col: e.matmul(psC[:, 0:1], ones1f[:],
                                               sc_s[0:1, c2:c2 + 1],
                                               start=True, stop=True),
                    waits=[wv, ("T", C["T"])])
            return ve(lambda e, d=dst: e.tensor_copy(d[:], psC[:, 0:1]),
                      waits=[("T", tt)])

        w_am1 = global_amax(am1[:, 0:16], cc1i, cc1o, 1, [("V", v_s4)])
        b1 = ve(lambda e: e.reciprocal(sc_s[0:1, 2:3], sc_s[0:1, 1:2]),
                waits=[w_am1])
        b2 = ve(lambda e: e.tensor_scalar(sc_s[0:1, 2:3], sc_s[0:1, 2:3], 32767.0,
                                          None, ALU.mult), waits=[("V", b1)])
        b3 = ve(lambda e: e.tensor_scalar(sc_s[0:1, 3:4], sc_s[0:1, 1:2], 1.0 / 32767.0,
                                          None, ALU.mult), waits=[("V", b2)])
        bs1 = bcast_scalar(s1t, 2, ("V", b3))
        bs2 = bcast_scalar(rs1t, 3, ("V", bs1))

        v_sfbc = 0
        for qd in range(4):
            te(lambda e, q=qd: e.matmul(psD[:, 0:512], ones1f[:],
                                        amaxf_row[0:1, 1024 * q:1024 * q + 512],
                                        start=True, stop=True),
               waits=[("V", v_sfrow), ("T", C["T"])] if qd == 0 else
                     [("V", v_sfbc)], inc=None)
            t2 = te(lambda e, q=qd: e.matmul(psD[:, 512:1024], ones1f[:],
                                             amaxf_row[0:1, 1024 * q + 512:1024 * q + 1024],
                                             start=True, stop=True))
            v_sfbc = ve(lambda e, q=qd: e.tensor_copy(
                sf_bc[:, 1024 * q:1024 * q + 1024], psD[:]), waits=[("T", t2)])

        h2r_w = [0] * 16
        v_xq1 = [0] * 16
        for tc in range(16):
            h2r_w[tc] = dma("sync", f"DH{tc % 2}", h2st[tc % 2][:], h2_d[tc],
                            waits=([("V", v_xq1[tc - 2])] if tc >= 2 else
                                   tot("DH0", "DH1")))
            y1 = ve(lambda e, t2=tc: e.tensor_scalar(
                xq1st[t2 % 2][:], h2st[t2 % 2][:], s1t[:], MAGIC, ALU.mult, ALU.add),
                waits=[h2r_w[tc], ("V", bs2), ("V", C["V"])])
            y2 = ve(lambda e, t2=tc: e.tensor_scalar(
                xq1st[t2 % 2][:], xq1st[t2 % 2][:], MAGIC, rs1t[:],
                ALU.subtract, ALU.mult),
                waits=[("V", y1)])
            ps_ = psC if (tc % 2 == 0) else psD
            tt = 0
            for ec in range(8):
                tt = te(lambda e, t2=tc, k=ec, P=ps_: e.transpose(
                    P[:, 128 * k:128 * k + 128],
                    xq1st[t2 % 2][:, 128 * k:128 * k + 128], ident[:]),
                    waits=([("V", y2)] +
                           ([("T", C["T"])] if tc <= 1 else []) +
                           ([("V", v_xq1[tc - 2])] if tc >= 2 else
                            [("V", v_sfbc)]))
                    if ec == 0 else [],
                    inc="T" if ec == 7 else None)
            v_xq1[tc] = ve(lambda e, t2=tc, P=ps_: e.tensor_copy(
                xq1f_all[:].rearrange("p (g c) -> p g c", c=T_OWN)
                [:, :, 128 * t2:128 * t2 + 128],
                P[:].rearrange("p (g c) -> p g c", c=128)),
                waits=[("T", tt)])
        v_xq1e = C["V"]

        v_mm1 = [0] * 32
        s_mm1 = 0
        qc_hw = []
        for fc in range(32):
            qe, qs = [("scalar", "DW0"), ("sync", "DN0"),
                      ("scalar", "DW1"), ("sync", "DN1")][fc % 4]
            wv = dma(qe, qs, win_st[fc % 3][:],
                     WIN[:, 128 * fc:128 * fc + 128].rearrange("(k p) f -> p k f", p=128),
                     waits=([("V", v_mm1[fc - 3])] if fc >= 3 else
                            tot("DS0", "DS1") + [("V", v_xq1e)]))
            q1 = ve(lambda e, f=fc: e.tensor_tensor(
                wqf32[f % 2][:].rearrange("p (k f) -> p k f", k=8),
                win_st[f % 3][:].rearrange("p (k f) -> p k f", k=8),
                sf_bc[:, None, 128 * f:128 * f + 128].broadcast_to([128, 8, 128]),
                ALU.mult),
                waits=[wv, ("V", v_sfbc), ("V", C["V"])])
            q2_ = ve(lambda e, f=fc: e.tensor_scalar(
                wint1[f % 2][:], wqf32[f % 2][:], MAGIC, MAGIC, ALU.add, ALU.subtract),
                waits=[("V", q1)])
            pp = (psA, psB) if fc % 2 == 0 else (psC, psD)
            tl = 0
            for th in range(2):
                for ec in range(8):
                    for sg in range(2):
                        tl = te(lambda e, f=fc, t2=th, k=ec, g=sg, P=pp[th]: e.matmul(
                            P[:, 512 * g:512 * g + 512],
                            wint1[f % 2][:, 128 * k:128 * k + 128],
                            xq1f[k][:, 1024 * t2 + 512 * g:1024 * t2 + 512 * g + 512],
                            start=(k == 0), stop=(k == 7)),
                            waits=([("V", q2_)] +
                                   ([("S", s_mm1)] if fc >= 2 else [("T", C["T"])]))
                            if (th == 0 and ec == 0 and sg == 0) else [],
                            inc="T" if (th == 1 and ec == 7 and sg == 1) else None)
            for th in range(2):
                s_mm1 = sc(lambda e, f=fc, t2=th, P=pp[th]: e.activation(
                    hidst[f % 2][:, 1024 * t2:1024 * t2 + 1024], P[:], ACT.Gelu,
                    bias=b_inT[:, f:f + 1], scale=rsf[:, f:f + 1]),
                    waits=([("T", tl), ("V", v_bint), ("S", C["S"])] if th == 0 else
                           [("S", C["S"])]) +
                          ([qc_hw[-2]] if (th == 0 and len(qc_hw) >= 2) else []))
            vr = ve(lambda e, f=fc: e.tensor_reduce(
                am2a[:, 2 * f:2 * f + 1], hidst[f % 2][:, 0:1024], AX.X, ALU.max,
                apply_absolute_value=True),
                waits=[("S", s_mm1), ("V", C["V"])])
            v_mm1[fc] = ve(lambda e, f=fc: e.tensor_reduce(
                am2a[:, 2 * f + 1:2 * f + 2], hidst[f % 2][:, 1024:2048], AX.X, ALU.max,
                apply_absolute_value=True),
                waits=[("V", vr)])
            qc_hw.append(dma("scalar", f"DI{fc % 2}", hid_d[fc], hidst[fc % 2][:],
                             waits=[("V", v_mm1[fc])]))
        t_mm1 = C["T"]
        pass

        w_am2 = global_amax(am2a[:, 0:64], cc2i, cc2o, 5, [("V", v_mm1[31])])
        e1 = ve(lambda e: e.reciprocal(sc_s[0:1, 6:7], sc_s[0:1, 5:6]),
                waits=[w_am2])
        e2 = ve(lambda e: e.tensor_scalar(sc_s[0:1, 6:7], sc_s[0:1, 6:7], 32767.0,
                                          None, ALU.mult), waits=[("V", e1)])
        e3 = ve(lambda e: e.tensor_scalar(sc_s[0:1, 7:8], sc_s[0:1, 5:6], 1.0 / 32767.0,
                                          None, ALU.mult), waits=[("V", e2)])
        bs3 = bcast_scalar(s2t, 6, ("V", e3))
        bs4 = bcast_scalar(rs2t, 7, ("V", bs3))
        e4 = bs4

        out_w = []
        v_mm2 = defaultdict(int)
        qg_weh = 0
        t_eh0 = 0
        qx_d = {}
        tl_d = {}
        y1_d = {}
        for eh in range(2):
            qg_weh = dma("sync", "DZ",
                         w2eh[:].rearrange("p (c f) -> p c f", c=32),
                         w2_d[:, :, 512 * eh:512 * eh + 512].rearrange("c p f -> p c f"),
                         waits=tot("DI0", "DI1") + [("V", v_mm1[31])] if eh == 0 else
                               [("T", t_eh0), ("V", C["V"])])
            vsw = ve(lambda e: e.tensor_scalar(
                w2eh[:], w2eh[:], rs2t[:], None, ALU.mult),
                waits=[qg_weh, ("V", e4), ("V", C["V"])])
            for tc in range(16):
                pp = psA if tc % 2 == 0 else psB
                if eh == 0:
                    hw = dma("sync" if tc % 2 == 0 else "gpsimd",
                             "DI0" if tc % 2 == 0 else "DE1",
                             hidld[tc % 2][:],
                             hid_d[:, :, 128 * tc:128 * tc + 128].rearrange("c f t -> f c t"),
                             waits=([("S", y1_d[tc - 2])] if tc >= 2 else
                                    tot("DI0", "DI1")))
                    y1_d[tc] = sc(lambda e, t2=tc: e.activation(
                        hidy[:], hidld[t2 % 2][:], ACT.Copy,
                        bias=MAGIC, scale=s2t[:]),
                        waits=[hw, ("V", e4), ("S", C["S"])])
                    y2s = ve(lambda e, t2=tc: e.tensor_scalar(
                        xq2t[t2 % 2][:], hidy[:], MAGIC, None, ALU.subtract),
                        waits=[("S", y1_d[tc]), ("V", C["V"])] +
                              ([qx_d[(0, tc - 2)], ("T", tl_d[(0, tc - 2)])]
                               if tc >= 2 else []))
                    qx = dma("scalar", f"DV{tc % 2}", xq2_d[tc], xq2t[tc % 2][:],
                             waits=[("V", y2s)])
                    wmm = [("V", y2s)]
                else:
                    qx = dma("sync", f"DV{tc % 2}", xq2t[tc % 2][:], xq2_d[tc],
                             waits=([("T", tl_d[(1, tc - 2)])] if tc >= 2 else
                                    tot("DV0", "DV1") + [("T", t_eh0)]))
                    wmm = [qx]
                xw = dma("sync", f"DY{tc % 2}", xst[tc % 2][:, 0:512],
                         x2_d[tc, :, 512 * eh:512 * eh + 512],
                         waits=(tot("DY0", "DY1") if (eh == 0 and tc <= 1) else []) +
                               ([("V", v_mm2[(eh, tc - 2)])] if tc >= 2 else []))
                tl = 0
                for f5 in range(32):
                    tl = te(lambda e, f6=f5, t2=tc, P=pp: e.matmul(
                        P[:, 0:512], xq2t[t2 % 2][:, 128 * f6:128 * f6 + 128],
                        w2eh[:, 512 * f6:512 * f6 + 512],
                        start=(f6 == 0), stop=(f6 == 31)),
                        waits=(wmm + [("V", vsw)] +
                               ([("T", t_mm1)] if (eh == 0 and tc <= 1) else []) +
                               ([("V", v_mm2[(eh, tc - 2)])] if tc >= 2 else []))
                        if f5 == 0 else [],
                        inc="T" if f5 == 31 else None)
                qx_d[(eh, tc)] = qx
                tl_d[(eh, tc)] = tl
                o1 = ve(lambda e, t2=tc, P=pp, g=eh: e.scalar_tensor_tensor(
                    scrap[t2 % 2][:, 0:512], P[:, 0:512], 0.0,
                    bc_gb[:, 512 * g:512 * g + 512], ALU.add, ALU.mult),
                    waits=[("T", tl), ("V", C["V"])])
                o2 = ve(lambda e, t2=tc, g=eh: e.tensor_tensor(
                    scrap[t2 % 2][:, 0:512], scrap[t2 % 2][:, 0:512],
                    bc_gbo[:, 512 * g:512 * g + 512], ALU.add),
                    waits=[("V", o1)])
                o3 = ve(lambda e, t2=tc, g=eh: e.tensor_tensor(
                    outst[t2 % 2][:], scrap[t2 % 2][:, 0:512], xst[t2 % 2][:, 0:512],
                    ALU.add),
                    waits=[("V", o2), xw] +
                          ([out_w[-2]] if len(out_w) >= 2 else []))
                v_mm2[(eh, tc)] = o3
                out_w.append(dma("sync", f"DU{tc % 2}",
                                 OUT[128 * tc:128 * tc + 128, 512 * eh:512 * eh + 512],
                                 outst[tc % 2][:], waits=[("V", o3)]))
                if tc == 15:
                    t_eh0 = C["T"]
        du0, du1 = C["DU0"], C["DU1"]
        p.op("sync", lambda e: e.wait_ge(sems["DU0"], du0))
        p.op("sync", lambda e: e.wait_ge(sems["DU1"], du1))

        emit(block, p, sems)
    return nc


_BUILT = None


def kernel(L0_attn_x, L0_attn_w_qkv, L0_attn_w_out, L0_ffn_W_in, L0_ffn_b_in,
           L0_ffn_W_out, L0_ffn_b_out, L0_c, L0_wmod):
    global _BUILT
    if _BUILT is None:
        _BUILT = build()
    nc = _BUILT
    x = np.ascontiguousarray(np.asarray(L0_attn_x, dtype=np.float32))
    wqkv = np.ascontiguousarray(np.asarray(L0_attn_w_qkv, dtype=np.float32))
    wout = np.ascontiguousarray(np.asarray(L0_attn_w_out, dtype=np.float32))
    win = np.ascontiguousarray(np.asarray(L0_ffn_W_in, dtype=np.float32))
    b_in = np.ascontiguousarray(np.asarray(L0_ffn_b_in, dtype=np.float32).reshape(1, FF))
    wo2 = np.ascontiguousarray(np.asarray(L0_ffn_W_out, dtype=np.float32))
    b_out = np.ascontiguousarray(np.asarray(L0_ffn_b_out, dtype=np.float32).reshape(1, D))
    cv = np.ascontiguousarray(np.asarray(L0_c, dtype=np.float32))
    wmod = np.ascontiguousarray(np.asarray(L0_wmod, dtype=np.float32))
    in_maps = []
    for c in range(8):
        b, h = c // 2, c % 2
        xb = x[b]
        xperm = np.ascontiguousarray(
            np.concatenate([xb[T_OWN * h:T_OWN * h + T_OWN],
                            xb[T_OWN * (1 - h):T_OWN * (1 - h) + T_OWN]], axis=0))
        in_maps.append({
            "x": xperm, "wqkv": wqkv, "wout": wout, "win": win, "bin": b_in,
            "wo2": wo2, "bout": b_out, "cvec": np.ascontiguousarray(cv[b].reshape(1, D)),
            "wmod": wmod,
        })
    res = run_bass_kernel_spmd(nc, in_maps, core_ids=list(range(8)))
    out = np.empty((4, T_FULL, D), dtype=np.float32)
    for c in range(8):
        b, h = c // 2, c % 2
        out[b, T_OWN * h:T_OWN * h + T_OWN] = res.results[c]["out"]
    return out


# version 19 (final)
# speedup vs baseline: 1.0378x; 1.0069x over previous
import contextlib
from collections import defaultdict

import numpy as np
import concourse.bass as bass
import concourse.mybir as mybir
from concourse.bass_utils import run_bass_kernel_spmd

F32 = mybir.dt.float32
BF16 = mybir.dt.bfloat16
AX = mybir.AxisListType
ALU = mybir.AluOpType
ACT = mybir.ActivationFunctionType

MAGIC = 12582912.0
NCOR = 8
D = 1024
FF = 4096
T_FULL = 4096
T_OWN = 2048
EPS = 1e-5
ENGS = ("sync", "gpsimd", "vector", "scalar", "tensor")
KB = 1024


class ColView:

    def __init__(self, t, off, n):
        self.t = t
        self.off = off
        self.n = n

    def __getitem__(self, idx):
        if isinstance(idx, tuple):
            p, c = idx
        else:
            p, c = idx, slice(None)
        cs = c.start if c.start is not None else 0
        ce = c.stop if c.stop is not None else self.n
        return self.t[p, self.off + cs:self.off + ce]


class Prog:
    def __init__(self):
        self.ops = {e: [] for e in ENGS}
        self.cnt = defaultdict(int)

    def op(self, eng, fn, waits=(), inc=None, inc_n=1):
        w = [(s, v) for (s, v) in waits if v and v > 0]
        self.ops[eng].append((w, fn, inc, inc_n))
        if inc is not None:
            self.cnt[inc] += inc_n
            return self.cnt[inc]
        return None


def emit(block, prog, sems):
    def make_body(ops):
        def body(eng):
            for waits, fn, inc, inc_n in ops:
                for s, v in waits:
                    eng.wait_ge(sems[s], v)
                inst = fn(eng)
                if inc is not None:
                    inst.then_inc(sems[inc], inc_n)
        return body
    for e in ENGS:
        getattr(block, e)(make_body(prog.ops[e]))


def build(debug=False):
    nc = bass.Bass("TRN2", target_bir_lowering=False)

    X = nc.dram_tensor("x", [T_FULL, D], F32, kind="ExternalInput").ap()
    WQKV = nc.dram_tensor("wqkv", [3 * D, D], F32, kind="ExternalInput").ap()
    WOUT = nc.dram_tensor("wout", [D, D], F32, kind="ExternalInput").ap()
    WIN = nc.dram_tensor("win", [D, FF], F32, kind="ExternalInput").ap()
    BIN = nc.dram_tensor("bin", [1, FF], F32, kind="ExternalInput").ap()
    WO2 = nc.dram_tensor("wo2", [FF, D], F32, kind="ExternalInput").ap()
    BOUT = nc.dram_tensor("bout", [1, D], F32, kind="ExternalInput").ap()
    CVEC = nc.dram_tensor("cvec", [1, D], F32, kind="ExternalInput").ap()
    WMOD = nc.dram_tensor("wmod", [D, 6 * D], F32, kind="ExternalInput").ap()
    OUT = nc.dram_tensor("out", [T_OWN, D], F32, kind="ExternalOutput").ap()

    kd = "ExternalOutput" if debug else "Internal"
    kv_k = nc.dram_tensor("kv_k", [8, 128, T_FULL], BF16, kind=kd).ap()
    q_d = nc.dram_tensor("q_d", [8, 128, T_OWN], BF16, kind=kd).ap()
    kv_v = nc.dram_tensor("kv_v", [32, 128, 8 * 130], BF16, kind=kd).ap()
    at_d = nc.dram_tensor("at_d", [8, 128, T_OWN], BF16, kind=kd).ap()
    x2_d = nc.dram_tensor("x2_d", [16, 128, D], F32, kind=kd).ap()
    h2_d = nc.dram_tensor("h2_d", [16, 128, D], BF16, kind=kd).ap()
    hid_d = nc.dram_tensor("hid_d", [32, 128, T_OWN], BF16, kind=kd).ap()
    w2_d = nc.dram_tensor("w2_d", [32, 128, D], BF16, kind=kd).ap()
    xq2_d = nc.dram_tensor("xq2_d", [16, 128, FF], BF16, kind=kd).ap()
    cc1i = nc.dram_tensor("cc1i", [1, 1], F32)
    cc1o = nc.dram_tensor("cc1o", [1, 1], F32)
    cc2i = nc.dram_tensor("cc2i", [1, 1], F32)
    cc2o = nc.dram_tensor("cc2o", [1, 1], F32)

    with contextlib.ExitStack() as ctx:
        sb = lambda n, s, d: ctx.enter_context(nc.sbuf_tensor(n, s, d))

        ident = sb("ident", [128, 128], F32)
        ones_mat = sb("ones_mat", [128, 128], BF16)
        ones1f = sb("ones1f", [1, 128], F32)
        e65 = sb("e65", [65, 128], F32)
        negones = sb("negones", [1, 128], BF16)
        bc_sa = sb("bc_sa", [128, D], F32)
        bc_ca = sb("bc_ca", [128, D], F32)
        bc_ga = sb("bc_ga", [128, D], F32)
        bc_sbv = sb("bc_sbv", [128, D], F32)
        bc_cb = sb("bc_cb", [128, D], F32)
        bc_gb = sb("bc_gb", [128, D], F32)
        bc_gbo = sb("bc_gbo", [128, D], F32)
        xst = [sb(f"xst{i}", [128, D], F32) for i in range(2)]
        scrap = [sb(f"scrap{i}", [128, D], F32) for i in range(2)]
        h1t32 = [sb(f"h1t32{i}", [128, D], F32) for i in range(2)]
        ln_a = sb("ln_a", [128, 1], F32)
        ln_b = sb("ln_b", [128, 1], F32)
        ln_c = sb("ln_c", [128, 1], F32)
        ln_d = sb("ln_d", [128, 1], F32)
        nmq = sb("nmq", [128, 1], F32)
        nmk = sb("nmk", [128, 1], F32)
        nmt = sb("nmt", [128, 1], F32)
        negmA = [sb(f"negmA{i}", [128, 1], F32) for i in range(2)]
        negmB = [sb(f"negmB{i}", [128, 1], F32) for i in range(2)]
        amx = sb("amx", [128, 1], F32)
        amrow = sb("amrow", [1, 128], F32)
        sc_s = sb("sc_s", [1, 8], F32)
        s1t = sb("s1t", [128, 1], F32)
        rs1t = sb("rs1t", [128, 1], F32)
        s2t = sb("s2t", [128, 1], F32)
        rs2t = sb("rs2t", [128, 1], F32)
        am1 = sb("am1", [128, 16], F32)
        am2a = sb("am2a", [128, 64], F32)
        red1 = sb("red1", [128, 1], F32)
        eps_t = sb("eps_t", [128, 1], F32)
        nb26 = sb("nb26", [128, 1], F32)
        magict = sb("magict", [128, 1], F32)
        nmagict = sb("nmagict", [128, 1], F32)
        rsf = sb("rsf", [128, 32], F32)
        amaxr = sb("amaxr", [128, 32], F32)
        srr = sb("srr", [128, 32], F32)
        rsr = sb("rsr", [128, 32], F32)
        rs2rsr = sb("rs2rsr", [128, 32], F32)
        b_inT = sb("b_inT", [128, 32], F32)

        top = nc.sbuf_top
        Z1 = top - 58 * KB
        Z2 = Z1 - 56 * KB
        Z3 = Z2 - 36 * KB

        def at(name, shape, dtype, off):
            return nc.alloc_sbuf_tensor_at(name, shape, dtype, offset=off)

        wmod_st = [at(f"wmod_st{i}", [128, 1536], F32, Z1 + 6 * KB * i) for i in range(4)]
        wm16 = [at(f"wm16_{i}", [128, 1536], BF16, Z1 + 24 * KB + 3 * KB * i)
                for i in range(2)]
        kbufA = [at(f"kbufA{i}", [65, T_FULL], BF16, Z1 + 8 * KB * i) for i in range(2)]
        kbufB = [at(f"kbufB{i}", [65, T_FULL], BF16, Z1 + 16 * KB + 8 * KB * i) for i in range(2)]
        vbuf = [at(f"vbuf{i}", [128, 32 * 130], BF16, Z1 + 32 * KB + 8320 * i) for i in range(2)]
        probs = [at(f"probs{i}", [128, KB], BF16, Z1 + 49408 + 2 * KB * i) for i in range(4)]
        probsP = [at(f"probsP{i}", [128, 2 * KB], BF16, Z1 + 49408 + 4 * KB * i)
                  for i in range(2)]
        sksq = [at(f"sksq{i}", [65, 512], BF16, Z1 + 55552 + KB * i) for i in range(2)]
        hidld = [at(f"hidld{i}", [128, FF], BF16, Z1 + 8 * KB * i) for i in range(2)]
        hidy = at("hidy", [128, FF], F32, Z1 + 16 * KB)
        atres = at("atres", [128, 16 * KB], BF16, Z1)
        xq2t = [at(f"xq2t{i}", [128, FF], BF16, Z1 + 32 * KB + 8 * KB * i) for i in range(2)]
        outst = [at(f"outst{i}", [128, 512], F32, Z1 + 48 * KB + 2 * KB * i) for i in range(2)]

        wqkvT_all = at("wqkvT_all", [128, 24 * KB], BF16, Z2)
        wqkvT = [ColView(wqkvT_all, 3 * D * e, 3 * D) for e in range(8)]
        wq_st = [at(f"wq_st{i}", [128, D], F32, Z2 + 48 * KB + 4 * KB * i) for i in range(2)]
        m1q = at("m1q", [128, KB], F32, Z2)
        w1st = [at(f"w1st{i}", [128, KB], F32, Z2 + 4 * KB + 4 * KB * i) for i in range(2)]
        wo2_st = [at(f"wo2_st{i}", [128, D], F32, Z2 + 12 * KB + 4 * KB * i) for i in range(2)]
        amaxf_row = at("amaxf_row", [1, FF], F32, Z2 + 20 * KB)
        woutT = at("woutT", [128, 8 * D], BF16, Z2 + 36 * KB)
        xq1f_all = at("xq1f_all", [128, 16 * KB], BF16, Z2)
        xq1f = [ColView(xq1f_all, T_OWN * e, T_OWN) for e in range(8)]
        xq1st = [at(f"xq1st{i}", [128, D], F32, Z2 + 32 * KB + 4 * KB * i) for i in range(2)]
        win_st = [at(f"win_st{i}", [128, KB], F32, Z2 + 40 * KB + 4 * KB * i) for i in range(3)]
        wqf32 = [at(f"wqf32{i}", [128, KB], F32, Z2 + 32 * KB + 4 * KB * i) for i in range(2)]
        sq16 = at("sq16", [128, KB], BF16, Z2 + 52 * KB)

        silu_c32 = at("silu_c32", [1, D], F32, Z3)
        gate_tmp = at("gate_tmp", [1, D], F32, Z3 + 4 * KB)
        silu_cT = at("silu_cT", [128, 8], F32, Z3 + 8 * KB)
        silu_cT16 = at("silu_cT16", [128, 8], BF16, Z3 + 8 * KB + 64)
        mod_raw = at("mod_raw", [1, 6 * D], F32, Z3 + 12 * KB)
        h1seg_all = at("h1seg_all", [128, 8 * KB], BF16, Z3)
        h1seg = [ColView(h1seg_all, KB * e, KB) for e in range(8)]
        kstage = [at(f"kstage{i}", [128, 512], BF16, Z3 + 16 * KB + KB * i) for i in range(2)]
        qstage = [at(f"qstage{i}", [128, 512], BF16, Z3 + 18 * KB + KB * i) for i in range(2)]
        vstage = [at(f"vstage{i}", [128, 8 * 130], BF16, Z3 + 20 * KB + 2080 * i) for i in range(2)]
        qaugA = [at(f"qaugA{i}", [65, T_OWN], BF16, Z3 + 4 * KB * i) for i in range(2)]
        qaugB = [at(f"qaugB{i}", [65, T_OWN], BF16, Z3 + 8 * KB + 4 * KB * i) for i in range(2)]
        den_sb = at("den_sb", [65, 2 * KB], F32, Z3 + 16 * KB)
        attAst = at("attAst", [64, KB], BF16, Z2 + 54 * KB)
        rbc = at("rbc", [128, KB], F32, Z3 + 24 * KB)
        atstage = [at(f"atstage{i}", [128, T_OWN], BF16, Z3 + 28 * KB + 4 * KB * i) for i in range(2)]
        h2st = [at(f"h2st{i}", [128, D], BF16, Z3 + 28 * KB + 2 * KB * i) for i in range(2)]
        wint1 = [at(f"wint1{i}", [128, KB], BF16, Z3 + 2 * KB * i) for i in range(2)]
        hidst = [at(f"hidst{i}", [128, T_OWN], BF16, Z3 + 4 * KB + 4 * KB * i) for i in range(2)]
        sf_bc = at("sf_bc", [128, FF], F32, Z3 + 12 * KB)
        w2eh = at("w2eh", [128, 32 * 512], BF16, Z3 + 0)

        assert nc.sbuf_bytes_remaining >= (top - Z3), f"sbuf overflow: {nc.sbuf_bytes_remaining} < {top - Z3}"

        psAB = ctx.enter_context(nc.psum_tensor("psAB", [128, 2 * KB], F32))
        psA = ColView(psAB, 0, KB)
        psB = ColView(psAB, KB, KB)
        psC = ctx.enter_context(nc.psum_tensor("psC", [128, KB], F32))
        psD = ctx.enter_context(nc.psum_tensor("psD", [128, KB], F32))

        semnames = (["T", "V", "S", "G", "DM", "DZ", "DC"] +
                    [f"D{c}{i}" for c in "WNXKQVOASYHIUGE" for i in range(2)])
        sems = {n: ctx.enter_context(nc.semaphore(n)) for n in semnames}
        block = ctx.enter_context(nc.Block())

        p = Prog()
        C = p.cnt

        def dma(qeng, qsem, out, in_, waits=(), **kw):
            n = p.op(qeng, lambda e, o=out, i=in_, k=kw: e.dma_start(out=o, in_=i, **k),
                     waits=waits, inc=qsem, inc_n=16)
            return (qsem, n)

        def tot(*names):
            return [(s, C[s]) for s in names if C[s] > 0]

        def ve(fn, waits=()):
            return p.op("vector", fn, waits=waits, inc="V")

        def sc(fn, waits=()):
            return p.op("scalar", fn, waits=waits, inc="S")

        def te(fn, waits=(), inc="T"):
            return p.op("tensor", fn, waits=waits, inc=inc)

        g_ms = p.op("gpsimd", lambda e: e.memset(ident[:], 0.0), inc="G")
        g_id = p.op("gpsimd", lambda e: e.affine_select(
            out=ident[:], in_=ident[:], compare_op=ALU.not_equal, fill=1.0,
            base=0, pattern=[[-1, 128]], channel_multiplier=1),
            waits=[("G", g_ms)], inc="G")
        p.op("vector", lambda e: e.memset(ones_mat[:], 1.0))
        p.op("vector", lambda e: e.memset(negones[:], -1.0))
        p.op("vector", lambda e: e.memset(ones1f[:], 1.0))
        v_e65 = ve(lambda e: e.memset(e65[:], 0.0))
        p.op("vector", lambda e: e.memset(e65[0:1, :], 1.0),
             waits=[("V", v_e65)])
        p.op("vector", lambda e: e.memset(eps_t[:], EPS))
        p.op("vector", lambda e: e.memset(magict[:], MAGIC))
        p.op("vector", lambda e: e.memset(nmagict[:], -MAGIC))
        p.op("vector", lambda e: e.memset(nb26[:], -26.0))
        v_consts = ve(lambda e: e.memset(amx[:], 0.0))

        wq_fin = [0] * 24
        for fc in range(24):
            wv = dma("sync", f"DW{fc % 2}", wq_st[fc % 2][:], WQKV[128 * fc:128 * fc + 128, :],
                     waits=[("V", wq_fin[fc - 2])] if fc >= 2 else [])
            ps_ = psC if (fc % 2 == 0) else psD
            tt = 0
            for ec in range(8):
                tt = te(lambda e, f=fc, k=ec, P=ps_: e.transpose(
                    P[:, 128 * k:128 * k + 128],
                    wq_st[f % 2][:, 128 * k:128 * k + 128], ident[:]),
                    waits=([wv, ("G", g_id)] +
                           ([("V", wq_fin[fc - 2])] if fc >= 2 else []))
                    if ec == 0 else [],
                    inc="T" if ec == 7 else None)
            wq_fin[fc] = ve(lambda e, f=fc, P=ps_: e.tensor_copy(
                wqkvT_all[:].rearrange("p (g c) -> p g c", c=3 * D)
                [:, :, 128 * f:128 * f + 128],
                P[:].rearrange("p (g c) -> p g c", c=128)),
                waits=[("T", tt)])
        v_wqkvT = C["V"]
        t_s1a = C["T"]

        w_c = dma("sync", "DM", silu_c32[:], CVEC)
        s_sil = sc(lambda e: e.activation(silu_c32[:], silu_c32[:], ACT.Silu),
                   waits=[w_c])
        last_sct = 0
        for k in range(8):
            tt = te(lambda e, kk=k: e.transpose(
                psA[:, 0:1], silu_c32[0:1, 128 * kk:128 * kk + 128], ident[0:1, 0:1]),
                waits=[("S", s_sil), ("G", g_id)] +
                      ([("V", last_sct)] if k > 0 else []))
            last_sct = ve(lambda e, kk=k: e.tensor_copy(silu_cT[:, kk:kk + 1], psA[:, 0:1]),
                          waits=[("T", tt)])
        v_sct = ve(lambda e: e.tensor_copy(silu_cT16[:], silu_cT[:]),
                   waits=[("V", last_sct)])

        s_mod = 0
        t_modmm = 0
        wm_w = {}
        vc_d = {}
        tmm = {}

        def wm_load(ch):
            q2, rc = divmod(ch, 8)
            qeng, qsem = [("scalar", "DN0"), ("gpsimd", "DG0"),
                          ("scalar", "DN1"), ("gpsimd", "DG1")][ch % 4]
            wm_w[ch] = dma(qeng, qsem, wmod_st[ch % 4][:],
                           WMOD[128 * rc:128 * rc + 128,
                                1536 * q2:1536 * q2 + 1536],
                           waits=[("V", vc_d[ch - 4])] if ch >= 4 else [])

        for ch0 in range(4):
            wm_load(ch0)
        for q2 in range(4):
            for rc in range(8):
                ch = 8 * q2 + rc
                vc = ve(lambda e, c2=ch: e.tensor_copy(wm16[c2 % 2][:],
                                                       wmod_st[c2 % 4][:]),
                        waits=[wm_w[ch], ("V", v_sct)] +
                              ([("T", tmm[ch - 2])] if ch >= 2 else []))
                vc_d[ch] = vc
                if ch + 4 < 32:
                    wm_load(ch + 4)
                for sg in range(3):
                    pst = [psB, psB, psC][sg]
                    off = [0, 512, 0][sg]
                    t_modmm = te(lambda e, c2=ch, P=pst, o=off, s=sg: e.matmul(
                        P[0:1, o:o + 512], silu_cT16[:, c2 % 8:c2 % 8 + 1],
                        wm16[c2 % 2][:, 512 * s:512 * s + 512],
                        start=(c2 % 8 == 0), stop=(c2 % 8 == 7)),
                        waits=([("V", vc)] if sg == 0 else []) +
                              ([("T", t_s1a), ("S", C["S"])]
                               if (ch == 0 and sg == 0) else []) +
                              ([("S", s_mod)]
                               if (rc == 0 and q2 > 0 and sg == 0) else []),
                        inc="T" if sg == 2 else None)
                tmm[ch] = t_modmm
            for sg in range(3):
                pst = [psB, psB, psC][sg]
                off = [0, 512, 0][sg]
                s_mod = sc(lambda e, s=sg, P=pst, o=off, q3=q2: e.copy(
                    mod_raw[0:1, 1536 * q3 + 512 * s:1536 * q3 + 512 * s + 512],
                    P[0:1, o:o + 512]),
                    waits=[("T", t_modmm)])
        t_mod_ps = C["T"]

        def gate_fix(off):
            a = sc(lambda e, o=off: e.activation(
                gate_tmp[0:1, :], mod_raw[0:1, o:o + D], ACT.Sign),
                waits=[("S", s_mod), ("V", C["V"])])
            c = sc(lambda e, o=off: e.activation(
                mod_raw[0:1, o:o + D], mod_raw[0:1, o:o + D], ACT.Abs),
                waits=[("S", a)])
            d = ve(lambda e, o=off: e.tensor_scalar_max(
                mod_raw[0:1, o:o + D], mod_raw[0:1, o:o + D], 0.25),
                waits=[("S", c)])
            f = ve(lambda e, o=off: e.tensor_tensor(
                mod_raw[0:1, o:o + D], mod_raw[0:1, o:o + D],
                gate_tmp[0:1, :], ALU.mult),
                waits=[("V", d)])
            return f
        gate_fix(2 * D)
        gate_fix(5 * D)
        ve(lambda e: e.tensor_scalar_add(mod_raw[0:1, D:2 * D],
                                         mod_raw[0:1, D:2 * D], 1.0),
           waits=[("S", s_mod)])
        v_modfix = ve(lambda e: e.tensor_scalar_add(mod_raw[0:1, 4 * D:5 * D],
                                                    mod_raw[0:1, 4 * D:5 * D], 1.0),
                      waits=[("S", s_mod)])

        def bcast_row(dst, row_ap, waits0):
            te(lambda e: e.matmul(psA[:, 0:512], ones1f[:], row_ap[0:1, 0:512],
                                  start=True, stop=True),
               waits=list(waits0), inc=None)
            t2 = te(lambda e: e.matmul(psA[:, 512:1024], ones1f[:], row_ap[0:1, 512:1024],
                                       start=True, stop=True))
            return ve(lambda e, d=dst: e.tensor_copy(d[:], psA[:]), waits=[("T", t2)])

        v_bc = bcast_row(bc_sa, mod_raw[0:1, 0:D],
                         [("V", v_modfix), ("S", C["S"]), ("V", C["V"]), ("T", t_mod_ps)])
        v_bc = bcast_row(bc_ca, mod_raw[0:1, D:2 * D], [("V", v_bc)])
        v_bc_ln1 = v_bc
        v_bc = bcast_row(bc_ga, mod_raw[0:1, 2 * D:3 * D], [("V", v_bc)])
        v_bc = bcast_row(bc_sbv, mod_raw[0:1, 3 * D:4 * D], [("V", v_bc)])
        v_bc = bcast_row(bc_cb, mod_raw[0:1, 4 * D:5 * D], [("V", v_bc)])
        v_bc = bcast_row(bc_gb, mod_raw[0:1, 5 * D:6 * D], [("V", v_bc)])
        w_bo = dma("sync", "DM", gate_tmp[0:1, :], BOUT, waits=[("V", C["V"])])
        v_bo = bcast_row(bc_gbo, gate_tmp[0:1, :], [w_bo, ("V", v_bc)])
        v_gbo = ve(lambda e: e.tensor_tensor(bc_gbo[:], bc_gbo[:], bc_gb[:], ALU.mult),
                   waits=[("V", v_bo)])
        v_bc_all = v_gbo

        v_vst = 0
        for sl in range(2):
            v_vst = ve(lambda e, s=sl: e.memset(
                vstage[s][:].rearrange("r (p c) -> r p c", c=130)[:, :, 64:65], 1.0),
                waits=[("V", v_bc_all)] if sl == 0 else [])
            v_vst = ve(lambda e, s=sl: e.memset(
                vstage[s][:].rearrange("r (p c) -> r p c", c=130)[:, :, 129:130], 1.0))
        x_w = [0] * 32
        v_xfree = [0] * 32
        v_h1seg = [0] * 8
        ev_h1 = [0] * 32
        s_kq = 0
        qs_k_w = [0] * 100
        qc_v_w = []
        qg_q_w = []
        nqs_k = 0
        for i in range(32):
            s = i // 4
            x_w[i] = dma("sync" if i % 2 == 0 else "gpsimd",
                         "DX0" if i % 2 == 0 else "DE0",
                         xst[i % 2][:], X[128 * i:128 * i + 128, :],
                         waits=[("V", v_xfree[i - 2])] if i >= 2 else [])
            r1 = ve(lambda e, ii=i: e.tensor_reduce(
                ln_a[:], xst[ii % 2][:], AX.X, ALU.add, negate=True),
                waits=[x_w[i], ("V", C["V"])])
            s1 = sc(lambda e, ii=i: e.activation(
                scrap[ii % 2][:], xst[ii % 2][:], ACT.Square, accum_out=ln_b[:]),
                waits=[x_w[i], ("S", C["S"]), ("V", C["V"] - 1)])
            r2 = ve(lambda e: e.tensor_scalar(ln_a[:], ln_a[:], 1.0 / D, None, ALU.mult),
                    waits=[("V", r1)])
            r3 = ve(lambda e: e.tensor_tensor(ln_d[:], ln_a[:], ln_a[:], ALU.mult),
                    waits=[("V", r2)])
            r4 = ve(lambda e: e.tensor_scalar(ln_c[:], ln_b[:], 1.0 / D, None, ALU.mult),
                    waits=[("S", s1), ("V", r3)])
            r5 = ve(lambda e: e.tensor_tensor(ln_c[:], ln_c[:], ln_d[:], ALU.subtract),
                    waits=[("V", r4)])
            s2_ = sc(lambda e: e.activation(ln_c[:], ln_c[:], ACT.Sqrt, bias=eps_t[:]),
                     waits=[("V", r5), ("S", C["S"])])
            r6 = ve(lambda e: e.reciprocal(ln_c[:], ln_c[:]), waits=[("S", s2_)])
            z1 = ve(lambda e, ii=i: e.tensor_scalar(
                scrap[ii % 2][:], xst[ii % 2][:], ln_a[:], ln_c[:], ALU.add, ALU.mult),
                waits=[("V", r6)])
            v_xfree[i] = z1
            z2 = ve(lambda e, ii=i: e.tensor_tensor(
                h1t32[ii % 2][:], scrap[ii % 2][:], bc_ca[:], ALU.mult),
                waits=[("V", z1), ("V", v_bc_ln1)])
            z3 = ve(lambda e, ii=i: e.tensor_tensor(
                h1t32[ii % 2][:], h1t32[ii % 2][:], bc_sa[:], ALU.add),
                waits=[("V", z2)])
            ps_ = psC if (i % 2 == 0) else psD
            tt = 0
            for ec in range(8):
                tt = te(lambda e, ii=i, k=ec, P=ps_: e.transpose(
                    P[:, 128 * k:128 * k + 128],
                    h1t32[ii % 2][:, 128 * k:128 * k + 128], ident[:]),
                    waits=([("V", z3)] +
                           ([("T", t_mod_ps), ("V", v_bc_all)] if i == 0 else []) +
                           ([("V", ev_h1[i - 2])] if i >= 2 else []) +
                           ([("V", C["V"])] if (i % 4 == 0 and i > 0) else []))
                    if ec == 0 else [],
                    inc="T" if ec == 7 else None)
            ev_h1[i] = ve(lambda e, ii=i, P=ps_: e.tensor_copy(
                h1seg_all[:].rearrange("p (g c) -> p g c", c=KB)
                [:, :, 512 * ((ii // 4) % 2) + 128 * (ii % 4):
                       512 * ((ii // 4) % 2) + 128 * (ii % 4) + 128],
                P[:].rearrange("p (g c) -> p g c", c=128)),
                waits=[("T", tt)])
            if i % 4 != 3:
                continue
            v_h1seg[s] = C["V"]
            soff = 512 * (s % 2)
            for pr in range(8):
                ps_ = psA if (pr % 2 == 0) else psB
                tl = 0
                for ec in range(8):
                    tl = te(lambda e, P2=pr, k=ec, P=ps_, so=soff: e.matmul(
                        P[:, 0:512], wqkvT[k][:, D + 128 * P2:D + 128 * P2 + 128],
                        h1seg[k][:, so:so + 512],
                        start=(k == 0), stop=(k == 7)),
                        waits=([("V", v_h1seg[s]), ("V", v_wqkvT), ("S", s_kq)]
                               if ec == 0 else []),
                        inc="T" if ec == 7 else None)
                s_kq = sc(lambda e, P2=pr, P=ps_: e.copy(kstage[P2 % 2][:], P[:, 0:512]),
                          waits=[("T", tl)] +
                                ([qs_k_w[nqs_k - 2]] if nqs_k >= 2 else []))
                qs_k_w[nqs_k] = dma("sync", f"DK{pr % 2}", kv_k[pr, :, 512 * s:512 * s + 512],
                                    kstage[pr % 2][:], waits=[("S", s_kq)])
                nqs_k += 1
            if s < 4:
                for pr in range(8):
                    ps_ = psA if (pr % 2 == 0) else psB
                    tl = 0
                    for ec in range(8):
                        tl = te(lambda e, P2=pr, k=ec, P=ps_, so=soff: e.matmul(
                            P[:, 512:1024], wqkvT[k][:, 128 * P2:128 * P2 + 128],
                            h1seg[k][:, so:so + 512],
                            start=(k == 0), stop=(k == 7)),
                            waits=[("S", s_kq)] if ec == 0 else [],
                            inc="T" if ec == 7 else None)
                    s_kq = sc(lambda e, P2=pr, P=ps_: e.copy(qstage[P2 % 2][:],
                                                             P[:, 512:1024]),
                              waits=[("T", tl)] +
                                    ([qg_q_w[-2]] if len(qg_q_w) >= 2 else []))
                    qg_q_w.append(dma("sync", f"DQ{pr % 2}",
                                      q_d[pr, :, 512 * s:512 * s + 512],
                                      qstage[pr % 2][:], waits=[("S", s_kq)]))
            for it in range(4):
                ii = 4 * s + it
                tl = 0
                for ec in range(8):
                    for sg in range(2):
                        tl = te(lambda e, k=ec, g=sg, ad=soff + 128 * it: e.matmul(
                            psC[:, 512 * g:512 * g + 512],
                            h1seg[k][:, ad:ad + 128],
                            wqkvT[k][:, 2 * D + 512 * g:2 * D + 512 * g + 512],
                            start=(k == 0), stop=(k == 7)),
                            waits=([("V", C["V"])] if (ec == 0 and sg == 0) else []),
                            inc="T" if (ec == 7 and sg == 1) else None)
                vv1 = ve(lambda e, ii2=ii: e.tensor_copy(
                    vstage[ii2 % 2][:].rearrange("r (p c) -> r p c", c=130)[:, :, 0:64],
                    psC[:].rearrange("r (p c) -> r p c", c=128)[:, :, 0:64]),
                    waits=[("T", tl), ("V", v_vst)] +
                          ([qc_v_w[-2]] if len(qc_v_w) >= 2 else []))
                vv2 = ve(lambda e, ii2=ii: e.tensor_copy(
                    vstage[ii2 % 2][:].rearrange("r (p c) -> r p c", c=130)[:, :, 65:129],
                    psC[:].rearrange("r (p c) -> r p c", c=128)[:, :, 64:128]),
                    waits=[("V", vv1)])
                qc_v_w.append(dma("scalar", f"DV{ii % 2}", kv_v[ii, :, :],
                                  vstage[ii % 2][:], waits=[("V", vv2)]))
        v_s2 = C["V"]
        s_s2 = C["S"]
        t_s2 = C["T"]
        pass
        pass
        pass

        wo_fin = [0] * 8
        for ec in range(8):
            wv = dma("scalar", f"DO{ec % 2}", wo2_st[ec % 2][:], WOUT[128 * ec:128 * ec + 128, :],
                     waits=([("V", wo_fin[ec - 2])] if ec >= 2 else
                            [("V", v_s2)]))
            last = 0
            for fc in range(8):
                ps_ = psC if (fc % 2 == 0) else psD
                tt = te(lambda e, k=ec, f=fc, P=ps_: e.transpose(
                    P[:, 0:128], wo2_st[k % 2][:, 128 * f:128 * f + 128], ident[:]),
                    waits=[wv, ("T", t_s2)] +
                          ([("V", C["V"] - 1)] if (ec > 0 or fc >= 2) else []))
                last = ve(lambda e, k=ec, f=fc, P=ps_: e.tensor_copy(
                    woutT[:, D * f + 128 * k:D * f + 128 * k + 128], P[:, 0:128]),
                    waits=[("T", tt)])
            wo_fin[ec] = last
        v_woutT = C["V"]

        w_bi = dma("scalar", "DM", amaxf_row[:], BIN, waits=[("V", v_s2)])
        last_bi = 0
        for k in range(32):
            tt = te(lambda e, kk=k: e.transpose(
                psC[:, 0:1], amaxf_row[0:1, 128 * kk:128 * kk + 128], ident[0:1, 0:1]),
                waits=[w_bi, ("T", C["T"])] +
                      ([("V", last_bi)] if k > 0 else [("V", v_woutT)]))
            last_bi = ve(lambda e, kk=k: e.tensor_copy(b_inT[:, kk:kk + 1], psC[:, 0:1]),
                         waits=[("T", tt)])
        v_bint = last_bi

        kB_w = [0] * 8
        vv_w = [0] * 8
        qB_w = [0] * 8
        at_sp = [0] * 8
        v_kone = ve(lambda e: e.memset(kbufA[0][64:65, :], -1.0),
                    waits=[("T", C["T"]), ("V", C["V"]), ("S", C["S"])] +
                          tot("DK0", "DK1", "DQ0", "DQ1", "DV0", "DV1"))
        v_kone = ve(lambda e: e.memset(kbufA[1][64:65, :], -1.0))
        v_kone = ve(lambda e: e.memset(kbufB[0][64:65, :], -1.0))
        v_kone = ve(lambda e: e.memset(kbufB[1][64:65, :], -1.0))
        v_kone = ve(lambda e: e.memset(den_sb[:], 0.0))
        dma("sync", "DK0", kbufA[0][0:64, :], kv_k[0, 0:64, :],
            waits=tot("DK0", "DK1"))
        kB_w[0] = dma("sync", "DK0", kbufB[0][0:64, :], kv_k[0, 64:128, :])
        vv_w[0] = dma("sync", "DV0", vbuf[0][:].rearrange("r (i c) -> r i c", c=130),
                      kv_v[:, :, 0:130].rearrange("i r c -> r i c"),
                      waits=tot("DV0", "DV1"))
        dma("sync", "DQ0", qaugA[0][0:64, :], q_d[0, 0:64, :],
            waits=tot("DQ0", "DQ1"))
        qB_w[0] = dma("sync", "DQ0", qaugB[0][0:64, :], q_d[0, 64:128, :])
        v_attev = v_bint
        pvA_d = 0
        pvB_d = 0
        exA_last = 0
        exB_last = 0
        pv_ring = [0, 0, 0]
        for pr in range(8):
            slot = pr % 2
            if pr + 1 < 8:
                sl2 = (pr + 1) % 2
                dma("sync", f"DK{sl2}", kbufA[sl2][0:64, :], kv_k[pr + 1, 0:64, :],
                    waits=[("V", C["V"]), ("S", C["S"])])
                kB_w[pr + 1] = dma("sync", f"DK{sl2}", kbufB[sl2][0:64, :],
                                   kv_k[pr + 1, 64:128, :])
                vv_w[pr + 1] = dma("sync", f"DV{sl2}",
                                   vbuf[sl2][:].rearrange("r (i c) -> r i c", c=130),
                                   kv_v[:, :, 130 * (pr + 1):130 * (pr + 1) + 130]
                                   .rearrange("i r c -> r i c"),
                                   waits=[("T", C["T"])])
                dma("sync", f"DQ{sl2}", qaugA[sl2][0:64, :], q_d[pr + 1, 0:64, :],
                    waits=[("V", C["V"]), ("S", C["S"])])
                qB_w[pr + 1] = dma("sync", f"DQ{sl2}", qaugB[sl2][0:64, :],
                                   q_d[pr + 1, 64:128, :])
            w_free = [] if pr > 0 else [("V", v_kone)]
            s_nm = 0
            for head in range(2):
                kb = kbufA if head == 0 else kbufB
                qa = qaugA if head == 0 else qaugB
                mmn = [0] * 8
                lastv = 0
                for sg in range(8):
                    ps_ = psA if sg % 2 == 0 else psB
                    tt_in = ve(lambda e, g=sg, s2l=slot, K2=kb: e.tensor_tensor(
                        sksq[g % 2][:], K2[s2l][0:65, 512 * g:512 * g + 512],
                        K2[s2l][0:65, 512 * g:512 * g + 512], ALU.mult),
                        waits=([kB_w[pr], ("V", C["V"])] + w_free
                               if (head == 0 and sg == 0) else []) +
                              ([("T", mmn[sg - 2])] if sg >= 2 else []) +
                              ([("T", qmm1), ("S", s_nm)]
                               if (head == 1 and sg <= 1) else []))
                    mmn[sg] = te(lambda e, g=sg, P=ps_: e.matmul(
                        P[:, 0:512], ones_mat[0:65, :], sksq[g % 2][0:65, :],
                        start=True, stop=True),
                        waits=[("V", tt_in)] +
                              ([("V", v_attev)] if (head == 0 and sg <= 1) else []))
                    if sg == 0:
                        lastv = ve(lambda e, P=ps_: e.tensor_reduce(
                            nmk[:], P[:, 0:512], AX.X, ALU.max),
                            waits=[("T", mmn[sg]), ("V", C["V"])])
                    else:
                        rv = ve(lambda e, P=ps_: e.tensor_reduce(
                            nmt[:], P[:, 0:512], AX.X, ALU.max),
                            waits=[("T", mmn[sg]), ("V", lastv)])
                        lastv = ve(lambda e: e.tensor_tensor(
                            nmk[:], nmk[:], nmt[:], ALU.max), waits=[("V", rv)])
                vsc = ve(lambda e: e.tensor_scalar(nmt[:], nmk[:], 1.0404, None,
                                                   ALU.mult), waits=[("V", lastv)])
                qmm = [0] * 4
                for sg in range(4):
                    ps_ = psA if sg < 2 else psB
                    tq_in = ve(lambda e, g=sg, s2l=slot, Q2=qa: e.tensor_tensor(
                        sksq[g % 2][0:64, :], Q2[s2l][0:64, 512 * g:512 * g + 512],
                        Q2[s2l][0:64, 512 * g:512 * g + 512], ALU.mult),
                        waits=([qB_w[pr], ("V", vsc), ("T", mmn[6]), ("T", mmn[7])]
                               if sg == 0 else []) +
                              ([("V", vsc)] if sg == 1 else []) +
                              ([("T", qmm[sg - 2])] if sg >= 2 else []))
                    qmm[sg] = te(lambda e, g=sg, P=ps_: e.matmul(
                        P[:, 512 * (g % 2):512 * (g % 2) + 512],
                        ones_mat[0:65, :], sksq[g % 2][0:65, :],
                        start=True, stop=True),
                        waits=[("V", tq_in)] + ([("S", s_nm)] if sg == 0 else []))
                s_nm = sc(lambda e, Q2=qa, s2l=slot: e.activation(
                    Q2[s2l][64:65, 0:2048], psAB[64:65, 0:2048],
                    ACT.Sqrt, scale=nmt[64:65, :]),
                    waits=[("T", qmm[3]), ("S", C["S"])])
                qmm1 = qmm[3]
            for hf in range(2):
                qoff = 1024 * hf
                d1 = 0
                for head in range(2):
                    kb = kbufA if head == 0 else kbufB
                    qa = qaugA if head == 0 else qaugB
                    pp = psC if head == 0 else psD
                    slots = [psA, psB, psD] if head == 0 else [psA, psB, psC]
                    ex = [0] * 32
                    pvj = [0] * 32

                    def emit_pv(jj, head=head, pp=pp, ex=ex, pvj=pvj):
                        t_ = 0
                        for sg in range(2):
                            t_ = te(lambda e, j2=jj, g=sg, s2l=slot, h=head,
                                    P=pp: e.matmul(
                                P[0:65, 512 * g:512 * g + 512],
                                vbuf[s2l][:, 130 * j2 + 65 * h:130 * j2 + 65 * h + 65],
                                probs[j2 % 3][:, 512 * g:512 * g + 512],
                                start=(j2 == 0), stop=(j2 == 31)),
                                waits=([("S", ex[jj])] if sg == 0 else []) +
                                      ([("V", v_attev)] +
                                       ([vv_w[pr]] if hf == 0 else [])
                                       if (jj == 0 and sg == 0) else []))
                        pvj[jj] = t_
                        pv_ring[jj % 3] = t_

                    for j in range(32):
                        ps_ = slots[j % 3]
                        if j >= 3:
                            w_sc = [("S", ex[j - 3])]
                        elif j >= 1:
                            w_sc = []
                        elif head == 1:
                            w_sc = [("S", exA_last), ("V", av1)]
                        elif hf == 0:
                            w_sc = [("S", s_nm), ("V", C["V"])]
                        else:
                            w_sc = [("S", exB_last), ("V", v_attev)]
                        tsa = 0
                        for sg in range(2):
                            tsa = te(lambda e, jj=j, g=sg, s2l=slot, qo=qoff,
                                     P=ps_, K2=kb, Q2=qa: e.matmul(
                                P[:, 512 * g:512 * g + 512],
                                K2[s2l][0:65, 128 * jj:128 * jj + 128],
                                Q2[s2l][0:65, qo + 512 * g:qo + 512 * g + 512],
                                start=True, stop=True),
                                waits=w_sc if sg == 0 else [],
                                inc="T" if sg == 1 else None)
                        if j >= 2:
                            emit_pv(j - 2)
                        ex[j] = sc(lambda e, jj=j, P=ps_: e.activation(
                            probs[jj % 3][:], P[:], ACT.Exp,
                            bias=0.0, scale=0.125),
                            waits=[("T", tsa)] +
                                  ([("T", pv_ring[j % 3])] if pv_ring[j % 3] else []))
                    emit_pv(30)
                    emit_pv(31)
                    if head == 0:
                        exA_last = ex[31]
                        pvA_d = pvj[31]
                        d1 = ve(lambda e: e.tensor_copy(den_sb[0:1, 0:1024],
                                                        psC[64:65, :]),
                                waits=[("T", pvj[31]), ("V", C["V"])])
                        av1 = ve(lambda e: e.tensor_copy(attAst[:], psC[0:64, :]),
                                 waits=[("V", d1)])
                    else:
                        exB_last = ex[31]
                        pvB_d = pvj[31]
                d2 = ve(lambda e: e.tensor_copy(den_sb[0:1, 1024:2048], psD[64:65, :]),
                        waits=[("T", pvB_d), ("V", d1)])
                tb1 = te(lambda e: e.matmul(psA[:, 0:512], e65[:],
                                            den_sb[0:65, 0:512], start=True, stop=True),
                         waits=[("V", d2), ("S", exB_last)], inc=None)
                tb1 = te(lambda e: e.matmul(psA[:, 512:1024], e65[:],
                                            den_sb[0:65, 512:1024], start=True, stop=True))
                tb2 = te(lambda e: e.matmul(psB[:, 0:512], e65[:],
                                            den_sb[0:65, 1024:1536], start=True, stop=True),
                         inc=None)
                tb2 = te(lambda e: e.matmul(psB[:, 512:1024], e65[:],
                                            den_sb[0:65, 1536:2048], start=True, stop=True))
                rb1 = ve(lambda e: e.tensor_copy(rbc[0:64, :], psA[0:64, :]),
                         waits=[("T", tb1)])
                rb2 = ve(lambda e: e.tensor_copy(rbc[64:128, :], psB[64:128, :]),
                         waits=[("T", tb2), ("V", rb1)])
                rr = ve(lambda e: e.reciprocal(rbc[:], rbc[:]), waits=[("V", rb2)])
                a1 = ve(lambda e, s2l=slot, o=qoff: e.tensor_tensor(
                    atstage[s2l][0:64, o:o + 1024], attAst[:], rbc[0:64, :],
                    ALU.mult),
                    waits=[("V", rr)] +
                          ([at_sp[pr - 2]] if (hf == 0 and pr >= 2) else []))
                v_attev = ve(lambda e, s2l=slot, o=qoff: e.scalar_tensor_tensor(
                    atstage[s2l][64:128, o:o + 1024], psD[0:64, :], 1.0, rbc[64:128, :],
                    ALU.bypass, ALU.mult),
                    waits=[("V", a1)])
            at_sp[pr] = dma("sync", f"DA{pr % 2}", at_d[pr], atstage[slot][:],
                            waits=[("V", v_attev)])
        v_attn_end = C["V"]
        t_attn_end = C["T"]
        s_attn_end = C["S"]
        pass

        v_m1 = ve(lambda e: e.memset(m1q[:], 0.0), waits=[("T", t_s2), ("V", C["V"])])
        for fq in range(4):
            for ec in range(8):
                wv = dma("scalar", f"DN{ec % 2}", w1st[ec % 2][:],
                         WIN[128 * ec:128 * ec + 128, 1024 * fq:1024 * fq + 1024],
                         waits=[("V", v_m1)])
                vsq = ve(lambda e, k=ec: e.tensor_tensor(
                    sq16[:], w1st[k % 2][:], w1st[k % 2][:], ALU.mult),
                    waits=[wv, ("V", C["V"])])
                v_m1 = ve(lambda e: e.tensor_tensor(
                    m1q[:], m1q[:], sq16[:], ALU.max),
                    waits=[("V", vsq)])
            for b in range(8):
                fc = 8 * fq + b
                tt = te(lambda e, bb=b: e.transpose(
                    psC[:, 0:128], m1q[:, 128 * bb:128 * bb + 128], ident[:]),
                    waits=[("V", v_m1), ("T", t_attn_end), ("V", C["V"])])
                rv = ve(lambda e: e.tensor_reduce(red1[:], psC[:, 0:128], AX.X, ALU.max),
                        waits=[("T", tt)])
                rvs = sc(lambda e: e.activation(red1[:], red1[:], ACT.Sqrt),
                         waits=[("V", rv), ("S", C["S"])])
                rv2 = ve(lambda e: e.tensor_scalar_max(red1[:], red1[:], 1e-8),
                         waits=[("S", rvs)])
                rv3 = ve(lambda e, f=fc: e.tensor_scalar(
                    rsf[:, f:f + 1], red1[:], 1.0 / 127.0, None, ALU.mult),
                    waits=[("V", rv2)])
                tt2 = te(lambda e: e.transpose(psC[0:1, 512:640], red1[:], ident[:]),
                         waits=[("V", rv3)])
                v_m1 = ve(lambda e, f=fc: e.tensor_copy(
                    amaxf_row[0:1, 128 * f:128 * f + 128], psC[0:1, 512:640]),
                    waits=[("T", tt2)])
            if fq < 3:
                v_m1 = ve(lambda e: e.memset(m1q[:], 0.0), waits=[("V", v_m1)])
        f1 = ve(lambda e: e.tensor_scalar_max(amaxf_row[:], amaxf_row[:], 1e-8),
                waits=[("V", v_m1)])
        f2 = ve(lambda e: e.reciprocal(amaxf_row[:], amaxf_row[:]), waits=[("V", f1)])
        v_sfrow = ve(lambda e: e.tensor_scalar(amaxf_row[:], amaxf_row[:], 127.0, None,
                                               ALU.mult), waits=[("V", f2)])

        v_wo = 0
        for fc in range(32):
            wv = dma("scalar", f"DO{fc % 2}", wo2_st[fc % 2][:], WO2[128 * fc:128 * fc + 128, :],
                     waits=[("V", v_wo)] if fc >= 2 else [("V", v_woutT)])
            v_wo = ve(lambda e, f=fc: e.tensor_reduce(
                amaxr[:, f:f + 1], wo2_st[f % 2][:], AX.X, ALU.max,
                apply_absolute_value=True),
                waits=[wv])
        c1 = ve(lambda e: e.tensor_scalar_max(amaxr[:], amaxr[:], 1e-8),
                waits=[("V", v_wo)])
        c2 = ve(lambda e: e.reciprocal(srr[:], amaxr[:]), waits=[("V", c1)])
        c3 = ve(lambda e: e.tensor_scalar(rsr[:], amaxr[:], 1.0 / 127.0, None, ALU.mult),
                waits=[("V", c2)])
        c4 = ve(lambda e: e.tensor_scalar(srr[:], srr[:], 127.0, None, ALU.mult),
                waits=[("V", c3)])
        v_w2q = c4
        qg_w2 = []
        for fc in range(32):
            wv = dma("scalar", f"DO{fc % 2}", wo2_st[fc % 2][:], WO2[128 * fc:128 * fc + 128, :],
                     waits=[("V", v_w2q)] if fc >= 2 else [("V", c4)])
            y1 = ve(lambda e, f=fc: e.tensor_scalar(
                w1st[f % 2][:], wo2_st[f % 2][:], srr[:, f:f + 1], MAGIC,
                ALU.mult, ALU.add),
                waits=[wv, ("V", C["V"])] +
                      ([qg_w2[-2]] if len(qg_w2) >= 2 else []))
            v_w2q = ve(lambda e, f=fc: e.tensor_scalar(
                hidst[f % 2][:, 0:1024], w1st[f % 2][:], MAGIC, rsr[:, f:f + 1],
                ALU.subtract, ALU.mult),
                waits=[("V", y1)] + (tot("DA0", "DA1") if fc < 2 else []))
            qg_w2.append(dma("scalar", f"DS{fc % 2}", w2_d[fc], hidst[fc % 2][:, 0:1024],
                             waits=[("V", v_w2q)]))
        pass

        at_w = [0] * 16
        x4_w = [0] * 16
        v_x2w = [0] * 16
        qs_x2 = []
        qc_h2 = []
        v_tc = 0
        for tc in range(16):
            pp = psA if tc % 2 == 0 else psB
            if tc == 0:
                w0_at = ([("T", t_attn_end), ("S", s_attn_end), ("V", v_attn_end)] +
                         tot("DA0", "DA1", "DE0"))
                for c4 in range(8):
                    dma("sync" if c4 % 2 == 0 else "gpsimd",
                        "DA0" if c4 % 2 == 0 else "DE0",
                        atres[:, T_OWN * c4:T_OWN * c4 + T_OWN], at_d[c4],
                        waits=w0_at if c4 <= 1 else [])
                w_atres = tot("DA0", "DE0")
            x4_w[tc] = dma("sync", f"DX{tc % 2}", xst[tc % 2][:], X[128 * tc:128 * tc + 128, :],
                           waits=[("V", v_x2w[tc - 2])] if tc >= 2 else [("V", v_s2)])
            tl = 0
            for fc in range(8):
                for sg in range(2):
                    tl = te(lambda e, f=fc, g=sg, t2=tc, P=pp: e.matmul(
                        P[:, 512 * g:512 * g + 512],
                        atres[:, T_OWN * f + 128 * t2:T_OWN * f + 128 * t2 + 128],
                        woutT[:, D * f + 512 * g:D * f + 512 * g + 512],
                        start=(f == 0), stop=(f == 7)),
                        waits=(([("V", v_woutT)] +
                                (w_atres + [("T", t_attn_end), ("S", s_attn_end),
                                            ("V", v_attn_end)] if tc == 0 else []) +
                               ([("V", v_x2w[tc - 2])] if tc >= 2 else [])))
                        if (fc == 0 and sg == 0) else [],
                        inc="T" if (fc == 7 and sg == 1) else None)
            g1 = ve(lambda e, t2=tc, P=pp: e.scalar_tensor_tensor(
                scrap[t2 % 2][:], P[:], 0.0, bc_ga[:], ALU.add, ALU.mult),
                waits=[("T", tl), ("V", C["V"]), ("V", v_bc_all)])
            g2 = ve(lambda e, t2=tc: e.tensor_tensor(
                h1t32[t2 % 2][:], scrap[t2 % 2][:], xst[t2 % 2][:], ALU.add),
                waits=[("V", g1), x4_w[tc]] +
                      ([qs_x2[tc - 2]] if tc >= 2 else []))
            qs_x2.append(dma("scalar", f"DY{tc % 2}", x2_d[tc], h1t32[tc % 2][:],
                             waits=[("V", g2)]))
            r1 = ve(lambda e, t2=tc: e.tensor_reduce(
                ln_a[:], h1t32[t2 % 2][:], AX.X, ALU.add, negate=True),
                waits=[("V", g2)])
            sq = sc(lambda e, t2=tc: e.activation(
                scrap[t2 % 2][:], h1t32[t2 % 2][:], ACT.Square, accum_out=ln_b[:]),
                waits=[("V", g2), ("S", C["S"])])
            r2 = ve(lambda e: e.tensor_scalar(ln_a[:], ln_a[:], 1.0 / D, None, ALU.mult),
                    waits=[("V", r1)])
            r3 = ve(lambda e: e.tensor_tensor(ln_d[:], ln_a[:], ln_a[:], ALU.mult),
                    waits=[("V", r2)])
            r4 = ve(lambda e: e.tensor_scalar(ln_c[:], ln_b[:], 1.0 / D, None, ALU.mult),
                    waits=[("S", sq), ("V", r3)])
            r5 = ve(lambda e: e.tensor_tensor(ln_c[:], ln_c[:], ln_d[:], ALU.subtract),
                    waits=[("V", r4)])
            s3_ = sc(lambda e: e.activation(ln_c[:], ln_c[:], ACT.Sqrt, bias=eps_t[:]),
                     waits=[("V", r5), ("S", C["S"])])
            r6 = ve(lambda e: e.reciprocal(ln_c[:], ln_c[:]), waits=[("S", s3_)])
            z1 = ve(lambda e, t2=tc: e.tensor_scalar(
                scrap[t2 % 2][:], h1t32[t2 % 2][:], ln_a[:], ln_c[:], ALU.add, ALU.mult),
                waits=[("V", r6)])
            z2 = ve(lambda e, t2=tc: e.tensor_tensor(
                scrap[t2 % 2][:], scrap[t2 % 2][:], bc_cb[:], ALU.mult),
                waits=[("V", z1)])
            z3 = ve(lambda e, t2=tc: e.tensor_tensor(
                h2st[t2 % 2][:], scrap[t2 % 2][:], bc_sbv[:], ALU.add),
                waits=[("V", z2)] + ([qc_h2[-2]] if len(qc_h2) >= 2 else []))
            v_tc = ve(lambda e, t2=tc: e.tensor_reduce(
                am1[:, t2:t2 + 1], h2st[t2 % 2][:], AX.X, ALU.max,
                apply_absolute_value=True),
                waits=[("V", z3)])
            v_x2w[tc] = v_tc
            qc_h2.append(dma("scalar", f"DH{tc % 2}", h2_d[tc], h2st[tc % 2][:],
                             waits=[("V", v_tc)]))
        v_s4 = C["V"]
        pass

        def global_amax(src, cci, cco, dst_col, waits0):
            a1_ = ve(lambda e: e.tensor_reduce(amx[:], src, AX.X, ALU.max),
                     waits=list(waits0) + [("V", C["V"])])
            tt = te(lambda e: e.transpose(psC[0:1, 0:128], amx[:, 0:1], ident[:]),
                    waits=[("V", a1_), ("T", C["T"]), ("V", C["V"])])
            a2_ = ve(lambda e: e.tensor_copy(amrow[:], psC[0:1, 0:128]),
                     waits=[("T", tt)])
            a3_ = ve(lambda e: e.tensor_reduce(sc_s[0:1, 0:1], amrow[:], AX.X, ALU.max),
                     waits=[("V", a2_)])
            a4_ = ve(lambda e: e.tensor_scalar_max(sc_s[0:1, 0:1], sc_s[0:1, 0:1], 1e-8),
                     waits=[("V", a3_)])
            w1 = dma("gpsimd", "DC", cci.ap(), sc_s[0:1, 0:1], waits=[("V", a4_)])
            gc = p.op("gpsimd", lambda e: e.collective_compute(
                "AllReduce", ALU.max, replica_groups=[list(range(NCOR))],
                ins=[cci.ap()], outs=[cco.ap()]),
                waits=[w1], inc="G")
            return dma("gpsimd", "DC", sc_s[0:1, dst_col:dst_col + 1], cco.ap(),
                       waits=[("G", gc)])

        def bcast_scalar(dst, col, wv):
            tt = te(lambda e, c2=col: e.matmul(psC[:, 0:1], ones1f[:],
                                               sc_s[0:1, c2:c2 + 1],
                                               start=True, stop=True),
                    waits=[wv, ("T", C["T"])])
            return ve(lambda e, d=dst: e.tensor_copy(d[:], psC[:, 0:1]),
                      waits=[("T", tt)])

        w_am1 = global_amax(am1[:, 0:16], cc1i, cc1o, 1, [("V", v_s4)])
        b1 = ve(lambda e: e.reciprocal(sc_s[0:1, 2:3], sc_s[0:1, 1:2]),
                waits=[w_am1])
        b2 = ve(lambda e: e.tensor_scalar(sc_s[0:1, 2:3], sc_s[0:1, 2:3], 32767.0,
                                          None, ALU.mult), waits=[("V", b1)])
        b3 = ve(lambda e: e.tensor_scalar(sc_s[0:1, 3:4], sc_s[0:1, 1:2], 1.0 / 32767.0,
                                          None, ALU.mult), waits=[("V", b2)])
        bs1 = bcast_scalar(s1t, 2, ("V", b3))
        bs2 = bcast_scalar(rs1t, 3, ("V", bs1))

        v_sfbc = 0
        for qd in range(4):
            te(lambda e, q=qd: e.matmul(psD[:, 0:512], ones1f[:],
                                        amaxf_row[0:1, 1024 * q:1024 * q + 512],
                                        start=True, stop=True),
               waits=[("V", v_sfrow), ("T", C["T"])] if qd == 0 else
                     [("V", v_sfbc)], inc=None)
            t2 = te(lambda e, q=qd: e.matmul(psD[:, 512:1024], ones1f[:],
                                             amaxf_row[0:1, 1024 * q + 512:1024 * q + 1024],
                                             start=True, stop=True))
            v_sfbc = ve(lambda e, q=qd: e.tensor_copy(
                sf_bc[:, 1024 * q:1024 * q + 1024], psD[:]), waits=[("T", t2)])

        h2r_w = [0] * 16
        v_xq1 = [0] * 16
        for tc in range(16):
            h2r_w[tc] = dma("sync", f"DH{tc % 2}", h2st[tc % 2][:], h2_d[tc],
                            waits=([("V", v_xq1[tc - 2])] if tc >= 2 else
                                   tot("DH0", "DH1")))
            y1 = ve(lambda e, t2=tc: e.tensor_scalar(
                xq1st[t2 % 2][:], h2st[t2 % 2][:], s1t[:], MAGIC, ALU.mult, ALU.add),
                waits=[h2r_w[tc], ("V", bs2), ("V", C["V"])])
            y2 = ve(lambda e, t2=tc: e.tensor_scalar(
                xq1st[t2 % 2][:], xq1st[t2 % 2][:], MAGIC, rs1t[:],
                ALU.subtract, ALU.mult),
                waits=[("V", y1)])
            ps_ = psC if (tc % 2 == 0) else psD
            tt = 0
            for ec in range(8):
                tt = te(lambda e, t2=tc, k=ec, P=ps_: e.transpose(
                    P[:, 128 * k:128 * k + 128],
                    xq1st[t2 % 2][:, 128 * k:128 * k + 128], ident[:]),
                    waits=([("V", y2)] +
                           ([("T", C["T"])] if tc <= 1 else []) +
                           ([("V", v_xq1[tc - 2])] if tc >= 2 else
                            [("V", v_sfbc)]))
                    if ec == 0 else [],
                    inc="T" if ec == 7 else None)
            v_xq1[tc] = ve(lambda e, t2=tc, P=ps_: e.tensor_copy(
                xq1f_all[:].rearrange("p (g c) -> p g c", c=T_OWN)
                [:, :, 128 * t2:128 * t2 + 128],
                P[:].rearrange("p (g c) -> p g c", c=128)),
                waits=[("T", tt)])
        v_xq1e = C["V"]

        v_mm1 = [0] * 32
        s_mm1 = 0
        qc_hw = []
        for fc in range(32):
            qe, qs = [("scalar", "DW0"), ("sync", "DN0"),
                      ("scalar", "DW1"), ("sync", "DN1")][fc % 4]
            wv = dma(qe, qs, win_st[fc % 3][:],
                     WIN[:, 128 * fc:128 * fc + 128].rearrange("(k p) f -> p k f", p=128),
                     waits=([("V", v_mm1[fc - 3])] if fc >= 3 else
                            tot("DS0", "DS1") + [("V", v_xq1e)]))
            q1 = ve(lambda e, f=fc: e.tensor_tensor(
                wqf32[f % 2][:].rearrange("p (k f) -> p k f", k=8),
                win_st[f % 3][:].rearrange("p (k f) -> p k f", k=8),
                sf_bc[:, None, 128 * f:128 * f + 128].broadcast_to([128, 8, 128]),
                ALU.mult),
                waits=[wv, ("V", v_sfbc), ("V", C["V"])])
            q2_ = ve(lambda e, f=fc: e.tensor_scalar(
                wint1[f % 2][:], wqf32[f % 2][:], MAGIC, MAGIC, ALU.add, ALU.subtract),
                waits=[("V", q1)])
            pp = (psA, psB) if fc % 2 == 0 else (psC, psD)
            tl = 0
            for th in range(2):
                for ec in range(8):
                    for sg in range(2):
                        tl = te(lambda e, f=fc, t2=th, k=ec, g=sg, P=pp[th]: e.matmul(
                            P[:, 512 * g:512 * g + 512],
                            wint1[f % 2][:, 128 * k:128 * k + 128],
                            xq1f[k][:, 1024 * t2 + 512 * g:1024 * t2 + 512 * g + 512],
                            start=(k == 0), stop=(k == 7)),
                            waits=([("V", q2_)] +
                                   ([("S", s_mm1)] if fc >= 2 else [("T", C["T"])]))
                            if (th == 0 and ec == 0 and sg == 0) else [],
                            inc="T" if (th == 1 and ec == 7 and sg == 1) else None)
            for th in range(2):
                s_mm1 = sc(lambda e, f=fc, t2=th, P=pp[th]: e.activation(
                    hidst[f % 2][:, 1024 * t2:1024 * t2 + 1024], P[:], ACT.Gelu,
                    bias=b_inT[:, f:f + 1], scale=rsf[:, f:f + 1]),
                    waits=([("T", tl), ("V", v_bint), ("S", C["S"])] if th == 0 else
                           [("S", C["S"])]) +
                          ([qc_hw[-2]] if (th == 0 and len(qc_hw) >= 2) else []))
            vr = ve(lambda e, f=fc: e.tensor_reduce(
                am2a[:, 2 * f:2 * f + 1], hidst[f % 2][:, 0:1024], AX.X, ALU.max,
                apply_absolute_value=True),
                waits=[("S", s_mm1), ("V", C["V"])])
            v_mm1[fc] = ve(lambda e, f=fc: e.tensor_reduce(
                am2a[:, 2 * f + 1:2 * f + 2], hidst[f % 2][:, 1024:2048], AX.X, ALU.max,
                apply_absolute_value=True),
                waits=[("V", vr)])
            qc_hw.append(dma("scalar", f"DI{fc % 2}", hid_d[fc], hidst[fc % 2][:],
                             waits=[("V", v_mm1[fc])]))
        t_mm1 = C["T"]
        pass

        w_am2 = global_amax(am2a[:, 0:64], cc2i, cc2o, 5, [("V", v_mm1[31])])
        e1 = ve(lambda e: e.reciprocal(sc_s[0:1, 6:7], sc_s[0:1, 5:6]),
                waits=[w_am2])
        e2 = ve(lambda e: e.tensor_scalar(sc_s[0:1, 6:7], sc_s[0:1, 6:7], 32767.0,
                                          None, ALU.mult), waits=[("V", e1)])
        e3 = ve(lambda e: e.tensor_scalar(sc_s[0:1, 7:8], sc_s[0:1, 5:6], 1.0 / 32767.0,
                                          None, ALU.mult), waits=[("V", e2)])
        bs3 = bcast_scalar(s2t, 6, ("V", e3))
        bs4 = bcast_scalar(rs2t, 7, ("V", bs3))
        e4 = bs4

        out_w = []
        v_mm2 = defaultdict(int)
        qg_weh = 0
        t_eh0 = 0
        qx_d = {}
        tl_d = {}
        y1_d = {}
        for eh in range(2):
            w_weh0 = (tot("DI0", "DI1") + [("V", v_mm1[31])] if eh == 0 else
                      [("T", t_eh0), ("V", C["V"])])
            qg_weh = dma("sync", "DZ",
                         w2eh[:, 0:16 * 512].rearrange("p (c f) -> p c f", c=16),
                         w2_d[0:16, :, 512 * eh:512 * eh + 512].rearrange("c p f -> p c f"),
                         waits=w_weh0)
            qg_weh2 = dma("gpsimd", "DG0",
                          w2eh[:, 16 * 512:32 * 512].rearrange("p (c f) -> p c f", c=16),
                          w2_d[16:32, :, 512 * eh:512 * eh + 512].rearrange("c p f -> p c f"),
                          waits=w_weh0)
            vsw = ve(lambda e: e.tensor_scalar(
                w2eh[:], w2eh[:], rs2t[:], None, ALU.mult),
                waits=[qg_weh, qg_weh2, ("V", e4), ("V", C["V"])])
            for tc in range(16):
                pp = psA if tc % 2 == 0 else psB
                if eh == 0:
                    hw = dma("sync" if tc % 2 == 0 else "gpsimd",
                             "DI0" if tc % 2 == 0 else "DE1",
                             hidld[tc % 2][:],
                             hid_d[:, :, 128 * tc:128 * tc + 128].rearrange("c f t -> f c t"),
                             waits=([("S", y1_d[tc - 2])] if tc >= 2 else
                                    tot("DI0", "DI1")))
                    y1_d[tc] = sc(lambda e, t2=tc: e.activation(
                        hidy[:], hidld[t2 % 2][:], ACT.Copy,
                        bias=MAGIC, scale=s2t[:]),
                        waits=[hw, ("V", e4), ("S", C["S"])])
                    y2s = ve(lambda e, t2=tc: e.tensor_scalar(
                        xq2t[t2 % 2][:], hidy[:], MAGIC, None, ALU.subtract),
                        waits=[("S", y1_d[tc]), ("V", C["V"])] +
                              ([qx_d[(0, tc - 2)], ("T", tl_d[(0, tc - 2)])]
                               if tc >= 2 else []))
                    qx = dma("scalar", f"DV{tc % 2}", xq2_d[tc], xq2t[tc % 2][:],
                             waits=[("V", y2s)])
                    wmm = [("V", y2s)]
                else:
                    qx = dma("sync", f"DV{tc % 2}", xq2t[tc % 2][:], xq2_d[tc],
                             waits=([("T", tl_d[(1, tc - 2)])] if tc >= 2 else
                                    tot("DV0", "DV1") + [("T", t_eh0)]))
                    wmm = [qx]
                xw = dma("sync", f"DY{tc % 2}", xst[tc % 2][:, 0:512],
                         x2_d[tc, :, 512 * eh:512 * eh + 512],
                         waits=(tot("DY0", "DY1") if (eh == 0 and tc <= 1) else []) +
                               ([("V", v_mm2[(eh, tc - 2)])] if tc >= 2 else []))
                tl = 0
                for f5 in range(32):
                    tl = te(lambda e, f6=f5, t2=tc, P=pp: e.matmul(
                        P[:, 0:512], xq2t[t2 % 2][:, 128 * f6:128 * f6 + 128],
                        w2eh[:, 512 * f6:512 * f6 + 512],
                        start=(f6 == 0), stop=(f6 == 31)),
                        waits=(wmm + [("V", vsw)] +
                               ([("T", t_mm1)] if (eh == 0 and tc <= 1) else []) +
                               ([("V", v_mm2[(eh, tc - 2)])] if tc >= 2 else []))
                        if f5 == 0 else [],
                        inc="T" if f5 == 31 else None)
                qx_d[(eh, tc)] = qx
                tl_d[(eh, tc)] = tl
                o1 = ve(lambda e, t2=tc, P=pp, g=eh: e.scalar_tensor_tensor(
                    scrap[t2 % 2][:, 0:512], P[:, 0:512], 0.0,
                    bc_gb[:, 512 * g:512 * g + 512], ALU.add, ALU.mult),
                    waits=[("T", tl), ("V", C["V"])])
                o2 = ve(lambda e, t2=tc, g=eh: e.tensor_tensor(
                    scrap[t2 % 2][:, 0:512], scrap[t2 % 2][:, 0:512],
                    bc_gbo[:, 512 * g:512 * g + 512], ALU.add),
                    waits=[("V", o1)])
                o3 = ve(lambda e, t2=tc, g=eh: e.tensor_tensor(
                    outst[t2 % 2][:], scrap[t2 % 2][:, 0:512], xst[t2 % 2][:, 0:512],
                    ALU.add),
                    waits=[("V", o2), xw] +
                          ([out_w[-2]] if len(out_w) >= 2 else []))
                v_mm2[(eh, tc)] = o3
                out_w.append(dma("sync", f"DU{tc % 2}",
                                 OUT[128 * tc:128 * tc + 128, 512 * eh:512 * eh + 512],
                                 outst[tc % 2][:], waits=[("V", o3)]))
                if tc == 15:
                    t_eh0 = C["T"]
        du0, du1 = C["DU0"], C["DU1"]
        p.op("sync", lambda e: e.wait_ge(sems["DU0"], du0))
        p.op("sync", lambda e: e.wait_ge(sems["DU1"], du1))

        emit(block, p, sems)
    return nc


_BUILT = None


def kernel(L0_attn_x, L0_attn_w_qkv, L0_attn_w_out, L0_ffn_W_in, L0_ffn_b_in,
           L0_ffn_W_out, L0_ffn_b_out, L0_c, L0_wmod):
    global _BUILT
    if _BUILT is None:
        _BUILT = build()
    nc = _BUILT
    x = np.ascontiguousarray(np.asarray(L0_attn_x, dtype=np.float32))
    wqkv = np.ascontiguousarray(np.asarray(L0_attn_w_qkv, dtype=np.float32))
    wout = np.ascontiguousarray(np.asarray(L0_attn_w_out, dtype=np.float32))
    win = np.ascontiguousarray(np.asarray(L0_ffn_W_in, dtype=np.float32))
    b_in = np.ascontiguousarray(np.asarray(L0_ffn_b_in, dtype=np.float32).reshape(1, FF))
    wo2 = np.ascontiguousarray(np.asarray(L0_ffn_W_out, dtype=np.float32))
    b_out = np.ascontiguousarray(np.asarray(L0_ffn_b_out, dtype=np.float32).reshape(1, D))
    cv = np.ascontiguousarray(np.asarray(L0_c, dtype=np.float32))
    wmod = np.ascontiguousarray(np.asarray(L0_wmod, dtype=np.float32))
    in_maps = []
    for c in range(8):
        b, h = c // 2, c % 2
        xb = x[b]
        xperm = np.ascontiguousarray(
            np.concatenate([xb[T_OWN * h:T_OWN * h + T_OWN],
                            xb[T_OWN * (1 - h):T_OWN * (1 - h) + T_OWN]], axis=0))
        in_maps.append({
            "x": xperm, "wqkv": wqkv, "wout": wout, "win": win, "bin": b_in,
            "wo2": wo2, "bout": b_out, "cvec": np.ascontiguousarray(cv[b].reshape(1, D)),
            "wmod": wmod,
        })
    res = run_bass_kernel_spmd(nc, in_maps, core_ids=list(range(8)))
    out = np.empty((4, T_FULL, D), dtype=np.float32)
    for c in range(8):
        b, h = c // 2, c % 2
        out[b, T_OWN * h:T_OWN * h + T_OWN] = res.results[c]["out"]
    return out
